# version 03
# baseline (speedup 1.0000x reference)
import numpy as np
import ml_dtypes
import contextlib
import concourse.bass as bass
import concourse.mybir as mybir
from concourse.bass_utils import run_bass_kernel_spmd

F32 = mybir.dt.float32
BF16 = mybir.dt.bfloat16
I8 = mybir.dt.int8
AF = mybir.ActivationFunctionType
ALU = mybir.AluOpType
AX = mybir.AxisListType

B = 2
S = 4096
D = 2048
NTB = S // 128
NT_ALL = B * NTB
T_ALL = B * S
T_OWN = 1024
NT_OWN = 8
CC = 384
DI = 8192




_DRAIN_OPS = {
    "tensor_copy", "tensor_tensor", "tensor_add", "tensor_sub", "tensor_mul",
    "tensor_max", "tensor_scalar", "tensor_single_scalar", "tensor_scalar_add",
    "tensor_scalar_sub", "tensor_scalar_mul", "tensor_scalar_max", "tensor_scalar_min",
    "tensor_relu", "tensor_reduce", "reduce_max", "reduce_sum", "scalar_tensor_tensor",
    "tensor_tensor_scan", "memset", "activation", "copy", "mul", "add", "square",
    "sqrt", "sign", "reciprocal", "select", "bn_stats", "bn_aggr", "transpose",
    "stream_shuffle", "pool", "pool_max", "pool_avg", "copy_predicated", "memzero",
}


class _Drain:

    def __init__(self, eng):
        self._e = eng

    def __getattr__(self, n):
        f = getattr(self._e, n)
        if n in _DRAIN_OPS:
            def wrapped(*a, **kw):
                self._e.drain()
                return f(*a, **kw)
            return wrapped
        return f


def build_nc(debug=False):
    nc = bass.Bass("TRN2", target_bir_lowering=False, debug=False)

    x_all = nc.dram_tensor("x_all", [T_ALL, D], F32, kind="ExternalInput").ap()
    x_own = nc.dram_tensor("x_own", [T_OWN, D], F32, kind="ExternalInput").ap()
    w_inT = nc.dram_tensor("w_inT", [D, CC], F32, kind="ExternalInput").ap()
    w_outT = nc.dram_tensor("w_outT", [256, D], F32, kind="ExternalInput").ap()
    rms_col = nc.dram_tensor("rms_col", [128, 16], F32, kind="ExternalInput").ap()
    wgT = nc.dram_tensor("wgT", [D, DI], F32, kind="ExternalInput").ap()
    wuT = nc.dram_tensor("wuT", [D, DI], F32, kind="ExternalInput").ap()
    wdT = nc.dram_tensor("wdT", [DI, D], F32, kind="ExternalInput").ap()
    lnw_in = nc.dram_tensor("lnw_in", [16, 512], F32, kind="ExternalInput").ap()
    trig_cos = nc.dram_tensor("trig_cos", [128, 32 * NTB], F32, kind="ExternalInput").ap()
    trig_sin = nc.dram_tensor("trig_sin", [128, 32 * NTB], F32, kind="ExternalInput").ap()
    masks = nc.dram_tensor("masks", [128, 4 * 512], F32, kind="ExternalInput").ap()
    ident_i = nc.dram_tensor("ident_i", [128, 128], F32, kind="ExternalInput").ap()
    invcnt = nc.dram_tensor("invcnt", [1, S], BF16, kind="ExternalInput").ap()

    out = nc.dram_tensor("out", [T_OWN, D], F32, kind="ExternalOutput").ap()

    rs_in = nc.dram_tensor("rs_in", [T_ALL, D], BF16, kind="Internal").ap()
    rs_out = nc.dram_tensor("rs_out", [T_OWN, D], BF16, kind="Internal").ap()
    k2t_d = nc.dram_tensor("k2t_d", [DI, T_OWN], BF16, kind="Internal").ap()
    xn_park = nc.dram_tensor("xn_park", [T_OWN, D], BF16, kind="Internal").ap()

    tap = {}
    if debug:
        tap["qaug"] = nc.dram_tensor("tap_qaug", [8 * 65, S], BF16, kind="ExternalOutput").ap()
        tap["kaug"] = nc.dram_tensor("tap_kaug", [B * 65, S], BF16, kind="ExternalOutput").ap()
        tap["v"] = nc.dram_tensor("tap_v", [B * 128, NTB * 65], BF16, kind="ExternalOutput").ap()
        tap["oT"] = nc.dram_tensor("tap_oT", [2 * 128, T_ALL], BF16, kind="ExternalOutput").ap()
        tap["xn"] = nc.dram_tensor("tap_xn", [T_OWN, D], BF16, kind="ExternalOutput").ap()
        tap["g"] = nc.dram_tensor("tap_g", [T_OWN, D], BF16, kind="ExternalOutput").ap()
        tap["wsc"] = nc.dram_tensor("tap_wsc", [128, 8], F32, kind="ExternalOutput").ap()
        tap["rs"] = nc.dram_tensor("tap_rs", [T_OWN, D], BF16, kind="ExternalOutput").ap()
        tap["h"] = nc.dram_tensor("tap_h", [128, DI], BF16, kind="ExternalOutput").ap()
        tap["kx"] = nc.dram_tensor("tap_kx", [128, 16 * NT_OWN * 128], BF16, kind="ExternalOutput").ap()
        tap["hg"] = nc.dram_tensor("tap_hg", [128, 4 * 512], BF16, kind="ExternalOutput").ap()
        tap["wq"] = nc.dram_tensor("tap_wq", [128, 512], BF16, kind="ExternalOutput").ap()
        tap["hu"] = nc.dram_tensor("tap_hu", [128, 4 * 512], BF16, kind="ExternalOutput").ap()

    sb = nc.alloc_sbuf_tensor
    sems = {}

    def S_(name):
        if name not in sems:
            sems[name] = nc.alloc_semaphore(name)
        return sems[name]

    identb = sb("identb", [128, 128], BF16)
    ones_r = sb("ones_r", [1, 128], BF16)
    onesrow = sb("onesrow", [1, 512], F32)
    onescol = sb("onescol", [128, 1], F32)
    onescolb = sb("onescolb", [128, 1], BF16)
    c127 = sb("c127", [128, 1], F32)
    mcol = sb("mcol", [128, 1], F32)
    nmcol = sb("nmcol", [128, 1], F32)
    expb = sb("expb", [128, 1], F32)
    wsum = sb("wsum", [128, 192], F32)
    w1s = sb("w1s", [1, 4], F32)
    wsc = sb("wsc", [128, 4], F32)
    swc = sb("swc", [128, 4], F32)
    dsc8p = sb("dsc8p", [128, 8], F32)

    units = [(b, h, c) for b in range(B) for h in range(4) for c in range(8)]
    NU = len(units)
    grp_of = [2 * c + 2 for (_, _, c) in units]
    cum_grp = np.concatenate([[0], np.cumsum(grp_of)]).astype(int).tolist()
    NGRP = cum_grp[-1]

    _es0 = contextlib.ExitStack()
    masks_sb = _es0.enter_context(nc.sbuf_tensor("masks_sb", [128, 4 * 512], F32))
    invcnt_sb = _es0.enter_context(nc.sbuf_tensor("invcnt_sb", [1, S], BF16))
    winb = _es0.enter_context(nc.sbuf_tensor("winb", [128, 16 * CC], BF16))
    rmsc = _es0.enter_context(nc.sbuf_tensor("rmsc", [128, 16], F32))
    qa0 = _es0.enter_context(nc.sbuf_tensor("qa0", [65, S], BF16))
    qa1 = _es0.enter_context(nc.sbuf_tensor("qa1", [65, S], BF16))
    qa2 = _es0.enter_context(nc.sbuf_tensor("qa2", [65, S], BF16))
    qa3 = _es0.enter_context(nc.sbuf_tensor("qa3", [65, S], BF16))
    qa4 = _es0.enter_context(nc.sbuf_tensor("qa4", [65, S], BF16))
    qa5 = _es0.enter_context(nc.sbuf_tensor("qa5", [65, S], BF16))
    qa6 = _es0.enter_context(nc.sbuf_tensor("qa6", [65, S], BF16))
    qa7 = _es0.enter_context(nc.sbuf_tensor("qa7", [65, S], BF16))
    ka0 = _es0.enter_context(nc.sbuf_tensor("ka0", [65, S], BF16))
    ka1 = _es0.enter_context(nc.sbuf_tensor("ka1", [65, S], BF16))
    vs0 = _es0.enter_context(nc.sbuf_tensor("vs0", [128, NTB * 65], BF16))
    vs1 = _es0.enter_context(nc.sbuf_tensor("vs1", [128, NTB * 65], BF16))
    if True:
        q_aug = [qa0, qa1, qa2, qa3, qa4, qa5, qa6, qa7]
        k_aug = [ka0, ka1]
        v_sb = [vs0, vs1]

        _es1 = contextlib.ExitStack()
        trigc = _es1.enter_context(nc.sbuf_tensor("trigc", [128, 32 * NTB], F32))
        trigs = _es1.enter_context(nc.sbuf_tensor("trigs", [128, 32 * NTB], F32))
        ident32 = _es1.enter_context(nc.sbuf_tensor("ident32", [128, 128], F32))
        win32 = _es1.enter_context(nc.sbuf_tensor("win32", [128, CC], F32))
        xt0 = _es1.enter_context(nc.sbuf_tensor("xt0", [128, D], F32))
        xt1 = _es1.enter_context(nc.sbuf_tensor("xt1", [128, D], F32))
        xnt0 = _es1.enter_context(nc.sbuf_tensor("xnt0", [128, D], BF16))
        xnt1 = _es1.enter_context(nc.sbuf_tensor("xnt1", [128, D], BF16))
        xnT = _es1.enter_context(nc.sbuf_tensor("xnT", [128, 16 * 128], BF16))
        sqs = _es1.enter_context(nc.sbuf_tensor("sqs", [128, D], BF16))
        stat = _es1.enter_context(nc.sbuf_tensor("stat", [128, 24], F32))
        ropeq0 = _es1.enter_context(nc.sbuf_tensor("ropeq0", [128, 256], BF16))
        ropeq1 = _es1.enter_context(nc.sbuf_tensor("ropeq1", [128, 256], BF16))
        ropek0 = _es1.enter_context(nc.sbuf_tensor("ropek0", [128, 64], BF16))
        ropek1 = _es1.enter_context(nc.sbuf_tensor("ropek1", [128, 64], BF16))
        rsc1 = _es1.enter_context(nc.sbuf_tensor("rsc1", [128, 128], F32))
        rsc2 = _es1.enter_context(nc.sbuf_tensor("rsc2", [128, 128], F32))
        pw0 = _es1.enter_context(nc.sbuf_tensor("pw0", [128, 2048], F32))
        pw1 = _es1.enter_context(nc.sbuf_tensor("pw1", [128, 2048], F32))
        pabs = _es1.enter_context(nc.sbuf_tensor("pabs", [128, 2048], BF16))
        ps_tr0 = _es1.enter_context(nc.psum_tensor("ps_tr0", [128, 512], F32))
        ps_tr1 = _es1.enter_context(nc.psum_tensor("ps_tr1", [128, 512], F32))
        ps_y = _es1.enter_context(nc.psum_tensor("ps_y", [128, CC], F32))
        ps_qt = _es1.enter_context(nc.psum_tensor("ps_qt", [64, 512], F32))
        ps_kt = _es1.enter_context(nc.psum_tensor("ps_kt", [64, 128], F32))
        blk = _es1.enter_context(nc.Block())
        if True:
            xt = [xt0, xt1]
            xnt = [xnt0, xnt1]
            ropeq = [ropeq0, ropeq1]
            ropek = [ropek0, ropek1]
            c_ld, win_ld = S_("c_ld"), S_("win_ld")
            x_ldp = [S_("x_ld0"), S_("x_ld1")]
            v_cst, win_cv = S_("v_cst"), S_("win_cv")
            s_sq, v_st, v_red = S_("s_sq"), S_("v_st"), S_("v_red")
            t_tr, v_xnT, t_y = S_("t_tr"), S_("v_xnT"), S_("t_y")
            v_rope, t_qk, v_qkev = S_("v_rope"), S_("t_qk"), S_("v_qkev")
            gu_ldp = [S_("gu_ld0"), S_("gu_ld1")]
            s_pw = S_("s_pw")

            @blk.sync
            def _(sync):
                for dst, src in [(trigc[:], trig_cos[:]), (trigs[:], trig_sin[:]),
                                 (masks_sb[:], masks[:]), (invcnt_sb[:], invcnt[:]),
                                 (rmsc[:], rms_col[:]), (ident32[:], ident_i[:])]:
                    sync.dma_start(out=dst, in_=src).then_inc(c_ld, 16)
                for ti in range(NT_ALL):
                    if ti >= 2:
                        sync.wait_ge(s_sq, 4 * (ti - 2) + 4)
                        sync.wait_ge(v_red, ti - 1)
                    sync.dma_start(out=xt[ti % 2][:], in_=x_all[ti * 128:(ti + 1) * 128, :]).then_inc(x_ldp[ti % 2], 16)

            @blk.gpsimd
            def _(g):
                for mi in range(16):
                    if mi >= 1:
                        g.wait_ge(win_cv, mi)
                    g.dma_start(out=win32[:], in_=w_inT[mi * 128:(mi + 1) * 128, :]).then_inc(win_ld, 16)
                ch = 0
                for w_ap in (wgT, wuT):
                    for r in range(16):
                        for cc2 in range(4):
                            if ch >= 2:
                                g.wait_ge(s_pw, ch - 1)
                            g.dma_start(out=(pw0 if ch % 2 == 0 else pw1)[:],
                                        in_=w_ap[r * 128:(r + 1) * 128, cc2 * 2048:(cc2 + 1) * 2048]).then_inc(gu_ldp[ch % 2], 16)
                            ch += 1

            @blk.vector
            def _(v):
                v = _Drain(v)
                v.wait_ge(c_ld, 16 * 6)
                v.memset(ones_r[:], 1.0)
                v.memset(onesrow[:], 1.0)
                v.memset(onescol[:], 1.0)
                v.memset(onescolb[:], 1.0)
                v.memset(c127[:], 127.0)
                v.memset(mcol[:], 12582912.0)
                v.memset(nmcol[:], -12582912.0)
                v.memset(expb[:], -40.0)
                v.tensor_copy(identb[:], ident32[:])
                for b in range(B):
                    v.memset(k_aug[b][64:65, :], 1.0)
                    v.memset(v_sb[b].rearrange("p (n c) -> p n c", c=65)[:, :, 64:65], 1.0)
                v.memset(wsum[:], 0.0).then_inc(v_cst, 1)
                for mi in range(16):
                    v.wait_ge(win_ld, 16 * (mi + 1))
                    v.tensor_scalar(winb[:, mi * CC:(mi + 1) * CC], win32[:],
                                    rmsc[:, mi:mi + 1], None, ALU.mult).then_inc(win_cv, 1)

                for ti in range(NT_ALL):
                    xb = xt[ti % 2]
                    o = 12 * (ti % 2)
                    v.wait_ge(x_ldp[ti % 2], 16 * (ti // 2 + 1))
                    v.reduce_sum(stat[:, o + 0:o + 1], xb[:], AX.X).then_inc(v_red, 1)
                    v.wait_ge(s_sq, 4 * ti + 1)
                    v.tensor_scalar_mul(stat[:, o + 2:o + 3], stat[:, o + 0:o + 1], 1.0 / D)
                    v.tensor_scalar_mul(stat[:, o + 3:o + 4], stat[:, o + 1:o + 2], 1.0 / D)
                    v.tensor_scalar(stat[:, o + 4:o + 5], stat[:, o + 2:o + 3], stat[:, o + 2:o + 3], None, ALU.mult)
                    v.tensor_sub(stat[:, o + 5:o + 6], stat[:, o + 3:o + 4], stat[:, o + 4:o + 5])
                    v.tensor_scalar_add(stat[:, o + 6:o + 7], stat[:, o + 5:o + 6], 1e-5)
                    v.scalar_tensor_tensor(stat[:, o + 7:o + 8], stat[:, o + 6:o + 7], 1e-6,
                                           stat[:, o + 5:o + 6], ALU.mult, ALU.add).then_inc(v_st, 1)
                    v.wait_ge(s_sq, 4 * ti + 3)
                    v.tensor_scalar(stat[:, o + 11:o + 12], stat[:, o + 10:o + 11], stat[:, o + 10:o + 11], None, ALU.mult)
                    v.scalar_tensor_tensor(stat[:, o + 11:o + 12], stat[:, o + 11:o + 12], -0.5,
                                           stat[:, o + 7:o + 8], ALU.mult, ALU.mult)
                    v.tensor_scalar_add(stat[:, o + 11:o + 12], stat[:, o + 11:o + 12], 1.5)
                    v.tensor_mul(stat[:, o + 10:o + 11], stat[:, o + 11:o + 12], stat[:, o + 10:o + 11])
                    v.tensor_scalar(stat[:, o + 11:o + 12], stat[:, o + 2:o + 3],
                                    stat[:, o + 10:o + 11], -1.0, ALU.mult, ALU.mult).then_inc(v_st, 1)

                    for r in range(4):
                        g4 = 4 * ti + r
                        pst = [ps_tr0, ps_tr1][g4 % 2]
                        v.wait_ge(t_tr, g4 + 1)
                        v.tensor_copy(xnT[:, r * 512:(r + 1) * 512], pst[:]).then_inc(v_xnT, 1)

                    v.wait_ge(t_y, ti + 1)
                    b = ti // NTB
                    pt = ti % NTB
                    cos4 = trigc[:, pt * 32:(pt + 1) * 32].unsqueeze(1).to_broadcast([128, 4, 32])
                    sin4 = trigs[:, pt * 32:(pt + 1) * 32].unsqueeze(1).to_broadcast([128, 4, 32])
                    ya = ps_y[:, 0:256].rearrange("p (h x) -> p h x", h=4)
                    rqv = ropeq[ti % 2][:].rearrange("p (h x) -> p h x", h=4)
                    r1 = rsc1[:].rearrange("p (h x) -> p h x", h=4)[:, :, 0:32]
                    r2 = rsc2[:].rearrange("p (h x) -> p h x", h=4)[:, :, 0:32]
                    v.tensor_tensor(r1, ya[:, :, 0:32], cos4, ALU.mult)
                    v.tensor_tensor(r2, ya[:, :, 32:64], sin4, ALU.mult)
                    v.tensor_sub(rqv[:, :, 0:32], r1, r2)
                    v.tensor_tensor(r1, ya[:, :, 32:64], cos4, ALU.mult)
                    v.tensor_tensor(r2, ya[:, :, 0:32], sin4, ALU.mult)
                    v.tensor_add(rqv[:, :, 32:64], r1, r2)
                    cos1 = trigc[:, pt * 32:(pt + 1) * 32]
                    sin1 = trigs[:, pt * 32:(pt + 1) * 32]
                    rk = ropek[ti % 2]
                    v.tensor_tensor(rsc1[:, 0:32], ps_y[:, 256:288], cos1, ALU.mult)
                    v.tensor_tensor(rsc2[:, 0:32], ps_y[:, 288:320], sin1, ALU.mult)
                    v.tensor_sub(rk[:, 0:32], rsc1[:, 0:32], rsc2[:, 0:32])
                    v.tensor_tensor(rsc1[:, 0:32], ps_y[:, 288:320], cos1, ALU.mult)
                    v.tensor_tensor(rsc2[:, 0:32], ps_y[:, 256:288], sin1, ALU.mult)
                    v.tensor_add(rk[:, 32:64], rsc1[:, 0:32], rsc2[:, 0:32])
                    v.tensor_copy(v_sb[b][:, pt * 65:pt * 65 + 64], ps_y[:, 320:384]).then_inc(v_rope, 1)

                    v.wait_ge(t_qk, ti + 1)
                    for h in range(4):
                        v.tensor_copy(q_aug[b * 4 + h][0:64, pt * 128:(pt + 1) * 128],
                                      ps_qt[0:64, h * 128:(h + 1) * 128])
                    v.tensor_copy(k_aug[b][0:64, pt * 128:(pt + 1) * 128], ps_kt[:, :]).then_inc(v_qkev, 1)

            @blk.scalar
            def _(s):
                s = _Drain(s)
                s.wait_ge(v_cst, 1)
                ch = 0
                for ti in range(NT_ALL):
                    xb = xt[ti % 2]
                    o = 12 * (ti % 2)
                    s.wait_ge(x_ldp[ti % 2], 16 * (ti // 2 + 1))
                    s.activation(sqs[:], xb[:], AF.Square, accum_out=stat[:, o + 1:o + 2]).then_inc(s_sq, 1)
                    s.wait_ge(v_st, 2 * ti + 1)
                    s.activation(stat[:, o + 9:o + 10], stat[:, o + 7:o + 8], AF.Ln).then_inc(s_sq, 1)
                    s.activation(stat[:, o + 10:o + 11], stat[:, o + 9:o + 10], AF.Exp, scale=-0.5).then_inc(s_sq, 1)
                    s.wait_ge(v_st, 2 * ti + 2)
                    if ti >= 2:
                        s.wait_ge(t_tr, 4 * (ti - 1))
                    s.activation(xnt[ti % 2][:], xb[:], AF.Identity,
                                 bias=stat[:, o + 11:o + 12], scale=stat[:, o + 10:o + 11]).then_inc(s_sq, 1)
                    for _2 in range(2):
                        if ch < 128:
                            s.wait_ge(gu_ldp[ch % 2], 16 * (ch // 2 + 1))
                            s.activation(pabs[:], (pw0 if ch % 2 == 0 else pw1)[:], AF.Abs,
                                         accum_out=wsum[:, ch:ch + 1]).then_inc(s_pw, 1)
                            ch += 1

            @blk.tensor
            def _(t):
                t.wait_ge(v_cst, 1)
                t.wait_ge(win_cv, 16)
                for ti in range(NT_ALL):
                    t.wait_ge(s_sq, 4 * ti + 4)
                    for r in range(4):
                        g4 = 4 * ti + r
                        pst = [ps_tr0, ps_tr1][g4 % 2]
                        if g4 >= 2:
                            t.wait_ge(v_xnT, g4 - 1)
                        ins = None
                        for j in range(4):
                            mi = r * 4 + j
                            ins = t.matmul(pst[:, j * 128:(j + 1) * 128],
                                           xnt[ti % 2][:, mi * 128:(mi + 1) * 128], identb[:],
                                           start=True, stop=True)
                        ins.then_inc(t_tr, 1)
                    t.wait_ge(v_xnT, 4 * ti + 4)
                    if ti >= 1:
                        t.wait_ge(v_rope, ti)
                    ins = None
                    for mi in range(16):
                        ins = t.matmul(ps_y[:], xnT[:, mi * 128:(mi + 1) * 128],
                                       winb[:, mi * CC:(mi + 1) * CC], start=(mi == 0), stop=(mi == 15))
                    ins.then_inc(t_y, 1)
                    t.wait_ge(v_rope, ti + 1)
                    if ti >= 1:
                        t.wait_ge(v_qkev, ti)
                    for h in range(4):
                        t.matmul(ps_qt[0:64, h * 128:(h + 1) * 128],
                                 ropeq[ti % 2][:, h * 64:(h + 1) * 64], identb[:],
                                 start=True, stop=True)
                    t.matmul(ps_kt[:, :], ropek[ti % 2][:], identb[:],
                             start=True, stop=True).then_inc(t_qk, 1)

        _es1.close()
        _es2 = contextlib.ExitStack()
        kpre = _es2.enter_context(nc.sbuf_tensor("kpre", [64, S], F32))
        dotbf = _es2.enter_context(nc.sbuf_tensor("dotbf", [64, S], BF16))
        wred = _es2.enter_context(nc.sbuf_tensor("wred", [128, 2], F32))
        ps_d0 = _es2.enter_context(nc.psum_tensor("ps_d0", [1, 512], F32))
        ps_d1 = _es2.enter_context(nc.psum_tensor("ps_d1", [1, 512], F32))
        ps_w1 = _es2.enter_context(nc.psum_tensor("ps_w1", [1, 2], F32))
        ps_bc = _es2.enter_context(nc.psum_tensor("ps_bc", [128, 2], F32))
        blk2 = _es2.enter_context(nc.Block())
        if True:
            v_scan, t_dot, v_aug = S_("v_scan"), S_("t_dot"), S_("v_aug")
            v_wred, t_w1, v_w1 = S_("v_wred"), S_("t_w1"), S_("v_w1")

            @blk2.vector
            def _(v):
                v = _Drain(v)
                v.wait_ge(S_("s_pw"), 128)
                v.wait_ge(S_("t_qk"), NT_ALL)
                v.reduce_sum(wred[:, 0:1], wsum[:, 0:64], AX.X)
                v.reduce_sum(wred[:, 1:2], wsum[:, 64:128], AX.X).then_inc(v_wred, 2)
                v.wait_ge(t_w1, 2)
                v.tensor_scalar_mul(w1s[:, 0:2], ps_w1[0:1, 0:2], 1.0 / (D * DI))
                v.tensor_scalar_max(w1s[:, 0:2], w1s[:, 0:2], 1e-5).then_inc(v_w1, 1)
                v.wait_ge(t_w1, 4)
                v.tensor_copy(wsc[:, 0:2], ps_bc[:, 0:2])
                v.reciprocal(swc[:, 0:2], wsc[:, 0:2]).then_inc(v_w1, 1)

                v.wait_ge(S_("v_qkev"), NT_ALL)
                for b in range(B):
                    v.tensor_tensor_scan(kpre[:], onescol[0:64, 0:1].to_broadcast([64, S]),
                                         k_aug[b][0:64, :], 0.0, ALU.mult, ALU.add).then_inc(v_scan, 1)
                    for h in range(4):
                        i = b * 4 + h
                        if i > 0:
                            v.wait_ge(t_dot, i * 8)
                        v.tensor_mul(dotbf[:], q_aug[i][0:64, :], kpre[:]).then_inc(v_scan, 1)
                        for c in range(8):
                            v.wait_ge(t_dot, i * 8 + c + 1)
                            psd = [ps_d0, ps_d1][c % 2]
                            v.scalar_tensor_tensor(q_aug[i][64:65, c * 512:(c + 1) * 512],
                                                   psd[0:1, :], -1.0,
                                                   invcnt_sb[0:1, c * 512:(c + 1) * 512],
                                                   ALU.mult, ALU.mult).then_inc(v_aug, 1)

            @blk2.tensor
            def _(t):
                t.wait_ge(v_wred, 2)
                t.matmul(ps_w1[0:1, 0:1], wred[:, 0:1], onescol[:, 0:1], start=True, stop=True)
                t.matmul(ps_w1[0:1, 1:2], wred[:, 1:2], onescol[:, 0:1], start=True, stop=True).then_inc(t_w1, 2)
                t.wait_ge(v_w1, 1)
                t.matmul(ps_bc[:, 0:1], onesrow[0:1, 0:128], w1s[0:1, 0:1], start=True, stop=True)
                t.matmul(ps_bc[:, 1:2], onesrow[0:1, 0:128], w1s[0:1, 1:2], start=True, stop=True).then_inc(t_w1, 2)
                for b in range(B):
                    for h in range(4):
                        i = b * 4 + h
                        t.wait_ge(S_("v_scan"), b + i + 2)
                        for c in range(8):
                            psd = [ps_d0, ps_d1][c % 2]
                            if c >= 2:
                                t.wait_ge(S_("v_aug"), i * 8 + c - 1)
                            t.matmul(psd[0:1, :], onescolb[0:64, 0:1], dotbf[:, c * 512:(c + 1) * 512],
                                     start=True, stop=True).then_inc(t_dot, 1)

        _es2.close()
        _es3 = contextlib.ExitStack()
        oT0 = _es3.enter_context(nc.sbuf_tensor("oT0", [128, T_ALL], BF16))
        oT1 = _es3.enter_context(nc.sbuf_tensor("oT1", [128, T_ALL], BF16))
        if True:
            oT = [oT0, oT1]
            _es4 = contextlib.ExitStack()
            pt0 = _es4.enter_context(nc.sbuf_tensor("pt0", [128, 1024], BF16))
            pt1 = _es4.enter_context(nc.sbuf_tensor("pt1", [128, 1024], BF16))
            rden = _es4.enter_context(nc.sbuf_tensor("rden", [1, 512], BF16))
            rdenf = _es4.enter_context(nc.sbuf_tensor("rdenf", [1, 512], F32))
            sce = _es4.enter_context(nc.sbuf_tensor("sce", [64, 512], BF16))
            tmpo0 = _es4.enter_context(nc.sbuf_tensor("tmpo0", [64, 512], BF16))
            tmpo1 = _es4.enter_context(nc.sbuf_tensor("tmpo1", [64, 512], BF16))
            pwd0 = _es4.enter_context(nc.sbuf_tensor("pwd0", [128, 2048], F32))
            pwd1 = _es4.enter_context(nc.sbuf_tensor("pwd1", [128, 2048], F32))
            ps_s0 = _es4.enter_context(nc.psum_tensor("ps_s0", [128, 1024], F32))
            ps_s1 = _es4.enter_context(nc.psum_tensor("ps_s1", [128, 1024], F32))
            ps_o0 = _es4.enter_context(nc.psum_tensor("ps_o0", [65, 512], F32))
            ps_o1 = _es4.enter_context(nc.psum_tensor("ps_o1", [65, 512], F32))
            ps_e = _es4.enter_context(nc.psum_tensor("ps_e", [64, 512], F32))
            ps_wb = _es4.enter_context(nc.psum_tensor("ps_wb", [128, 2], F32))
            blk3 = _es4.enter_context(nc.Block())
            if True:
                ptile = [pt0, pt1]
                tmpo = [tmpo0, tmpo1]
                ps_s = [ps_s0, ps_s1]
                ps_o = [ps_o0, ps_o1]
                t_qk3, v_msk, s_exp = S_("t_qk3"), S_("v_msk"), S_("s_exp")
                t_av, v_den1, t_ex = S_("t_av"), S_("v_den1"), S_("t_ex")
                v_den2 = S_("v_den2")
                g_crossp = [S_("g_cross0"), S_("g_cross1")]
                wd_ldp = [S_("wd_ld0"), S_("wd_ld1")]
                v_pwd = S_("v_pwd")
                v_wredd, t_w1d, v_w1d = S_("v_wredd"), S_("t_w1d"), S_("v_w1d")

                @blk3.gpsimd
                def _(g):
                    g.wait_ge(S_("v_aug"), 64)
                    g.wait_ge(S_("t_dot"), 64)
                    for u, (b, h, c) in enumerate(units):
                        ch = u
                        if ch < 64:
                            if ch >= 2:
                                g.wait_ge(v_pwd, ch - 1)
                            g.dma_start(out=(pwd0 if ch % 2 == 0 else pwd1)[:],
                                        in_=wdT[ch * 128:(ch + 1) * 128, :]).then_inc(wd_ldp[ch % 2], 16)
                        g.wait_ge(v_den2, u + 1)
                        g.dma_start(out=oT[h // 2][(h % 2) * 64:(h % 2) * 64 + 64,
                                                   b * S + c * 512:b * S + (c + 1) * 512],
                                    in_=tmpo[u % 2][:, :]).then_inc(g_crossp[u % 2], 16)

                @blk3.tensor
                def _(t):
                    t.wait_ge(S_("v_aug"), 64)
                    for u, (b, h, c) in enumerate(units):
                        i = b * 4 + h
                        ng = 2 * c + 2

                        def qk(g2):
                            gg = cum_grp[u] + g2
                            if gg >= 2:
                                t.wait_ge(s_exp, gg - 1)
                            ins = None
                            for j in range(2):
                                kvt = 2 * g2 + j
                                ins = t.matmul(ps_s[gg % 2][:, j * 512:(j + 1) * 512],
                                               k_aug[b][:, kvt * 128:(kvt + 1) * 128],
                                               q_aug[i][:, c * 512:(c + 1) * 512],
                                               start=True, stop=True)
                            ins.then_inc(t_qk3, 1)

                        def av(g2):
                            gg = cum_grp[u] + g2
                            t.wait_ge(s_exp, gg + 1)
                            if g2 == 0 and u >= 2:
                                t.wait_ge(v_den2, u - 1)
                            ins = None
                            for j in range(2):
                                kvt = 2 * g2 + j
                                ins = t.matmul(ps_o[u % 2][:, :],
                                               v_sb[b][:, kvt * 65:(kvt + 1) * 65],
                                               ptile[gg % 2][:, j * 512:(j + 1) * 512],
                                               start=(kvt == 0), stop=(kvt == 2 * ng - 1))
                            ins.then_inc(t_av, 1)

                        for g2 in range(ng):
                            qk(g2)
                            if g2 >= 1:
                                av(g2 - 1)
                        av(ng - 1)
                        t.wait_ge(v_den1, u + 1)
                        t.matmul(ps_e[:, :], ones_r[0:1, 0:64], rden[0:1, :],
                                 start=True, stop=True).then_inc(t_ex, 1)
                    t.wait_ge(v_wredd, 1)
                    t.matmul(ps_wb[0:1, 0:1], wsc[:, 3:4], onescol[:, 0:1], start=True, stop=True).then_inc(t_w1d, 1)
                    t.wait_ge(v_w1d, 1)
                    t.matmul(ps_wb[:, 1:2], onesrow[0:1, 0:128], w1s[0:1, 2:3], start=True, stop=True).then_inc(t_w1d, 1)

                @blk3.vector
                def _(v):
                    v = _Drain(v)
                    wd_ch = 0
                    for u, (b, h, c) in enumerate(units):
                        ng = 2 * c + 2
                        for dg in range(2):
                            g2 = ng - 2 + dg
                            gg = cum_grp[u] + g2
                            v.wait_ge(t_qk3, gg + 1)
                            for j in range(2):
                                kvt = 2 * g2 + j
                                r = kvt - 4 * c
                                v.tensor_add(ps_s[gg % 2][:, j * 512:(j + 1) * 512],
                                             ps_s[gg % 2][:, j * 512:(j + 1) * 512],
                                             masks_sb[:, r * 512:(r + 1) * 512])
                            v.engine_nop().then_inc(v_msk, 1)
                        v.wait_ge(t_av, cum_grp[u + 1])
                        v.reciprocal(rdenf[0:1, :], ps_o[u % 2][64:65, :])
                        v.tensor_copy(rden[0:1, :], rdenf[0:1, :]).then_inc(v_den1, 1)
                        v.wait_ge(t_ex, u + 1)
                        v.tensor_copy(sce[:, :], ps_e[:, :])
                        if u >= 2:
                            v.wait_ge(g_crossp[u % 2], 16 * (u // 2))
                        v.tensor_mul(tmpo[u % 2][:, :], ps_o[u % 2][0:64, :], sce[:, :]).then_inc(v_den2, 1)
                        if wd_ch < 64:
                            v.wait_ge(wd_ldp[wd_ch % 2], 16 * (wd_ch // 2 + 1))
                            v.reduce_sum(wsum[:, 128 + wd_ch:129 + wd_ch],
                                         (pwd0 if wd_ch % 2 == 0 else pwd1)[:], AX.X,
                                         apply_absolute_value=True).then_inc(v_pwd, 1)
                            wd_ch += 1
                    v.reduce_sum(wsc[:, 3:4], wsum[:, 128:192], AX.X).then_inc(v_wredd, 1)
                    v.wait_ge(t_w1d, 1)
                    v.tensor_scalar_mul(w1s[:, 2:3], ps_wb[0:1, 0:1], 1.0 / (D * DI))
                    v.tensor_scalar_max(w1s[:, 2:3], w1s[:, 2:3], 1e-5).then_inc(v_w1d, 1)
                    v.wait_ge(t_w1d, 2)
                    v.tensor_copy(wsc[:, 2:3], ps_wb[:, 1:2])
                    v.reciprocal(swc[:, 2:3], wsc[:, 2:3]).then_inc(v_w1d, 1)

                @blk3.scalar
                def _(s):
                    s = _Drain(s)
                    for u, (b, h, c) in enumerate(units):
                        ng = 2 * c + 2
                        for g2 in range(ng):
                            gg = cum_grp[u] + g2
                            if g2 >= ng - 2:
                                s.wait_ge(v_msk, 2 * u + (g2 - (ng - 2)) + 1)
                            else:
                                s.wait_ge(t_qk3, gg + 1)
                            if gg >= 2:
                                s.wait_ge(t_av, gg - 1)
                            s.activation(ptile[gg % 2][:, :], ps_s[gg % 2][:, :], AF.Exp,
                                         bias=expb[:, 0:1], scale=0.125).then_inc(s_exp, 1)

            _es4.close()
            _es5 = contextlib.ExitStack()
            wo32 = _es5.enter_context(nc.sbuf_tensor("wo32", [128, 512], F32))
            woutb = _es5.enter_context(nc.sbuf_tensor("woutb", [128, 2 * D], BF16))
            rst = _es5.enter_context(nc.sbuf_tensor("rst", [128, 4 * 512], BF16))
            ps_w0 = _es5.enter_context(nc.psum_tensor("ps_w0", [128, 512], F32))
            ps_w1b = _es5.enter_context(nc.psum_tensor("ps_w1b", [128, 512], F32))
            blk4 = _es5.enter_context(nc.Block())
            if True:
                wo_ld, wo_cv, t_wo = S_("wo_ld"), S_("wo_cv"), S_("t_wo")
                v_rs, g_coll = S_("v_rs"), S_("g_coll")
                rs_stp = [S_("rs_st0"), S_("rs_st1"), S_("rs_st2"), S_("rs_st3")]
                ps_w = [ps_w0, ps_w1b]

                @blk4.gpsimd
                def _(g):
                    g.wait_ge(S_("g_cross0"), 16 * (NU // 2))
                    g.wait_ge(S_("g_cross1"), 16 * (NU // 2))
                    for j in range(8):
                        ct, mc = j // 4, j % 4
                        if j >= 1:
                            g.wait_ge(wo_cv, j)
                        g.dma_start(out=wo32[:],
                                    in_=w_outT[ct * 128:(ct + 1) * 128, mc * 512:(mc + 1) * 512]).then_inc(wo_ld, 16)
                    for q4 in range(4):
                        g.wait_ge(rs_stp[q4], 16 * 64)
                    g.collective_compute(
                        "ReduceScatter", ALU.add,
                        replica_groups=[list(range(8))],
                        ins=[rs_in[:, :]],
                        outs=[rs_out[:, :]],
                    ).then_inc(g_coll, 1)

                @blk4.vector
                def _(v):
                    v = _Drain(v)
                    v.wait_ge(S_("g_cross0"), 16 * (NU // 2))
                    v.wait_ge(S_("g_cross1"), 16 * (NU // 2))
                    for j in range(8):
                        v.wait_ge(wo_ld, 16 * (j + 1))
                        ct, mc = j // 4, j % 4
                        v.tensor_copy(woutb[:, ct * D + mc * 512:ct * D + (mc + 1) * 512],
                                      wo32[:]).then_inc(wo_cv, 1)
                    for idx in range(256):
                        v.wait_ge(t_wo, idx + 1)
                        if idx >= 4:
                            v.wait_ge(rs_stp[idx % 4], 16 * (idx // 4))
                        v.tensor_copy(rst[:, (idx % 4) * 512:(idx % 4 + 1) * 512],
                                      ps_w[idx % 2][:, :]).then_inc(v_rs, 1)

                @blk4.tensor
                def _(t):
                    t.wait_ge(wo_cv, 8)
                    t.wait_ge(S_("g_cross0"), 16 * (NU // 2))
                    t.wait_ge(S_("g_cross1"), 16 * (NU // 2))
                    for idx in range(256):
                        tt, mc = idx // 4, idx % 4
                        if idx >= 2:
                            t.wait_ge(v_rs, idx - 1)
                        ins = None
                        for ct in range(2):
                            ins = t.matmul(ps_w[idx % 2][:, :], oT[ct][:, tt * 128:(tt + 1) * 128],
                                           woutb[:, ct * D + mc * 512:ct * D + (mc + 1) * 512],
                                           start=(ct == 0), stop=(ct == 1))
                        ins.then_inc(t_wo, 1)

                @blk4.sync
                def _(sync):
                    for idx in range(256):
                        tt, mc = idx // 4, idx % 4
                        sync.wait_ge(v_rs, idx + 1)
                        sync.dma_start(out=rs_in[tt * 128:(tt + 1) * 128, mc * 512:(mc + 1) * 512],
                                       in_=rst[:, (idx % 4) * 512:(idx % 4 + 1) * 512]).then_inc(rs_stp[idx % 4], 16)

                if debug:
                    @blk4.scalar
                    def _(s):
                        s = _Drain(s)
                        s.wait_ge(S_("g_cross0"), 16 * (NU // 2))
                        s.wait_ge(S_("g_cross1"), 16 * (NU // 2))
                        for i in range(8):
                            s.dma_start(out=tap["qaug"][i * 65:(i + 1) * 65, :], in_=q_aug[i][:, :]).then_inc(S_("tapsem"), 16)
                        for b in range(B):
                            s.dma_start(out=tap["kaug"][b * 65:(b + 1) * 65, :], in_=k_aug[b][:, :]).then_inc(S_("tapsem"), 16)
                            s.dma_start(out=tap["v"][b * 128:(b + 1) * 128, :], in_=v_sb[b][:, :]).then_inc(S_("tapsem"), 16)
                        for p in range(2):
                            s.dma_start(out=tap["oT"][p * 128:(p + 1) * 128, :], in_=oT[p][:, :]).then_inc(S_("tapsem"), 16)
                        s.wait_ge(S_("tapsem"), 16 * 14)

    _es5.close()
    _es3.close()
    _es0.close()
    with nc.sbuf_tensor("hslab", [128, 8 * DI], BF16) as hslab:
        _es6 = contextlib.ExitStack()
        kxT = _es6.enter_context(nc.sbuf_tensor("kxT", [128, 16 * NT_OWN * 128], BF16))
        mxc8 = _es6.enter_context(nc.sbuf_tensor("mxc8", [128, 8], F32))
        gsc8 = _es6.enter_context(nc.sbuf_tensor("gsc8", [128, 8], F32))
        usc8 = _es6.enter_context(nc.sbuf_tensor("usc8", [128, 8], F32))
        if True:
            _es7 = contextlib.ExitStack()
            xo0 = _es7.enter_context(nc.sbuf_tensor("xo0", [128, D], F32))
            xo1 = _es7.enter_context(nc.sbuf_tensor("xo1", [128, D], F32))
            xno0 = _es7.enter_context(nc.sbuf_tensor("xno0", [128, D], BF16))
            xno1 = _es7.enter_context(nc.sbuf_tensor("xno1", [128, D], BF16))
            sq5 = _es7.enter_context(nc.sbuf_tensor("sq5", [128, D], BF16))
            st5 = _es7.enter_context(nc.sbuf_tensor("st5", [128, 24], F32))
            kxf = _es7.enter_context(nc.sbuf_tensor("kxf", [128, D], F32))
            kxb = _es7.enter_context(nc.sbuf_tensor("kxb", [128, D], BF16))
            ps_t50 = _es7.enter_context(nc.psum_tensor("ps_t50", [128, 512], F32))
            ps_t51 = _es7.enter_context(nc.psum_tensor("ps_t51", [128, 512], F32))
            blk5a = _es7.enter_context(nc.Block())
            if True:
                xo = [xo0, xo1]
                xno = [xno0, xno1]
                ps_t5 = [ps_t50, ps_t51]
                xo_ldp = [S_("xo_ld0"), S_("xo_ld1")]
                s_sq5, v_st5, v_red5 = S_("s_sq5"), S_("v_st5"), S_("v_red5")
                v_kx, t_kxT, v_kxT = S_("v_kx"), S_("t_kxT"), S_("v_kxT")
                xp_stp = [S_("xp_st0"), S_("xp_st1")]

                @blk5a.sync
                def _(sync):
                    for q4 in range(4):
                        sync.wait_ge(S_(f"rs_st{q4}"), 16 * 64)
                    def park(j):
                        sync.wait_ge(v_kx, 3 * j + 2)
                        sync.dma_start(out=xn_park[j * 128:(j + 1) * 128, :], in_=xno[j % 2][:]).then_inc(xp_stp[j % 2], 16)
                    for ti in range(NT_OWN):
                        if ti >= 2:
                            sync.wait_ge(s_sq5, 4 * (ti - 2) + 4)
                            sync.wait_ge(v_red5, ti - 1)
                        sync.dma_start(out=xo[ti % 2][:], in_=x_own[ti * 128:(ti + 1) * 128, :]).then_inc(xo_ldp[ti % 2], 16)
                        if ti >= 2:
                            park(ti - 2)
                    park(NT_OWN - 2)
                    park(NT_OWN - 1)

                @blk5a.scalar
                def _(s):
                    s = _Drain(s)
                    for q4 in range(4):
                        s.wait_ge(S_(f"rs_st{q4}"), 16 * 64)
                    for ti in range(NT_OWN):
                        xb = xo[ti % 2]
                        o = 12 * (ti % 2)
                        s.wait_ge(xo_ldp[ti % 2], 16 * (ti // 2 + 1))
                        s.activation(sq5[:], xb[:], AF.Square, accum_out=st5[:, o + 1:o + 2]).then_inc(s_sq5, 1)
                        s.wait_ge(v_st5, 2 * ti + 1)
                        s.activation(st5[:, o + 9:o + 10], st5[:, o + 6:o + 7], AF.Ln).then_inc(s_sq5, 1)
                        s.activation(st5[:, o + 10:o + 11], st5[:, o + 9:o + 10], AF.Exp, scale=-0.5).then_inc(s_sq5, 1)
                        s.wait_ge(v_st5, 2 * ti + 2)
                        if ti >= 2:
                            s.wait_ge(v_kx, 3 * (ti - 2) + 3)
                            s.wait_ge(xp_stp[ti % 2], 16 * (ti // 2))
                        s.activation(xno[ti % 2][:], xb[:], AF.Identity,
                                     bias=st5[:, o + 11:o + 12], scale=st5[:, o + 10:o + 11]).then_inc(s_sq5, 1)

                @blk5a.vector
                def _(v):
                    v = _Drain(v)
                    for q4 in range(4):
                        v.wait_ge(S_(f"rs_st{q4}"), 16 * 64)
                    for ti in range(NT_OWN):
                        xb = xo[ti % 2]
                        o = 12 * (ti % 2)
                        v.wait_ge(xo_ldp[ti % 2], 16 * (ti // 2 + 1))
                        v.reduce_sum(st5[:, o + 0:o + 1], xb[:], AX.X).then_inc(v_red5, 1)
                        v.wait_ge(s_sq5, 4 * ti + 1)
                        v.tensor_scalar_mul(st5[:, o + 2:o + 3], st5[:, o + 0:o + 1], 1.0 / D)
                        v.tensor_scalar_mul(st5[:, o + 3:o + 4], st5[:, o + 1:o + 2], 1.0 / D)
                        v.tensor_scalar(st5[:, o + 4:o + 5], st5[:, o + 2:o + 3], st5[:, o + 2:o + 3], None, ALU.mult)
                        v.tensor_sub(st5[:, o + 5:o + 6], st5[:, o + 3:o + 4], st5[:, o + 4:o + 5])
                        v.tensor_scalar_add(st5[:, o + 6:o + 7], st5[:, o + 5:o + 6], 1e-5).then_inc(v_st5, 1)
                        v.wait_ge(s_sq5, 4 * ti + 3)
                        v.tensor_scalar(st5[:, o + 11:o + 12], st5[:, o + 10:o + 11], st5[:, o + 10:o + 11], None, ALU.mult)
                        v.scalar_tensor_tensor(st5[:, o + 11:o + 12], st5[:, o + 11:o + 12], -0.5,
                                               st5[:, o + 6:o + 7], ALU.mult, ALU.mult)
                        v.tensor_scalar_add(st5[:, o + 11:o + 12], st5[:, o + 11:o + 12], 1.5)
                        v.tensor_mul(st5[:, o + 10:o + 11], st5[:, o + 11:o + 12], st5[:, o + 10:o + 11])
                        v.tensor_scalar(st5[:, o + 11:o + 12], st5[:, o + 2:o + 3],
                                        st5[:, o + 10:o + 11], -1.0, ALU.mult, ALU.mult).then_inc(v_st5, 1)
                        v.wait_ge(s_sq5, 4 * ti + 4)
                        v.reduce_max(st5[:, o + 4:o + 5], xno[ti % 2][:], AX.X, apply_absolute_value=True)
                        v.tensor_scalar_max(st5[:, o + 4:o + 5], st5[:, o + 4:o + 5], 1e-5)
                        v.reciprocal(st5[:, o + 5:o + 6], st5[:, o + 4:o + 5])
                        v.tensor_scalar_mul(st5[:, o + 5:o + 6], st5[:, o + 5:o + 6], 127.0)
                        v.tensor_scalar_mul(mxc8[:, ti:ti + 1], st5[:, o + 4:o + 5], 1.0 / 127.0).then_inc(v_kx, 1)
                        if ti >= 1:
                            v.wait_ge(t_kxT, 4 * ti)
                        v.tensor_scalar(kxf[:], xno[ti % 2][:], st5[:, o + 5:o + 6], 12582912.0,
                                        ALU.mult, ALU.add).then_inc(v_kx, 1)
                        v.tensor_scalar_add(kxb[:], kxf[:], -12582912.0).then_inc(v_kx, 1)
                        for r in range(4):
                            g4 = 4 * ti + r
                            v.wait_ge(t_kxT, g4 + 1)
                            pst = ps_t5[g4 % 2]
                            for j in range(4):
                                hi = r * 4 + j
                                v.tensor_copy(kxT[:, (hi * NT_OWN + ti) * 128:(hi * NT_OWN + ti + 1) * 128],
                                              pst[:, j * 128:(j + 1) * 128])
                            v.engine_nop().then_inc(v_kxT, 1)
                    v.wait_ge(S_("v_w1"), 2)
                    v.tensor_scalar(gsc8[:], mxc8[:], wsc[:, 0:1], None, ALU.mult)
                    v.tensor_scalar(usc8[:], mxc8[:], wsc[:, 1:2], None, ALU.mult).then_inc(v_kx, 1)

                @blk5a.tensor
                def _(t):
                    for q4 in range(4):
                        t.wait_ge(S_(f"rs_st{q4}"), 16 * 64)
                    for ti in range(NT_OWN):
                        t.wait_ge(v_kx, 3 * ti + 3)
                        for r in range(4):
                            g4 = 4 * ti + r
                            pst = ps_t5[g4 % 2]
                            if g4 >= 2:
                                t.wait_ge(v_kxT, g4 - 1)
                            ins = None
                            for j in range(4):
                                hi = r * 4 + j
                                ins = t.matmul(pst[:, j * 128:(j + 1) * 128],
                                               kxb[:, hi * 128:(hi + 1) * 128], identb[:],
                                               start=True, stop=True)
                            ins.then_inc(t_kxT, 1)

            _es7.close()
            _es8 = contextlib.ExitStack()
            wst0 = _es8.enter_context(nc.sbuf_tensor("wst0", [128, 512], F32))
            wst1 = _es8.enter_context(nc.sbuf_tensor("wst1", [128, 512], F32))
            wq32x0 = _es8.enter_context(nc.sbuf_tensor("wq32x0", [128, 512], F32))
            wq32x1 = _es8.enter_context(nc.sbuf_tensor("wq32x1", [128, 512], F32))
            wqi8a0 = _es8.enter_context(nc.sbuf_tensor("wqi8a", [128, 512], I8))
            wqi8a1 = _es8.enter_context(nc.sbuf_tensor("wqi8b", [128, 512], I8))
            wqb0 = _es8.enter_context(nc.sbuf_tensor("wqb0", [128, 512], BF16))
            wqb1 = _es8.enter_context(nc.sbuf_tensor("wqb1", [128, 512], BF16))
            hg = _es8.enter_context(nc.sbuf_tensor("hg", [128, 4 * 512], BF16))
            hu = _es8.enter_context(nc.sbuf_tensor("hu", [128, 4 * 512], BF16))
            ps_g = _es8.enter_context(nc.psum_tensor("ps_g", [128, 4 * 512], F32))
            ps_u = _es8.enter_context(nc.psum_tensor("ps_u", [128, 4 * 512], F32))
            blk5b = _es8.enter_context(nc.Block())
            if True:
                wst = [wst0, wst1]
                wq32x = [wq32x0, wq32x1]
                wqi8 = [wqi8a0, wqi8a1]
                wqb = [wqb0, wqb1]
                wgu_ldp = [S_("wgu_ld0"), S_("wgu_ld1")]
                v_wq, s_wq = S_("v_wq"), S_("s_wq")
                t_gu, s_hg, s_hu, v_h = S_("t_gu"), S_("s_hg"), S_("s_hu"), S_("v_h")

                @blk5b.gpsimd
                def _(g):
                    g.wait_ge(S_("xp_st0"), 16 * (NT_OWN // 2))
                    g.wait_ge(S_("xp_st1"), 16 * (NT_OWN // 2))
                    g.wait_ge(S_("t_kxT"), 4 * NT_OWN)
                    g.wait_ge(S_("v_kxT"), 4 * NT_OWN)
                    if debug:
                        g.dma_start(out=tap["kx"][:, :], in_=kxT[:, :]).then_inc(S_("tapsem4"), 16)
                    k = 0
                    for ic in range(16):
                        for th in range(2):
                            for wsel in range(2):
                                w_ap = wgT if wsel == 0 else wuT
                                for hi in range(16):
                                    if debug and k == 2:
                                        g.wait_ge(S_("v_wq"), 2)
                                        g.dma_start(out=tap["wq"][:, :], in_=wqb[0][:, :]).then_inc(S_("tapsem6"), 16)
                                    if k >= 2:
                                        g.wait_ge(s_wq, k - 1)
                                    g.dma_start(out=wst[k % 2][:],
                                                in_=w_ap[hi * 128:(hi + 1) * 128,
                                                         ic * 512:(ic + 1) * 512]).then_inc(wgu_ldp[k % 2], 16)
                                    k += 1

                @blk5b.vector
                def _(v):
                    v = _Drain(v)
                    v.wait_ge(S_("xp_st0"), 16 * (NT_OWN // 2))
                    v.wait_ge(S_("xp_st1"), 16 * (NT_OWN // 2))
                    v.wait_ge(S_("v_kx"), 3 * NT_OWN + 1)
                    k = 0
                    for ic in range(16):
                        for th in range(2):
                            for wsel in range(2):
                                for hi in range(16):
                                    v.wait_ge(s_wq, k + 1)
                                    if k >= 2:
                                        v.wait_ge(t_gu, k - 1)
                                    if debug and k == 2:
                                        v.wait_ge(S_("tapsem6"), 16)
                                    v.tensor_scalar(wqb[k % 2][:], wq32x[k % 2][:], 1.0, -1.0,
                                                    ALU.min, ALU.max).then_inc(v_wq, 2)
                                    k += 1
                            jj = ic * 2 + th
                            v.wait_ge(s_hu, jj + 1)
                            for q in range(4):
                                tt = th * 4 + q
                                v.tensor_mul(hslab[:, tt * DI + ic * 512:tt * DI + (ic + 1) * 512],
                                             hg[:, q * 512:(q + 1) * 512], hu[:, q * 512:(q + 1) * 512])
                            v.engine_nop().then_inc(v_h, 1)

                @blk5b.scalar
                def _(s):
                    s = _Drain(s)
                    s.wait_ge(S_("xp_st0"), 16 * (NT_OWN // 2))
                    s.wait_ge(S_("xp_st1"), 16 * (NT_OWN // 2))
                    k = 0
                    for ic in range(16):
                        for th in range(2):
                            jj = ic * 2 + th
                            for wsel in range(2):
                                for hi in range(16):
                                    s.wait_ge(wgu_ldp[k % 2], 16 * (k // 2 + 1))
                                    if k >= 2:
                                        s.wait_ge(v_wq, 2 * (k - 1))
                                    s.activation(wq32x[k % 2][:], wst[k % 2][:], AF.Identity,
                                                 bias=mcol[:, 0:1], scale=swc[:, wsel:wsel + 1])
                                    s.activation(wq32x[k % 2][:], wq32x[k % 2][:], AF.Identity,
                                                 bias=nmcol[:, 0:1]).then_inc(s_wq, 1)
                                    k += 1
                            s.wait_ge(t_gu, jj * 32 + 16)
                            if jj >= 1:
                                s.wait_ge(v_h, jj)
                            if debug and jj == 1:
                                s.wait_ge(S_("tapsem5"), 32)
                            ins = None
                            for q in range(4):
                                tt = th * 4 + q
                                ins = s.activation(hg[:, q * 512:(q + 1) * 512], ps_g[:, q * 512:(q + 1) * 512],
                                                   AF.Silu, scale=gsc8[:, tt:tt + 1])
                            ins.then_inc(s_hg, 1)
                            s.wait_ge(t_gu, jj * 32 + 32)
                            ins = None
                            for q in range(4):
                                tt = th * 4 + q
                                ins = s.activation(hu[:, q * 512:(q + 1) * 512], ps_u[:, q * 512:(q + 1) * 512],
                                                   AF.Identity, scale=usc8[:, tt:tt + 1])
                            ins.then_inc(s_hu, 1)
                            if debug and jj == 0:
                                s.dma_start(out=tap["hg"][:, :], in_=hg[:, :]).then_inc(S_("tapsem5"), 16)
                                s.dma_start(out=tap["hu"][:, :], in_=hu[:, :]).then_inc(S_("tapsem5"), 16)

                @blk5b.tensor
                def _(t):
                    k = 0
                    for ic in range(16):
                        for th in range(2):
                            jj = ic * 2 + th
                            for wsel in range(2):
                                psx = ps_g if wsel == 0 else ps_u
                                for hi in range(16):
                                    t.wait_ge(v_wq, 2 * (k + 1))
                                    if hi == 0 and jj >= 1:
                                        t.wait_ge(s_hg if wsel == 0 else s_hu, jj)
                                    ins = None
                                    for q in range(4):
                                        tt = th * 4 + q
                                        ins = t.matmul(psx[:, q * 512:(q + 1) * 512],
                                                       kxT[:, (hi * NT_OWN + tt) * 128:(hi * NT_OWN + tt + 1) * 128],
                                                       wqb[k % 2][:],
                                                       start=(hi == 0), stop=(hi == 15))
                                    ins.then_inc(t_gu, 1)
                                    k += 1

        _es8.close()
        _es6.close()
        _es9 = contextlib.ExitStack()
        lnwT = _es9.enter_context(nc.sbuf_tensor("lnwT", [128, DI], BF16))
        lnw32 = _es9.enter_context(nc.sbuf_tensor("lnw32", [1, 512], F32))
        sq8k = _es9.enter_context(nc.sbuf_tensor("sq8k", [128, DI], BF16))
        hsq8 = _es9.enter_context(nc.sbuf_tensor("hsq8", [128, 8], F32))
        mxh8 = _es9.enter_context(nc.sbuf_tensor("mxh8", [128, 8], F32))
        rs8 = _es9.enter_context(nc.sbuf_tensor("rs8", [128, 8], F32))
        f28 = _es9.enter_context(nc.sbuf_tensor("f28", [128, 8], F32))
        dscr = _es9.enter_context(nc.sbuf_tensor("dscr", [128, 8], F32))
        sqout = _es9.enter_context(nc.sbuf_tensor("sqout", [128, DI], BF16))
        k2f = _es9.enter_context(nc.sbuf_tensor("k2f", [128, DI // 2], F32))
        ktr0 = _es9.enter_context(nc.sbuf_tensor("ktr", [128, 512], BF16))
        ktr1 = _es9.enter_context(nc.sbuf_tensor("ktr1", [128, 512], BF16))
        ps_c0 = _es9.enter_context(nc.psum_tensor("ps_c0", [128, 512], F32))
        ps_c1 = _es9.enter_context(nc.psum_tensor("ps_c1", [128, 512], F32))
        blk5c = _es9.enter_context(nc.Block())
        if True:
            ktr = [ktr0, ktr1]
            ps_c = [ps_c0, ps_c1]
            lnw_ld, t_lnw, v_lnw = S_("lnw_ld"), S_("t_lnw"), S_("v_lnw")
            s_hsq, v_hlw, v_col5, s_ln5 = S_("s_hsq"), S_("v_hlw"), S_("v_col5"), S_("s_ln5")
            v_k2, s_k2 = S_("v_k2"), S_("s_k2")
            t_k2t, v_k2t = S_("t_k2t"), S_("v_k2t")
            k2t_stp = [S_("k2t_st0"), S_("k2t_st1")]

            @blk5c.gpsimd
            def _(g):
                g.wait_ge(S_("t_gu"), 1024)
                g.wait_ge(S_("v_h"), 32)
                if debug:
                    g.wait_ge(S_("tapsem4"), 16)
                for cidx in range(16):
                    if cidx >= 1:
                        g.wait_ge(t_lnw, cidx)
                    g.dma_start(out=lnw32[:], in_=lnw_in[cidx:cidx + 1, :]).then_inc(lnw_ld, 16)

            @blk5c.tensor
            def _(t):
                for cidx in range(16):
                    t.wait_ge(lnw_ld, 16 * (cidx + 1))
                    if cidx >= 2:
                        t.wait_ge(v_lnw, cidx - 1)
                    t.matmul(ps_c[cidx % 2][:, :], onesrow[0:1, 0:128], lnw32[0:1, :],
                             start=True, stop=True).then_inc(t_lnw, 1)
                t.wait_ge(v_lnw, 16)
                for tt in range(NT_OWN):
                    t.wait_ge(v_k2, tt + 1)
                    for r in range(16):
                        g4 = tt * 16 + r
                        if g4 >= 2:
                            t.wait_ge(v_k2t, g4 - 1)
                        ins = None
                        for j in range(4):
                            ii = r * 4 + j
                            ins = t.matmul(ps_c[g4 % 2][:, j * 128:(j + 1) * 128],
                                           hslab[:, tt * DI + ii * 128:tt * DI + (ii + 1) * 128],
                                           identb[:], start=True, stop=True)
                        ins.then_inc(t_k2t, 1)

            @blk5c.vector
            def _(v):
                v = _Drain(v)
                v.wait_ge(S_("v_h"), 32)
                for cidx in range(16):
                    v.wait_ge(t_lnw, cidx + 1)
                    v.tensor_copy(lnwT[:, cidx * 512:(cidx + 1) * 512], ps_c[cidx % 2][:, :]).then_inc(v_lnw, 1)
                for tt in range(NT_OWN):
                    v.tensor_mul(sq8k[:], hslab[:, tt * DI:(tt + 1) * DI], lnwT[:])
                    v.reduce_max(mxh8[:, tt:tt + 1], sq8k[:], AX.X, apply_absolute_value=True).then_inc(v_hlw, 1)
                v.wait_ge(S_("s_hsq"), 8)
                v.tensor_scalar(rs8[:], hsq8[:], 1.0 / DI, 1e-6, ALU.mult, ALU.add).then_inc(v_col5, 1)
                v.wait_ge(s_ln5, 2)
                v.wait_ge(S_("v_w1d"), 2)
                v.tensor_mul(dscr[:], f28[:], f28[:])
                v.tensor_tensor(dscr[:], dscr[:], rs8[:], ALU.mult)
                v.tensor_scalar(dscr[:], dscr[:], -0.5, 1.5, ALU.mult, ALU.add)
                v.tensor_mul(rs8[:], dscr[:], f28[:])
                v.tensor_mul(f28[:], rs8[:], mxh8[:])
                v.tensor_scalar_max(f28[:], f28[:], 1e-5)
                v.tensor_scalar(dsc8p[:], f28[:], wsc[:, 2:3], 1.0 / 127.0, ALU.mult, ALU.mult)
                v.tensor_scalar_mul(mxh8[:], rs8[:], 127.0)
                v.reciprocal(dscr[:], f28[:])
                v.tensor_mul(f28[:], mxh8[:], dscr[:]).then_inc(v_col5, 1)
                for tt in range(NT_OWN):
                    if debug and tt == 0:
                        v.wait_ge(S_("tapsem3"), 16)
                    v.tensor_mul(sq8k[:], hslab[:, tt * DI:(tt + 1) * DI], lnwT[:])
                    for hf in range(2):
                        v.tensor_scalar(k2f[:], sq8k[:, hf * (DI // 2):(hf + 1) * (DI // 2)],
                                        f28[:, tt:tt + 1], 12582912.0, ALU.mult, ALU.add)
                        ins = v.tensor_scalar_add(hslab[:, tt * DI + hf * (DI // 2):tt * DI + (hf + 1) * (DI // 2)],
                                                  k2f[:], -12582912.0)
                    ins.then_inc(v_k2, 1)
                for tt in range(NT_OWN):
                    for r in range(16):
                        g4 = tt * 16 + r
                        v.wait_ge(t_k2t, g4 + 1)
                        if g4 >= 2:
                            v.wait_ge(k2t_stp[g4 % 2], 16 * (g4 // 2))
                        v.tensor_copy(ktr[g4 % 2][:], ps_c[g4 % 2][:, :]).then_inc(v_k2t, 1)

            @blk5c.scalar
            def _(s):
                s = _Drain(s)
                for tt in range(NT_OWN):
                    s.wait_ge(S_("v_h"), 32)
                    s.activation(sqout[:], hslab[:, tt * DI:(tt + 1) * DI], AF.Square,
                                 accum_out=hsq8[:, tt:tt + 1]).then_inc(s_hsq, 1)
                s.wait_ge(v_col5, 1)
                s.activation(f28[:], rs8[:], AF.Ln).then_inc(s_ln5, 1)
                s.activation(f28[:], f28[:], AF.Exp, scale=-0.5).then_inc(s_ln5, 1)
                s.wait_ge(S_("tapsem3"), 16) if debug else None

            @blk5c.sync
            def _(sync):
                if debug:
                    tps = S_("tapsem3")
                    sync.wait_ge(S_("s_hsq"), 8)
                    sync.dma_start(out=tap["h"][:, :], in_=hslab[:, 0:DI]).then_inc(tps, 16)
                for tt in range(NT_OWN):
                    for r in range(16):
                        g4 = tt * 16 + r
                        sync.wait_ge(v_k2t, g4 + 1)
                        dst = k2t_d[(r * 4) * 128:(r * 4 + 4) * 128,
                                    tt * 128:(tt + 1) * 128].rearrange("(n p) c -> p n c", p=128)
                        srcv = ktr[g4 % 2][:].rearrange("p (n c) -> p n c", n=4)
                        sync.dma_start(out=dst, in_=srcv).then_inc(k2t_stp[g4 % 2], 16)

        _es9.close()
    gslab = sb("gslab", [128, 8 * D], BF16)
    _es10 = contextlib.ExitStack()
    wst5a = _es10.enter_context(nc.sbuf_tensor("wst5a", [128, 512], F32))
    wst5b = _es10.enter_context(nc.sbuf_tensor("wst5b", [128, 512], F32))
    wq32dx0 = _es10.enter_context(nc.sbuf_tensor("wq32dx0", [128, 512], F32))
    wq32dx1 = _es10.enter_context(nc.sbuf_tensor("wq32dx1", [128, 512], F32))
    wqi8d0 = _es10.enter_context(nc.sbuf_tensor("wqi8d0", [128, 512], I8))
    wqi8d1 = _es10.enter_context(nc.sbuf_tensor("wqi8d1", [128, 512], I8))
    wqbd0 = _es10.enter_context(nc.sbuf_tensor("wqbd0", [128, 512], BF16))
    wqbd1 = _es10.enter_context(nc.sbuf_tensor("wqbd1", [128, 512], BF16))
    k2c0 = _es10.enter_context(nc.sbuf_tensor("k2c0", [128, 1024], BF16))
    k2c1 = _es10.enter_context(nc.sbuf_tensor("k2c1", [128, 1024], BF16))
    ps_dn0 = _es10.enter_context(nc.psum_tensor("ps_dn0", [128, 512], F32))
    ps_dn1 = _es10.enter_context(nc.psum_tensor("ps_dn1", [128, 512], F32))
    ps_dn2 = _es10.enter_context(nc.psum_tensor("ps_dn2", [128, 512], F32))
    ps_dn3 = _es10.enter_context(nc.psum_tensor("ps_dn3", [128, 512], F32))
    ps_dn4 = _es10.enter_context(nc.psum_tensor("ps_dn4", [128, 512], F32))
    ps_dn5 = _es10.enter_context(nc.psum_tensor("ps_dn5", [128, 512], F32))
    ps_dn6 = _es10.enter_context(nc.psum_tensor("ps_dn6", [128, 512], F32))
    ps_dn7 = _es10.enter_context(nc.psum_tensor("ps_dn7", [128, 512], F32))
    blk5e = _es10.enter_context(nc.Block())
    if True:
        wst5 = [wst5a, wst5b]
        wq32dx = [wq32dx0, wq32dx1]
        wqi8d = [wqi8d0, wqi8d1]
        wqbd = [wqbd0, wqbd1]
        k2c = [k2c0, k2c1]
        ps_dn = [ps_dn0, ps_dn1, ps_dn2, ps_dn3, ps_dn4, ps_dn5, ps_dn6, ps_dn7]
        wd2_ldp = [S_("wd2_ld0"), S_("wd2_ld1")]
        k2_ldp = [S_("k2_ld0"), S_("k2_ld1")]
        v_wq2, s_wq2 = S_("v_wq2"), S_("s_wq2")
        t_dn, s_g = S_("t_dn"), S_("s_g")

        @blk5e.gpsimd
        def _(g):
            g.wait_ge(S_("k2t_st0"), 16 * 64)
            g.wait_ge(S_("k2t_st1"), 16 * 64)
            k = 0
            for mc in range(4):
                for ii in range(64):
                    if k >= 2:
                        g.wait_ge(s_wq2, k - 1)
                    g.dma_start(out=wst5[k % 2][:],
                                in_=wdT[ii * 128:(ii + 1) * 128, mc * 512:(mc + 1) * 512]).then_inc(wd2_ldp[k % 2], 16)
                    k += 1

        @blk5e.sync
        def _(sync):
            sync.wait_ge(S_("k2t_st0"), 16 * 64)
            sync.wait_ge(S_("k2t_st1"), 16 * 64)
            k = 0
            for mc in range(4):
                for ii in range(64):
                    if k >= 2:
                        sync.wait_ge(t_dn, 8 * (k - 1))
                    sync.dma_start(out=k2c[k % 2][:], in_=k2t_d[ii * 128:(ii + 1) * 128, :]).then_inc(k2_ldp[k % 2], 16)
                    k += 1

        @blk5e.vector
        def _(v):
            v = _Drain(v)
            k = 0
            for mc in range(4):
                for ii in range(64):
                    v.wait_ge(s_wq2, k + 1)
                    if k >= 2:
                        v.wait_ge(t_dn, 8 * (k - 1))
                    v.tensor_scalar(wqbd[k % 2][:], wq32dx[k % 2][:], 1.0, -1.0,
                                    ALU.min, ALU.max).then_inc(v_wq2, 2)
                    k += 1

        @blk5e.scalar
        def _(s):
            s = _Drain(s)
            k = 0
            for mc in range(4):
                for ii in range(64):
                    s.wait_ge(wd2_ldp[k % 2], 16 * (k // 2 + 1))
                    if k >= 2:
                        s.wait_ge(v_wq2, 2 * (k - 1))
                    s.activation(wq32dx[k % 2][:], wst5[k % 2][:], AF.Identity,
                                 bias=mcol[:, 0:1], scale=swc[:, 2:3])
                    s.activation(wq32dx[k % 2][:], wq32dx[k % 2][:], AF.Identity,
                                 bias=nmcol[:, 0:1]).then_inc(s_wq2, 1)
                    k += 1
                if mc == 0:
                    s.wait_ge(S_("v_col5"), 2)
                s.wait_ge(t_dn, 8 * 64 * (mc + 1))
                ins = None
                for tt in range(8):
                    ins = s.activation(gslab[:, tt * D + mc * 512:tt * D + (mc + 1) * 512],
                                       ps_dn[tt][:, :], AF.Silu, scale=dsc8p[:, tt:tt + 1])
                ins.then_inc(s_g, 1)

        @blk5e.tensor
        def _(t):
            for mc in range(4):
                if mc >= 1:
                    t.wait_ge(s_g, mc)
                for ii in range(64):
                    k = mc * 64 + ii
                    t.wait_ge(v_wq2, 2 * (k + 1))
                    t.wait_ge(k2_ldp[k % 2], 16 * (k // 2 + 1))
                    ins = None
                    for tt in range(8):
                        ins = t.matmul(ps_dn[tt][:, :],
                                       k2c[k % 2][:, tt * 128:(tt + 1) * 128],
                                       wqbd[k % 2][:],
                                       start=(ii == 0), stop=(ii == 63))
                    ins.then_inc(t_dn, 8)

    _es10.close()
    _es11 = contextlib.ExitStack()
    rso = _es11.enter_context(nc.sbuf_tensor("rso", [128, D], BF16))
    xnp = _es11.enter_context(nc.sbuf_tensor("xnp", [128, D], BF16))
    xob = _es11.enter_context(nc.sbuf_tensor("xob", [128, D], F32))
    srow = _es11.enter_context(nc.sbuf_tensor("srow", [128, D], F32))
    sgrow = _es11.enter_context(nc.sbuf_tensor("sgrow", [128, D], F32))
    outb0 = _es11.enter_context(nc.sbuf_tensor("outb0", [128, D], F32))
    outb1 = _es11.enter_context(nc.sbuf_tensor("outb1", [128, D], F32))
    blk6 = _es11.enter_context(nc.Block())
    if True:
        outb = [outb0, outb1]
        rso_ld, v_out = S_("rso_ld"), S_("v_out")
        out_stp = [S_("out_st0"), S_("out_st1")]

        @blk6.sync
        def _(sync):
            sync.wait_ge(S_("g_coll"), 1)
            sync.wait_ge(S_("s_g"), 4)
            for tt in range(NT_OWN):
                if tt >= 1:
                    sync.wait_ge(v_out, 3 * tt)
                sync.dma_start(out=rso[:], in_=rs_out[tt * 128:(tt + 1) * 128, :]).then_inc(rso_ld, 16)
                sync.dma_start(out=xnp[:], in_=xn_park[tt * 128:(tt + 1) * 128, :]).then_inc(rso_ld, 16)
                sync.dma_start(out=xob[:], in_=x_own[tt * 128:(tt + 1) * 128, :]).then_inc(rso_ld, 16)
                sync.wait_ge(v_out, 3 * tt + 3)
                sync.dma_start(out=out[tt * 128:(tt + 1) * 128, :], in_=outb[tt % 2][:]).then_inc(out_stp[tt % 2], 16)
            if debug:
                tapsem = S_("tapsem2")
                sync.dma_start(out=tap["xn"][:, :], in_=xn_park[:, :]).then_inc(tapsem, 16)
                sync.dma_start(out=tap["wsc"][:, 0:3], in_=wsc[:, 0:3]).then_inc(tapsem, 16)
                sync.dma_start(out=tap["wsc"][:, 4:7], in_=swc[:, 0:3]).then_inc(tapsem, 16)
                sync.dma_start(out=tap["rs"][:, :], in_=rs_out[:, :]).then_inc(tapsem, 16)
                for tt in range(NT_OWN):
                    sync.dma_start(out=tap["g"][tt * 128:(tt + 1) * 128, :],
                                   in_=gslab[:, tt * D:(tt + 1) * D]).then_inc(tapsem, 16)
                sync.wait_ge(tapsem, 16 * 12)
            sync.wait_ge(out_stp[0], 16 * (NT_OWN // 2))
            sync.wait_ge(out_stp[1], 16 * (NT_OWN // 2))

        @blk6.vector
        def _(v):
            v = _Drain(v)
            v.wait_ge(S_("s_g"), 4)
            for tt in range(NT_OWN):
                v.wait_ge(rso_ld, 16 * (3 * tt + 3))
                v.tensor_add(srow[:], rso[:], xnp[:]).then_inc(v_out, 1)
                v.tensor_mul(sgrow[:], srow[:], gslab[:, tt * D:(tt + 1) * D]).then_inc(v_out, 1)
                if tt >= 2:
                    v.wait_ge(out_stp[tt % 2], 16 * (tt // 2))
                v.tensor_add(outb[tt % 2][:], sgrow[:], xob[:]).then_inc(v_out, 1)

    _es11.close()
    return nc, tap


def _host_constants():
    d = 64
    inv = 1.0 / (10000.0 ** (np.arange(0, d, 2, dtype=np.float64) / d))
    pos = np.arange(S, dtype=np.float64)
    ang = pos[:, None] * inv[None, :]
    cos = np.cos(ang).astype(np.float32)
    sin = np.sin(ang).astype(np.float32)
    tc = np.zeros((128, 32 * NTB), np.float32)
    ts_ = np.zeros((128, 32 * NTB), np.float32)
    for pt in range(NTB):
        tc[:, pt * 32:(pt + 1) * 32] = cos[pt * 128:(pt + 1) * 128, :]
        ts_[:, pt * 32:(pt + 1) * 32] = sin[pt * 128:(pt + 1) * 128, :]
    mk = np.zeros((128, 4 * 512), np.float32)
    for r in range(4):
        p = np.arange(128)[:, None]
        f = np.arange(512)[None, :]
        mk[:, r * 512:(r + 1) * 512] = np.where(r * 128 + p > f, -1.0e30, 0.0)
    ident = np.eye(128, dtype=np.float32)
    invc = (1.0 / (np.arange(S, dtype=np.float64) + 1.0)).astype(ml_dtypes.bfloat16).reshape(1, S)
    return tc, ts_, mk, ident, invc


def kernel(L0_attn_x, L0_attn_w_in, L0_attn_w_out, L0_attn_rms_w,
           L0_ffn_w_gate, L0_ffn_w_up, L0_ffn_w_down, L0_ffn_ln_weight):
    x = np.ascontiguousarray(np.asarray(L0_attn_x, dtype=np.float32).reshape(T_ALL, D))
    w_in = np.asarray(L0_attn_w_in, dtype=np.float32)
    w_out = np.asarray(L0_attn_w_out, dtype=np.float32)
    rms_w = np.asarray(L0_attn_rms_w, dtype=np.float32)
    wg = np.asarray(L0_ffn_w_gate, dtype=np.float32)
    wu = np.asarray(L0_ffn_w_up, dtype=np.float32)
    wd = np.asarray(L0_ffn_w_down, dtype=np.float32)
    lnw = np.asarray(L0_ffn_ln_weight, dtype=np.float32)

    tc, ts_, mk, ident, invc = _host_constants()
    wgT = np.ascontiguousarray(wg.T)
    wuT = np.ascontiguousarray(wu.T)
    wdT = np.ascontiguousarray(wd.T)
    rms_colh = np.ascontiguousarray(rms_w.reshape(16, 128).T)
    lnw_h = np.ascontiguousarray(lnw.reshape(16, 512))

    nc, _ = build_nc(debug=False)
    in_maps = []
    for g in range(8):
        rows = np.concatenate([w_in[g * 256:(g + 1) * 256],
                               w_in[2048 + g * 64:2048 + (g + 1) * 64],
                               w_in[2560 + g * 64:2560 + (g + 1) * 64]], axis=0)
        w_inT_g = np.ascontiguousarray(rows.T)
        w_outT_g = np.ascontiguousarray(w_out[:, g * 256:(g + 1) * 256].T)
        in_maps.append({
            "x_all": x,
            "x_own": np.ascontiguousarray(x[g * T_OWN:(g + 1) * T_OWN]),
            "w_inT": w_inT_g,
            "w_outT": w_outT_g,
            "rms_col": rms_colh,
            "wgT": wgT, "wuT": wuT, "wdT": wdT,
            "lnw_in": lnw_h,
            "trig_cos": tc, "trig_sin": ts_,
            "masks": mk, "ident_i": ident, "invcnt": invc,
        })
    res = run_bass_kernel_spmd(nc, in_maps, core_ids=list(range(8)))
    outs = [res.results[g]["out"] for g in range(8)]
    return np.concatenate(outs, axis=0).reshape(B, S, D).astype(np.float32)


# version 04
# speedup vs baseline: 1.0374x; 1.0374x over previous
import numpy as np
import ml_dtypes
import contextlib
import concourse.bass as bass
import concourse.mybir as mybir
from concourse.bass_utils import run_bass_kernel_spmd

F32 = mybir.dt.float32
BF16 = mybir.dt.bfloat16
I8 = mybir.dt.int8
AF = mybir.ActivationFunctionType
ALU = mybir.AluOpType
AX = mybir.AxisListType

B = 2
S = 4096
D = 2048
NTB = S // 128
NT_ALL = B * NTB
T_ALL = B * S
T_OWN = 1024
NT_OWN = 8
CC = 384
DI = 8192




_DRAIN_OPS = {
    "tensor_copy", "tensor_tensor", "tensor_add", "tensor_sub", "tensor_mul",
    "tensor_max", "tensor_scalar", "tensor_single_scalar", "tensor_scalar_add",
    "tensor_scalar_sub", "tensor_scalar_mul", "tensor_scalar_max", "tensor_scalar_min",
    "tensor_relu", "tensor_reduce", "reduce_max", "reduce_sum", "scalar_tensor_tensor",
    "tensor_tensor_scan", "memset", "activation", "copy", "mul", "add", "square",
    "sqrt", "sign", "reciprocal", "select", "bn_stats", "bn_aggr", "transpose",
    "stream_shuffle", "pool", "pool_max", "pool_avg", "copy_predicated", "memzero",
}


class _Drain:

    def __init__(self, eng):
        self._e = eng

    def __getattr__(self, n):
        f = getattr(self._e, n)
        if n in _DRAIN_OPS:
            def wrapped(*a, **kw):
                self._e.drain()
                return f(*a, **kw)
            return wrapped
        return f


def build_nc(debug=False):
    nc = bass.Bass("TRN2", target_bir_lowering=False, debug=False)

    x_all = nc.dram_tensor("x_all", [T_ALL, D], F32, kind="ExternalInput").ap()
    x_own = nc.dram_tensor("x_own", [T_OWN, D], F32, kind="ExternalInput").ap()
    w_inT = nc.dram_tensor("w_inT", [D, CC], F32, kind="ExternalInput").ap()
    w_outT = nc.dram_tensor("w_outT", [256, D], F32, kind="ExternalInput").ap()
    rms_col = nc.dram_tensor("rms_col", [128, 16], F32, kind="ExternalInput").ap()
    wgT = nc.dram_tensor("wgT", [D, DI], F32, kind="ExternalInput").ap()
    wuT = nc.dram_tensor("wuT", [D, DI], F32, kind="ExternalInput").ap()
    wdT = nc.dram_tensor("wdT", [DI, D], F32, kind="ExternalInput").ap()
    lnw_in = nc.dram_tensor("lnw_in", [16, 512], F32, kind="ExternalInput").ap()
    trig_cos = nc.dram_tensor("trig_cos", [128, 32 * NTB], F32, kind="ExternalInput").ap()
    trig_sin = nc.dram_tensor("trig_sin", [128, 32 * NTB], F32, kind="ExternalInput").ap()
    masks = nc.dram_tensor("masks", [128, 4 * 512], F32, kind="ExternalInput").ap()
    ident_i = nc.dram_tensor("ident_i", [128, 128], F32, kind="ExternalInput").ap()
    invcnt = nc.dram_tensor("invcnt", [1, S], BF16, kind="ExternalInput").ap()

    out = nc.dram_tensor("out", [T_OWN, D], F32, kind="ExternalOutput").ap()

    rs_in = nc.dram_tensor("rs_in", [T_ALL, D], BF16, kind="Internal").ap()
    rs_out = nc.dram_tensor("rs_out", [T_OWN, D], BF16, kind="Internal").ap()
    k2t_d = nc.dram_tensor("k2t_d", [DI, T_OWN], BF16, kind="Internal").ap()
    xn_park = nc.dram_tensor("xn_park", [T_OWN, D], BF16, kind="Internal").ap()

    tap = {}
    if debug:
        tap["qaug"] = nc.dram_tensor("tap_qaug", [8 * 65, S], BF16, kind="ExternalOutput").ap()
        tap["kaug"] = nc.dram_tensor("tap_kaug", [B * 65, S], BF16, kind="ExternalOutput").ap()
        tap["v"] = nc.dram_tensor("tap_v", [B * 128, NTB * 65], BF16, kind="ExternalOutput").ap()
        tap["oT"] = nc.dram_tensor("tap_oT", [2 * 128, T_ALL], BF16, kind="ExternalOutput").ap()
        tap["xn"] = nc.dram_tensor("tap_xn", [T_OWN, D], BF16, kind="ExternalOutput").ap()
        tap["g"] = nc.dram_tensor("tap_g", [T_OWN, D], BF16, kind="ExternalOutput").ap()
        tap["wsc"] = nc.dram_tensor("tap_wsc", [128, 8], F32, kind="ExternalOutput").ap()
        tap["rs"] = nc.dram_tensor("tap_rs", [T_OWN, D], BF16, kind="ExternalOutput").ap()
        tap["h"] = nc.dram_tensor("tap_h", [128, DI], BF16, kind="ExternalOutput").ap()
        tap["kx"] = nc.dram_tensor("tap_kx", [128, 16 * NT_OWN * 128], BF16, kind="ExternalOutput").ap()
        tap["hg"] = nc.dram_tensor("tap_hg", [128, 4 * 512], BF16, kind="ExternalOutput").ap()
        tap["wq"] = nc.dram_tensor("tap_wq", [128, 512], BF16, kind="ExternalOutput").ap()
        tap["hu"] = nc.dram_tensor("tap_hu", [128, 4 * 512], BF16, kind="ExternalOutput").ap()

    sb = nc.alloc_sbuf_tensor
    sems = {}

    def S_(name):
        if name not in sems:
            sems[name] = nc.alloc_semaphore(name)
        return sems[name]

    identb = sb("identb", [128, 128], BF16)
    ones_r = sb("ones_r", [1, 128], BF16)
    onesrow = sb("onesrow", [1, 512], F32)
    onescol = sb("onescol", [128, 1], F32)
    onescolb = sb("onescolb", [128, 1], BF16)
    c127 = sb("c127", [128, 1], F32)
    mcol = sb("mcol", [128, 1], F32)
    nmcol = sb("nmcol", [128, 1], F32)
    expb = sb("expb", [128, 1], F32)
    wsum = sb("wsum", [128, 192], F32)
    w1s = sb("w1s", [1, 4], F32)
    wsc = sb("wsc", [128, 4], F32)
    swc = sb("swc", [128, 4], F32)
    dsc8p = sb("dsc8p", [128, 8], F32)

    units = [(b, h, c) for b in range(B) for h in range(4) for c in range(8)]
    NU = len(units)
    grp_of = [2 * c + 2 for (_, _, c) in units]
    cum_grp = np.concatenate([[0], np.cumsum(grp_of)]).astype(int).tolist()
    NGRP = cum_grp[-1]

    _es0 = contextlib.ExitStack()
    masks_sb = _es0.enter_context(nc.sbuf_tensor("masks_sb", [128, 4 * 512], F32))
    invcnt_sb = _es0.enter_context(nc.sbuf_tensor("invcnt_sb", [1, S], BF16))
    winb = _es0.enter_context(nc.sbuf_tensor("winb", [128, 16 * CC], BF16))
    rmsc = _es0.enter_context(nc.sbuf_tensor("rmsc", [128, 16], F32))
    qa0 = _es0.enter_context(nc.sbuf_tensor("qa0", [65, S], BF16))
    qa1 = _es0.enter_context(nc.sbuf_tensor("qa1", [65, S], BF16))
    qa2 = _es0.enter_context(nc.sbuf_tensor("qa2", [65, S], BF16))
    qa3 = _es0.enter_context(nc.sbuf_tensor("qa3", [65, S], BF16))
    qa4 = _es0.enter_context(nc.sbuf_tensor("qa4", [65, S], BF16))
    qa5 = _es0.enter_context(nc.sbuf_tensor("qa5", [65, S], BF16))
    qa6 = _es0.enter_context(nc.sbuf_tensor("qa6", [65, S], BF16))
    qa7 = _es0.enter_context(nc.sbuf_tensor("qa7", [65, S], BF16))
    ka0 = _es0.enter_context(nc.sbuf_tensor("ka0", [65, S], BF16))
    ka1 = _es0.enter_context(nc.sbuf_tensor("ka1", [65, S], BF16))
    vs0 = _es0.enter_context(nc.sbuf_tensor("vs0", [128, NTB * 65], BF16))
    vs1 = _es0.enter_context(nc.sbuf_tensor("vs1", [128, NTB * 65], BF16))
    if True:
        q_aug = [qa0, qa1, qa2, qa3, qa4, qa5, qa6, qa7]
        k_aug = [ka0, ka1]
        v_sb = [vs0, vs1]

        _es1 = contextlib.ExitStack()
        trigc = _es1.enter_context(nc.sbuf_tensor("trigc", [128, 32 * NTB], F32))
        trigs = _es1.enter_context(nc.sbuf_tensor("trigs", [128, 32 * NTB], F32))
        ident32 = _es1.enter_context(nc.sbuf_tensor("ident32", [128, 128], F32))
        win32 = _es1.enter_context(nc.sbuf_tensor("win32", [128, CC], F32))
        xt0 = _es1.enter_context(nc.sbuf_tensor("xt0", [128, D], F32))
        xt1 = _es1.enter_context(nc.sbuf_tensor("xt1", [128, D], F32))
        xnt0 = _es1.enter_context(nc.sbuf_tensor("xnt0", [128, D], BF16))
        xnt1 = _es1.enter_context(nc.sbuf_tensor("xnt1", [128, D], BF16))
        xnT = _es1.enter_context(nc.sbuf_tensor("xnT", [128, 16 * 128], BF16))
        sqs = _es1.enter_context(nc.sbuf_tensor("sqs", [128, D], BF16))
        stat = _es1.enter_context(nc.sbuf_tensor("stat", [128, 24], F32))
        ropeq0 = _es1.enter_context(nc.sbuf_tensor("ropeq0", [128, 256], BF16))
        ropeq1 = _es1.enter_context(nc.sbuf_tensor("ropeq1", [128, 256], BF16))
        ropek0 = _es1.enter_context(nc.sbuf_tensor("ropek0", [128, 64], BF16))
        ropek1 = _es1.enter_context(nc.sbuf_tensor("ropek1", [128, 64], BF16))
        rsc1 = _es1.enter_context(nc.sbuf_tensor("rsc1", [128, 128], F32))
        rsc2 = _es1.enter_context(nc.sbuf_tensor("rsc2", [128, 128], F32))
        pw0 = _es1.enter_context(nc.sbuf_tensor("pw0", [128, 2048], F32))
        pw1 = _es1.enter_context(nc.sbuf_tensor("pw1", [128, 2048], F32))
        pabs = _es1.enter_context(nc.sbuf_tensor("pabs", [128, 2048], BF16))
        ps_tr0 = _es1.enter_context(nc.psum_tensor("ps_tr0", [128, 512], F32))
        ps_tr1 = _es1.enter_context(nc.psum_tensor("ps_tr1", [128, 512], F32))
        ps_y = _es1.enter_context(nc.psum_tensor("ps_y", [128, CC], F32))
        ps_qt = _es1.enter_context(nc.psum_tensor("ps_qt", [64, 512], F32))
        ps_kt = _es1.enter_context(nc.psum_tensor("ps_kt", [64, 128], F32))
        blk = _es1.enter_context(nc.Block())
        if True:
            xt = [xt0, xt1]
            xnt = [xnt0, xnt1]
            ropeq = [ropeq0, ropeq1]
            ropek = [ropek0, ropek1]
            c_ld, win_ld = S_("c_ld"), S_("win_ld")
            x_ldp = [S_("x_ld0"), S_("x_ld1")]
            v_cst, win_cv = S_("v_cst"), S_("win_cv")
            s_sq, v_st, v_red = S_("s_sq"), S_("v_st"), S_("v_red")
            t_tr, v_xnT, t_y = S_("t_tr"), S_("v_xnT"), S_("t_y")
            v_rope, t_qk, v_qkev = S_("v_rope"), S_("t_qk"), S_("v_qkev")
            gu_ldp = [S_("gu_ld0"), S_("gu_ld1")]
            s_pw = S_("s_pw")

            @blk.sync
            def _(sync):
                for dst, src in [(trigc[:], trig_cos[:]), (trigs[:], trig_sin[:]),
                                 (masks_sb[:], masks[:]), (invcnt_sb[:], invcnt[:]),
                                 (rmsc[:], rms_col[:]), (ident32[:], ident_i[:])]:
                    sync.dma_start(out=dst, in_=src).then_inc(c_ld, 16)
                for ti in range(NT_ALL):
                    if ti >= 2:
                        sync.wait_ge(s_sq, 4 * (ti - 2) + 4)
                        sync.wait_ge(v_red, ti - 1)
                    sync.dma_start(out=xt[ti % 2][:], in_=x_all[ti * 128:(ti + 1) * 128, :]).then_inc(x_ldp[ti % 2], 16)

            @blk.gpsimd
            def _(g):
                for mi in range(16):
                    if mi >= 1:
                        g.wait_ge(win_cv, mi)
                    g.dma_start(out=win32[:], in_=w_inT[mi * 128:(mi + 1) * 128, :]).then_inc(win_ld, 16)
                ch = 0
                for w_ap in (wgT, wuT):
                    for r in range(16):
                        for cc2 in range(4):
                            if ch >= 2:
                                g.wait_ge(s_pw, ch - 1)
                            g.dma_start(out=(pw0 if ch % 2 == 0 else pw1)[:],
                                        in_=w_ap[r * 128:(r + 1) * 128, cc2 * 2048:(cc2 + 1) * 2048]).then_inc(gu_ldp[ch % 2], 16)
                            ch += 1

            @blk.vector
            def _(v):
                v = _Drain(v)
                v.wait_ge(c_ld, 16 * 6)
                v.memset(ones_r[:], 1.0)
                v.memset(onesrow[:], 1.0)
                v.memset(onescol[:], 1.0)
                v.memset(onescolb[:], 1.0)
                v.memset(c127[:], 127.0)
                v.memset(mcol[:], 12582912.0)
                v.memset(nmcol[:], -12582912.0)
                v.memset(expb[:], -40.0)
                v.tensor_copy(identb[:], ident32[:])
                for b in range(B):
                    v.memset(k_aug[b][64:65, :], 1.0)
                    v.memset(v_sb[b].rearrange("p (n c) -> p n c", c=65)[:, :, 64:65], 1.0)
                v.memset(wsum[:], 0.0).then_inc(v_cst, 1)
                for mi in range(16):
                    v.wait_ge(win_ld, 16 * (mi + 1))
                    v.tensor_scalar(winb[:, mi * CC:(mi + 1) * CC], win32[:],
                                    rmsc[:, mi:mi + 1], None, ALU.mult).then_inc(win_cv, 1)

                for ti in range(NT_ALL):
                    xb = xt[ti % 2]
                    o = 12 * (ti % 2)
                    v.wait_ge(x_ldp[ti % 2], 16 * (ti // 2 + 1))
                    v.reduce_sum(stat[:, o + 0:o + 1], xb[:], AX.X).then_inc(v_red, 1)
                    v.wait_ge(s_sq, 4 * ti + 1)
                    v.tensor_scalar_mul(stat[:, o + 2:o + 3], stat[:, o + 0:o + 1], 1.0 / D)
                    v.tensor_scalar_mul(stat[:, o + 3:o + 4], stat[:, o + 1:o + 2], 1.0 / D)
                    v.tensor_scalar(stat[:, o + 4:o + 5], stat[:, o + 2:o + 3], stat[:, o + 2:o + 3], None, ALU.mult)
                    v.tensor_sub(stat[:, o + 5:o + 6], stat[:, o + 3:o + 4], stat[:, o + 4:o + 5])
                    v.tensor_scalar_add(stat[:, o + 6:o + 7], stat[:, o + 5:o + 6], 1e-5)
                    v.scalar_tensor_tensor(stat[:, o + 7:o + 8], stat[:, o + 6:o + 7], 1e-6,
                                           stat[:, o + 5:o + 6], ALU.mult, ALU.add).then_inc(v_st, 1)
                    v.wait_ge(s_sq, 4 * ti + 3)
                    v.tensor_scalar(stat[:, o + 11:o + 12], stat[:, o + 10:o + 11], stat[:, o + 10:o + 11], None, ALU.mult)
                    v.scalar_tensor_tensor(stat[:, o + 11:o + 12], stat[:, o + 11:o + 12], -0.5,
                                           stat[:, o + 7:o + 8], ALU.mult, ALU.mult)
                    v.tensor_scalar_add(stat[:, o + 11:o + 12], stat[:, o + 11:o + 12], 1.5)
                    v.tensor_mul(stat[:, o + 10:o + 11], stat[:, o + 11:o + 12], stat[:, o + 10:o + 11])
                    v.tensor_scalar(stat[:, o + 11:o + 12], stat[:, o + 2:o + 3],
                                    stat[:, o + 10:o + 11], -1.0, ALU.mult, ALU.mult).then_inc(v_st, 1)

                    for r in range(4):
                        g4 = 4 * ti + r
                        pst = [ps_tr0, ps_tr1][g4 % 2]
                        v.wait_ge(t_tr, g4 + 1)
                        v.tensor_copy(xnT[:, r * 512:(r + 1) * 512], pst[:]).then_inc(v_xnT, 1)

                    v.wait_ge(t_y, ti + 1)
                    b = ti // NTB
                    pt = ti % NTB
                    cos4 = trigc[:, pt * 32:(pt + 1) * 32].unsqueeze(1).to_broadcast([128, 4, 32])
                    sin4 = trigs[:, pt * 32:(pt + 1) * 32].unsqueeze(1).to_broadcast([128, 4, 32])
                    ya = ps_y[:, 0:256].rearrange("p (h x) -> p h x", h=4)
                    rqv = ropeq[ti % 2][:].rearrange("p (h x) -> p h x", h=4)
                    r1 = rsc1[:].rearrange("p (h x) -> p h x", h=4)[:, :, 0:32]
                    r2 = rsc2[:].rearrange("p (h x) -> p h x", h=4)[:, :, 0:32]
                    v.tensor_tensor(r1, ya[:, :, 0:32], cos4, ALU.mult)
                    v.tensor_tensor(r2, ya[:, :, 32:64], sin4, ALU.mult)
                    v.tensor_sub(rqv[:, :, 0:32], r1, r2)
                    v.tensor_tensor(r1, ya[:, :, 32:64], cos4, ALU.mult)
                    v.tensor_tensor(r2, ya[:, :, 0:32], sin4, ALU.mult)
                    v.tensor_add(rqv[:, :, 32:64], r1, r2)
                    cos1 = trigc[:, pt * 32:(pt + 1) * 32]
                    sin1 = trigs[:, pt * 32:(pt + 1) * 32]
                    rk = ropek[ti % 2]
                    v.tensor_tensor(rsc1[:, 0:32], ps_y[:, 256:288], cos1, ALU.mult)
                    v.tensor_tensor(rsc2[:, 0:32], ps_y[:, 288:320], sin1, ALU.mult)
                    v.tensor_sub(rk[:, 0:32], rsc1[:, 0:32], rsc2[:, 0:32])
                    v.tensor_tensor(rsc1[:, 0:32], ps_y[:, 288:320], cos1, ALU.mult)
                    v.tensor_tensor(rsc2[:, 0:32], ps_y[:, 256:288], sin1, ALU.mult)
                    v.tensor_add(rk[:, 32:64], rsc1[:, 0:32], rsc2[:, 0:32])
                    v.tensor_copy(v_sb[b][:, pt * 65:pt * 65 + 64], ps_y[:, 320:384]).then_inc(v_rope, 1)

                    v.wait_ge(t_qk, ti + 1)
                    for h in range(4):
                        v.tensor_copy(q_aug[b * 4 + h][0:64, pt * 128:(pt + 1) * 128],
                                      ps_qt[0:64, h * 128:(h + 1) * 128])
                    v.tensor_copy(k_aug[b][0:64, pt * 128:(pt + 1) * 128], ps_kt[:, :]).then_inc(v_qkev, 1)

            @blk.scalar
            def _(s):
                s = _Drain(s)
                s.wait_ge(v_cst, 1)
                ch = 0
                for ti in range(NT_ALL):
                    xb = xt[ti % 2]
                    o = 12 * (ti % 2)
                    s.wait_ge(x_ldp[ti % 2], 16 * (ti // 2 + 1))
                    s.activation(sqs[:], xb[:], AF.Square, accum_out=stat[:, o + 1:o + 2]).then_inc(s_sq, 1)
                    s.wait_ge(v_st, 2 * ti + 1)
                    s.activation(stat[:, o + 9:o + 10], stat[:, o + 7:o + 8], AF.Ln).then_inc(s_sq, 1)
                    s.activation(stat[:, o + 10:o + 11], stat[:, o + 9:o + 10], AF.Exp, scale=-0.5).then_inc(s_sq, 1)
                    s.wait_ge(v_st, 2 * ti + 2)
                    if ti >= 2:
                        s.wait_ge(t_tr, 4 * (ti - 1))
                    s.activation(xnt[ti % 2][:], xb[:], AF.Identity,
                                 bias=stat[:, o + 11:o + 12], scale=stat[:, o + 10:o + 11]).then_inc(s_sq, 1)
                    for _2 in range(2):
                        if ch < 128:
                            s.wait_ge(gu_ldp[ch % 2], 16 * (ch // 2 + 1))
                            s.activation(pabs[:], (pw0 if ch % 2 == 0 else pw1)[:], AF.Abs,
                                         accum_out=wsum[:, ch:ch + 1]).then_inc(s_pw, 1)
                            ch += 1

            @blk.tensor
            def _(t):
                t.wait_ge(v_cst, 1)
                t.wait_ge(win_cv, 16)
                for ti in range(NT_ALL):
                    t.wait_ge(s_sq, 4 * ti + 4)
                    for r in range(4):
                        g4 = 4 * ti + r
                        pst = [ps_tr0, ps_tr1][g4 % 2]
                        if g4 >= 2:
                            t.wait_ge(v_xnT, g4 - 1)
                        ins = None
                        for j in range(4):
                            mi = r * 4 + j
                            ins = t.matmul(pst[:, j * 128:(j + 1) * 128],
                                           xnt[ti % 2][:, mi * 128:(mi + 1) * 128], identb[:],
                                           start=True, stop=True)
                        ins.then_inc(t_tr, 1)
                    t.wait_ge(v_xnT, 4 * ti + 4)
                    if ti >= 1:
                        t.wait_ge(v_rope, ti)
                    ins = None
                    for mi in range(16):
                        ins = t.matmul(ps_y[:], xnT[:, mi * 128:(mi + 1) * 128],
                                       winb[:, mi * CC:(mi + 1) * CC], start=(mi == 0), stop=(mi == 15))
                    ins.then_inc(t_y, 1)
                    t.wait_ge(v_rope, ti + 1)
                    if ti >= 1:
                        t.wait_ge(v_qkev, ti)
                    for h in range(4):
                        t.matmul(ps_qt[0:64, h * 128:(h + 1) * 128],
                                 ropeq[ti % 2][:, h * 64:(h + 1) * 64], identb[:],
                                 start=True, stop=True)
                    t.matmul(ps_kt[:, :], ropek[ti % 2][:], identb[:],
                             start=True, stop=True).then_inc(t_qk, 1)

        _es1.close()
        _es2 = contextlib.ExitStack()
        kpre = _es2.enter_context(nc.sbuf_tensor("kpre", [64, S], F32))
        dotbf = _es2.enter_context(nc.sbuf_tensor("dotbf", [64, S], BF16))
        wred = _es2.enter_context(nc.sbuf_tensor("wred", [128, 2], F32))
        ps_d0 = _es2.enter_context(nc.psum_tensor("ps_d0", [1, 512], F32))
        ps_d1 = _es2.enter_context(nc.psum_tensor("ps_d1", [1, 512], F32))
        ps_w1 = _es2.enter_context(nc.psum_tensor("ps_w1", [1, 2], F32))
        ps_bc = _es2.enter_context(nc.psum_tensor("ps_bc", [128, 2], F32))
        blk2 = _es2.enter_context(nc.Block())
        if True:
            v_scan, t_dot, v_aug = S_("v_scan"), S_("t_dot"), S_("v_aug")
            v_wred, t_w1, v_w1 = S_("v_wred"), S_("t_w1"), S_("v_w1")

            @blk2.vector
            def _(v):
                v = _Drain(v)
                v.wait_ge(S_("s_pw"), 128)
                v.wait_ge(S_("t_qk"), NT_ALL)
                v.reduce_sum(wred[:, 0:1], wsum[:, 0:64], AX.X)
                v.reduce_sum(wred[:, 1:2], wsum[:, 64:128], AX.X).then_inc(v_wred, 2)
                v.wait_ge(t_w1, 2)
                v.tensor_scalar_mul(w1s[:, 0:2], ps_w1[0:1, 0:2], 1.0 / (D * DI))
                v.tensor_scalar_max(w1s[:, 0:2], w1s[:, 0:2], 1e-5).then_inc(v_w1, 1)
                v.wait_ge(t_w1, 4)
                v.tensor_copy(wsc[:, 0:2], ps_bc[:, 0:2])
                v.reciprocal(swc[:, 0:2], wsc[:, 0:2]).then_inc(v_w1, 1)

                v.wait_ge(S_("v_qkev"), NT_ALL)
                for b in range(B):
                    v.tensor_tensor_scan(kpre[:], onescol[0:64, 0:1].to_broadcast([64, S]),
                                         k_aug[b][0:64, :], 0.0, ALU.mult, ALU.add).then_inc(v_scan, 1)
                    for h in range(4):
                        i = b * 4 + h
                        if i > 0:
                            v.wait_ge(t_dot, i * 8)
                        v.tensor_mul(dotbf[:], q_aug[i][0:64, :], kpre[:]).then_inc(v_scan, 1)
                        for c in range(8):
                            v.wait_ge(t_dot, i * 8 + c + 1)
                            psd = [ps_d0, ps_d1][c % 2]
                            v.scalar_tensor_tensor(q_aug[i][64:65, c * 512:(c + 1) * 512],
                                                   psd[0:1, :], -1.0,
                                                   invcnt_sb[0:1, c * 512:(c + 1) * 512],
                                                   ALU.mult, ALU.mult).then_inc(v_aug, 1)

            @blk2.tensor
            def _(t):
                t.wait_ge(v_wred, 2)
                t.matmul(ps_w1[0:1, 0:1], wred[:, 0:1], onescol[:, 0:1], start=True, stop=True)
                t.matmul(ps_w1[0:1, 1:2], wred[:, 1:2], onescol[:, 0:1], start=True, stop=True).then_inc(t_w1, 2)
                t.wait_ge(v_w1, 1)
                t.matmul(ps_bc[:, 0:1], onesrow[0:1, 0:128], w1s[0:1, 0:1], start=True, stop=True)
                t.matmul(ps_bc[:, 1:2], onesrow[0:1, 0:128], w1s[0:1, 1:2], start=True, stop=True).then_inc(t_w1, 2)
                for b in range(B):
                    for h in range(4):
                        i = b * 4 + h
                        t.wait_ge(S_("v_scan"), b + i + 2)
                        for c in range(8):
                            psd = [ps_d0, ps_d1][c % 2]
                            if c >= 2:
                                t.wait_ge(S_("v_aug"), i * 8 + c - 1)
                            t.matmul(psd[0:1, :], onescolb[0:64, 0:1], dotbf[:, c * 512:(c + 1) * 512],
                                     start=True, stop=True).then_inc(t_dot, 1)

        _es2.close()
        _es3 = contextlib.ExitStack()
        oT0 = _es3.enter_context(nc.sbuf_tensor("oT0", [128, T_ALL], BF16))
        oT1 = _es3.enter_context(nc.sbuf_tensor("oT1", [128, T_ALL], BF16))
        if True:
            oT = [oT0, oT1]
            _es4 = contextlib.ExitStack()
            pt0 = _es4.enter_context(nc.sbuf_tensor("pt0", [128, 1024], BF16))
            pt1 = _es4.enter_context(nc.sbuf_tensor("pt1", [128, 1024], BF16))
            rden = _es4.enter_context(nc.sbuf_tensor("rden", [1, 512], BF16))
            rdenf = _es4.enter_context(nc.sbuf_tensor("rdenf", [1, 512], F32))
            sce = _es4.enter_context(nc.sbuf_tensor("sce", [64, 512], BF16))
            tmpo0 = _es4.enter_context(nc.sbuf_tensor("tmpo0", [64, 512], BF16))
            tmpo1 = _es4.enter_context(nc.sbuf_tensor("tmpo1", [64, 512], BF16))
            pwd0 = _es4.enter_context(nc.sbuf_tensor("pwd0", [128, 2048], F32))
            pwd1 = _es4.enter_context(nc.sbuf_tensor("pwd1", [128, 2048], F32))
            ps_s0 = _es4.enter_context(nc.psum_tensor("ps_s0", [128, 1024], F32))
            ps_s1 = _es4.enter_context(nc.psum_tensor("ps_s1", [128, 1024], F32))
            ps_o0 = _es4.enter_context(nc.psum_tensor("ps_o0", [65, 512], F32))
            ps_o1 = _es4.enter_context(nc.psum_tensor("ps_o1", [65, 512], F32))
            ps_e = _es4.enter_context(nc.psum_tensor("ps_e", [64, 512], F32))
            ps_wb = _es4.enter_context(nc.psum_tensor("ps_wb", [128, 2], F32))
            blk3 = _es4.enter_context(nc.Block())
            if True:
                ptile = [pt0, pt1]
                tmpo = [tmpo0, tmpo1]
                ps_s = [ps_s0, ps_s1]
                ps_o = [ps_o0, ps_o1]
                t_qk3, v_msk, s_exp = S_("t_qk3"), S_("v_msk"), S_("s_exp")
                t_av, v_den1, t_ex = S_("t_av"), S_("v_den1"), S_("t_ex")
                v_den2 = S_("v_den2")
                g_crossp = [S_("g_cross0"), S_("g_cross1")]
                wd_ldp = [S_("wd_ld0"), S_("wd_ld1")]
                v_pwd = S_("v_pwd")
                v_wredd, t_w1d, v_w1d = S_("v_wredd"), S_("t_w1d"), S_("v_w1d")

                @blk3.gpsimd
                def _(g):
                    g.wait_ge(S_("v_aug"), 64)
                    g.wait_ge(S_("t_dot"), 64)
                    for u, (b, h, c) in enumerate(units):
                        ch = u
                        if ch < 64:
                            if ch >= 2:
                                g.wait_ge(v_pwd, ch - 1)
                            g.dma_start(out=(pwd0 if ch % 2 == 0 else pwd1)[:],
                                        in_=wdT[ch * 128:(ch + 1) * 128, :]).then_inc(wd_ldp[ch % 2], 16)
                        g.wait_ge(v_den2, u + 1)
                        g.dma_start(out=oT[h // 2][(h % 2) * 64:(h % 2) * 64 + 64,
                                                   b * S + c * 512:b * S + (c + 1) * 512],
                                    in_=tmpo[u % 2][:, :]).then_inc(g_crossp[u % 2], 16)

                @blk3.tensor
                def _(t):
                    t.wait_ge(S_("v_aug"), 64)
                    for u, (b, h, c) in enumerate(units):
                        i = b * 4 + h
                        ng = 2 * c + 2

                        def qk(g2):
                            gg = cum_grp[u] + g2
                            if gg >= 2:
                                t.wait_ge(s_exp, gg - 1)
                            ins = None
                            for j in range(2):
                                kvt = 2 * g2 + j
                                ins = t.matmul(ps_s[gg % 2][:, j * 512:(j + 1) * 512],
                                               k_aug[b][:, kvt * 128:(kvt + 1) * 128],
                                               q_aug[i][:, c * 512:(c + 1) * 512],
                                               start=True, stop=True)
                            ins.then_inc(t_qk3, 1)

                        def av(g2):
                            gg = cum_grp[u] + g2
                            t.wait_ge(s_exp, gg + 1)
                            if g2 == 0 and u >= 2:
                                t.wait_ge(v_den2, u - 1)
                            ins = None
                            for j in range(2):
                                kvt = 2 * g2 + j
                                ins = t.matmul(ps_o[u % 2][:, :],
                                               v_sb[b][:, kvt * 65:(kvt + 1) * 65],
                                               ptile[gg % 2][:, j * 512:(j + 1) * 512],
                                               start=(kvt == 0), stop=(kvt == 2 * ng - 1))
                            ins.then_inc(t_av, 1)

                        for g2 in range(ng):
                            qk(g2)
                            if g2 >= 1:
                                av(g2 - 1)
                        av(ng - 1)
                        t.wait_ge(v_den1, u + 1)
                        t.matmul(ps_e[:, :], ones_r[0:1, 0:64], rden[0:1, :],
                                 start=True, stop=True).then_inc(t_ex, 1)
                    t.wait_ge(v_wredd, 1)
                    t.matmul(ps_wb[0:1, 0:1], wsc[:, 3:4], onescol[:, 0:1], start=True, stop=True).then_inc(t_w1d, 1)
                    t.wait_ge(v_w1d, 1)
                    t.matmul(ps_wb[:, 1:2], onesrow[0:1, 0:128], w1s[0:1, 2:3], start=True, stop=True).then_inc(t_w1d, 1)

                @blk3.vector
                def _(v):
                    v = _Drain(v)
                    wd_ch = 0
                    for u, (b, h, c) in enumerate(units):
                        ng = 2 * c + 2
                        for dg in range(2):
                            g2 = ng - 2 + dg
                            gg = cum_grp[u] + g2
                            v.wait_ge(t_qk3, gg + 1)
                            for j in range(2):
                                kvt = 2 * g2 + j
                                r = kvt - 4 * c
                                v.tensor_add(ps_s[gg % 2][:, j * 512:(j + 1) * 512],
                                             ps_s[gg % 2][:, j * 512:(j + 1) * 512],
                                             masks_sb[:, r * 512:(r + 1) * 512])
                            v.engine_nop().then_inc(v_msk, 1)
                        v.wait_ge(t_av, cum_grp[u + 1])
                        v.reciprocal(rdenf[0:1, :], ps_o[u % 2][64:65, :])
                        v.tensor_copy(rden[0:1, :], rdenf[0:1, :]).then_inc(v_den1, 1)
                        v.wait_ge(t_ex, u + 1)
                        v.tensor_copy(sce[:, :], ps_e[:, :])
                        if u >= 2:
                            v.wait_ge(g_crossp[u % 2], 16 * (u // 2))
                        v.tensor_mul(tmpo[u % 2][:, :], ps_o[u % 2][0:64, :], sce[:, :]).then_inc(v_den2, 1)
                        if wd_ch < 64:
                            v.wait_ge(wd_ldp[wd_ch % 2], 16 * (wd_ch // 2 + 1))
                            v.reduce_sum(wsum[:, 128 + wd_ch:129 + wd_ch],
                                         (pwd0 if wd_ch % 2 == 0 else pwd1)[:], AX.X,
                                         apply_absolute_value=True).then_inc(v_pwd, 1)
                            wd_ch += 1
                    v.reduce_sum(wsc[:, 3:4], wsum[:, 128:192], AX.X).then_inc(v_wredd, 1)
                    v.wait_ge(t_w1d, 1)
                    v.tensor_scalar_mul(w1s[:, 2:3], ps_wb[0:1, 0:1], 1.0 / (D * DI))
                    v.tensor_scalar_max(w1s[:, 2:3], w1s[:, 2:3], 1e-5).then_inc(v_w1d, 1)
                    v.wait_ge(t_w1d, 2)
                    v.tensor_copy(wsc[:, 2:3], ps_wb[:, 1:2])
                    v.reciprocal(swc[:, 2:3], wsc[:, 2:3]).then_inc(v_w1d, 1)

                @blk3.scalar
                def _(s):
                    s = _Drain(s)
                    for u, (b, h, c) in enumerate(units):
                        ng = 2 * c + 2
                        for g2 in range(ng):
                            gg = cum_grp[u] + g2
                            if g2 >= ng - 2:
                                s.wait_ge(v_msk, 2 * u + (g2 - (ng - 2)) + 1)
                            else:
                                s.wait_ge(t_qk3, gg + 1)
                            if gg >= 2:
                                s.wait_ge(t_av, gg - 1)
                            s.activation(ptile[gg % 2][:, :], ps_s[gg % 2][:, :], AF.Exp,
                                         bias=expb[:, 0:1], scale=0.125).then_inc(s_exp, 1)

            _es4.close()
            _es5 = contextlib.ExitStack()
            wo32 = _es5.enter_context(nc.sbuf_tensor("wo32", [128, 512], F32))
            woutb = _es5.enter_context(nc.sbuf_tensor("woutb", [128, 2 * D], BF16))
            rst = _es5.enter_context(nc.sbuf_tensor("rst", [128, 4 * 512], BF16))
            ps_w0 = _es5.enter_context(nc.psum_tensor("ps_w0", [128, 512], F32))
            ps_w1b = _es5.enter_context(nc.psum_tensor("ps_w1b", [128, 512], F32))
            blk4 = _es5.enter_context(nc.Block())
            if True:
                wo_ld, wo_cv, t_wo = S_("wo_ld"), S_("wo_cv"), S_("t_wo")
                v_rs, g_coll = S_("v_rs"), S_("g_coll")
                rs_stp = [S_("rs_st0"), S_("rs_st1"), S_("rs_st2"), S_("rs_st3")]
                ps_w = [ps_w0, ps_w1b]

                @blk4.gpsimd
                def _(g):
                    g.wait_ge(S_("g_cross0"), 16 * (NU // 2))
                    g.wait_ge(S_("g_cross1"), 16 * (NU // 2))
                    for j in range(8):
                        ct, mc = j // 4, j % 4
                        if j >= 1:
                            g.wait_ge(wo_cv, j)
                        g.dma_start(out=wo32[:],
                                    in_=w_outT[ct * 128:(ct + 1) * 128, mc * 512:(mc + 1) * 512]).then_inc(wo_ld, 16)
                    for q4 in range(4):
                        g.wait_ge(rs_stp[q4], 16 * 64)
                    g.collective_compute(
                        "ReduceScatter", ALU.add,
                        replica_groups=[list(range(8))],
                        ins=[rs_in[:, :]],
                        outs=[rs_out[:, :]],
                    ).then_inc(g_coll, 1)

                @blk4.vector
                def _(v):
                    v = _Drain(v)
                    v.wait_ge(S_("g_cross0"), 16 * (NU // 2))
                    v.wait_ge(S_("g_cross1"), 16 * (NU // 2))
                    for j in range(8):
                        v.wait_ge(wo_ld, 16 * (j + 1))
                        ct, mc = j // 4, j % 4
                        v.tensor_copy(woutb[:, ct * D + mc * 512:ct * D + (mc + 1) * 512],
                                      wo32[:]).then_inc(wo_cv, 1)
                    for idx in range(256):
                        v.wait_ge(t_wo, idx + 1)
                        if idx >= 4:
                            v.wait_ge(rs_stp[idx % 4], 16 * (idx // 4))
                        v.tensor_copy(rst[:, (idx % 4) * 512:(idx % 4 + 1) * 512],
                                      ps_w[idx % 2][:, :]).then_inc(v_rs, 1)

                @blk4.tensor
                def _(t):
                    t.wait_ge(wo_cv, 8)
                    t.wait_ge(S_("g_cross0"), 16 * (NU // 2))
                    t.wait_ge(S_("g_cross1"), 16 * (NU // 2))
                    for idx in range(256):
                        tt, mc = idx // 4, idx % 4
                        if idx >= 2:
                            t.wait_ge(v_rs, idx - 1)
                        ins = None
                        for ct in range(2):
                            ins = t.matmul(ps_w[idx % 2][:, :], oT[ct][:, tt * 128:(tt + 1) * 128],
                                           woutb[:, ct * D + mc * 512:ct * D + (mc + 1) * 512],
                                           start=(ct == 0), stop=(ct == 1))
                        ins.then_inc(t_wo, 1)

                @blk4.sync
                def _(sync):
                    for idx in range(256):
                        tt, mc = idx // 4, idx % 4
                        sync.wait_ge(v_rs, idx + 1)
                        sync.dma_start(out=rs_in[tt * 128:(tt + 1) * 128, mc * 512:(mc + 1) * 512],
                                       in_=rst[:, (idx % 4) * 512:(idx % 4 + 1) * 512]).then_inc(rs_stp[idx % 4], 16)

                if debug:
                    @blk4.scalar
                    def _(s):
                        s = _Drain(s)
                        s.wait_ge(S_("g_cross0"), 16 * (NU // 2))
                        s.wait_ge(S_("g_cross1"), 16 * (NU // 2))
                        for i in range(8):
                            s.dma_start(out=tap["qaug"][i * 65:(i + 1) * 65, :], in_=q_aug[i][:, :]).then_inc(S_("tapsem"), 16)
                        for b in range(B):
                            s.dma_start(out=tap["kaug"][b * 65:(b + 1) * 65, :], in_=k_aug[b][:, :]).then_inc(S_("tapsem"), 16)
                            s.dma_start(out=tap["v"][b * 128:(b + 1) * 128, :], in_=v_sb[b][:, :]).then_inc(S_("tapsem"), 16)
                        for p in range(2):
                            s.dma_start(out=tap["oT"][p * 128:(p + 1) * 128, :], in_=oT[p][:, :]).then_inc(S_("tapsem"), 16)
                        s.wait_ge(S_("tapsem"), 16 * 14)

    _es5.close()
    _es3.close()
    _es0.close()
    with nc.sbuf_tensor("hslab", [128, 8 * DI], BF16) as hslab:
        _es6 = contextlib.ExitStack()
        kxT = _es6.enter_context(nc.sbuf_tensor("kxT", [128, 16 * NT_OWN * 128], BF16))
        mxc8 = _es6.enter_context(nc.sbuf_tensor("mxc8", [128, 8], F32))
        gsc8 = _es6.enter_context(nc.sbuf_tensor("gsc8", [128, 8], F32))
        usc8 = _es6.enter_context(nc.sbuf_tensor("usc8", [128, 8], F32))
        if True:
            _es7 = contextlib.ExitStack()
            xo0 = _es7.enter_context(nc.sbuf_tensor("xo0", [128, D], F32))
            xo1 = _es7.enter_context(nc.sbuf_tensor("xo1", [128, D], F32))
            xno0 = _es7.enter_context(nc.sbuf_tensor("xno0", [128, D], BF16))
            xno1 = _es7.enter_context(nc.sbuf_tensor("xno1", [128, D], BF16))
            sq5 = _es7.enter_context(nc.sbuf_tensor("sq5", [128, D], BF16))
            st5 = _es7.enter_context(nc.sbuf_tensor("st5", [128, 24], F32))
            kxf = _es7.enter_context(nc.sbuf_tensor("kxf", [128, D], F32))
            kxb = _es7.enter_context(nc.sbuf_tensor("kxb", [128, D], BF16))
            ps_t50 = _es7.enter_context(nc.psum_tensor("ps_t50", [128, 512], F32))
            ps_t51 = _es7.enter_context(nc.psum_tensor("ps_t51", [128, 512], F32))
            blk5a = _es7.enter_context(nc.Block())
            if True:
                xo = [xo0, xo1]
                xno = [xno0, xno1]
                ps_t5 = [ps_t50, ps_t51]
                xo_ldp = [S_("xo_ld0"), S_("xo_ld1")]
                s_sq5, v_st5, v_red5 = S_("s_sq5"), S_("v_st5"), S_("v_red5")
                v_kx, t_kxT, v_kxT = S_("v_kx"), S_("t_kxT"), S_("v_kxT")
                xp_stp = [S_("xp_st0"), S_("xp_st1")]

                @blk5a.sync
                def _(sync):
                    for q4 in range(4):
                        sync.wait_ge(S_(f"rs_st{q4}"), 16 * 64)
                    def park(j):
                        sync.wait_ge(v_kx, 3 * j + 2)
                        sync.dma_start(out=xn_park[j * 128:(j + 1) * 128, :], in_=xno[j % 2][:]).then_inc(xp_stp[j % 2], 16)
                    for ti in range(NT_OWN):
                        if ti >= 2:
                            sync.wait_ge(s_sq5, 4 * (ti - 2) + 4)
                            sync.wait_ge(v_red5, ti - 1)
                        sync.dma_start(out=xo[ti % 2][:], in_=x_own[ti * 128:(ti + 1) * 128, :]).then_inc(xo_ldp[ti % 2], 16)
                        if ti >= 2:
                            park(ti - 2)
                    park(NT_OWN - 2)
                    park(NT_OWN - 1)

                @blk5a.scalar
                def _(s):
                    s = _Drain(s)
                    for q4 in range(4):
                        s.wait_ge(S_(f"rs_st{q4}"), 16 * 64)
                    for ti in range(NT_OWN):
                        xb = xo[ti % 2]
                        o = 12 * (ti % 2)
                        s.wait_ge(xo_ldp[ti % 2], 16 * (ti // 2 + 1))
                        s.activation(sq5[:], xb[:], AF.Square, accum_out=st5[:, o + 1:o + 2]).then_inc(s_sq5, 1)
                        s.wait_ge(v_st5, 2 * ti + 1)
                        s.activation(st5[:, o + 9:o + 10], st5[:, o + 6:o + 7], AF.Ln).then_inc(s_sq5, 1)
                        s.activation(st5[:, o + 10:o + 11], st5[:, o + 9:o + 10], AF.Exp, scale=-0.5).then_inc(s_sq5, 1)
                        s.wait_ge(v_st5, 2 * ti + 2)
                        if ti >= 2:
                            s.wait_ge(v_kx, 3 * (ti - 2) + 3)
                            s.wait_ge(xp_stp[ti % 2], 16 * (ti // 2))
                        s.activation(xno[ti % 2][:], xb[:], AF.Identity,
                                     bias=st5[:, o + 11:o + 12], scale=st5[:, o + 10:o + 11]).then_inc(s_sq5, 1)

                @blk5a.vector
                def _(v):
                    v = _Drain(v)
                    for q4 in range(4):
                        v.wait_ge(S_(f"rs_st{q4}"), 16 * 64)
                    for ti in range(NT_OWN):
                        xb = xo[ti % 2]
                        o = 12 * (ti % 2)
                        v.wait_ge(xo_ldp[ti % 2], 16 * (ti // 2 + 1))
                        v.reduce_sum(st5[:, o + 0:o + 1], xb[:], AX.X).then_inc(v_red5, 1)
                        v.wait_ge(s_sq5, 4 * ti + 1)
                        v.tensor_scalar_mul(st5[:, o + 2:o + 3], st5[:, o + 0:o + 1], 1.0 / D)
                        v.tensor_scalar_mul(st5[:, o + 3:o + 4], st5[:, o + 1:o + 2], 1.0 / D)
                        v.tensor_scalar(st5[:, o + 4:o + 5], st5[:, o + 2:o + 3], st5[:, o + 2:o + 3], None, ALU.mult)
                        v.tensor_sub(st5[:, o + 5:o + 6], st5[:, o + 3:o + 4], st5[:, o + 4:o + 5])
                        v.tensor_scalar_add(st5[:, o + 6:o + 7], st5[:, o + 5:o + 6], 1e-5).then_inc(v_st5, 1)
                        v.wait_ge(s_sq5, 4 * ti + 3)
                        v.tensor_scalar(st5[:, o + 11:o + 12], st5[:, o + 10:o + 11], st5[:, o + 10:o + 11], None, ALU.mult)
                        v.scalar_tensor_tensor(st5[:, o + 11:o + 12], st5[:, o + 11:o + 12], -0.5,
                                               st5[:, o + 6:o + 7], ALU.mult, ALU.mult)
                        v.tensor_scalar_add(st5[:, o + 11:o + 12], st5[:, o + 11:o + 12], 1.5)
                        v.tensor_mul(st5[:, o + 10:o + 11], st5[:, o + 11:o + 12], st5[:, o + 10:o + 11])
                        v.tensor_scalar(st5[:, o + 11:o + 12], st5[:, o + 2:o + 3],
                                        st5[:, o + 10:o + 11], -1.0, ALU.mult, ALU.mult).then_inc(v_st5, 1)
                        v.wait_ge(s_sq5, 4 * ti + 4)
                        v.reduce_max(st5[:, o + 4:o + 5], xno[ti % 2][:], AX.X, apply_absolute_value=True)
                        v.tensor_scalar_max(st5[:, o + 4:o + 5], st5[:, o + 4:o + 5], 1e-5)
                        v.reciprocal(st5[:, o + 5:o + 6], st5[:, o + 4:o + 5])
                        v.tensor_scalar_mul(st5[:, o + 5:o + 6], st5[:, o + 5:o + 6], 127.0)
                        v.tensor_scalar_mul(mxc8[:, ti:ti + 1], st5[:, o + 4:o + 5], 1.0 / 127.0).then_inc(v_kx, 1)
                        if ti >= 1:
                            v.wait_ge(t_kxT, 4 * ti)
                        v.tensor_scalar(kxf[:], xno[ti % 2][:], st5[:, o + 5:o + 6], 12582912.0,
                                        ALU.mult, ALU.add).then_inc(v_kx, 1)
                        v.tensor_scalar_add(kxb[:], kxf[:], -12582912.0).then_inc(v_kx, 1)
                        for r in range(4):
                            g4 = 4 * ti + r
                            v.wait_ge(t_kxT, g4 + 1)
                            pst = ps_t5[g4 % 2]
                            for j in range(4):
                                hi = r * 4 + j
                                v.tensor_copy(kxT[:, (hi * NT_OWN + ti) * 128:(hi * NT_OWN + ti + 1) * 128],
                                              pst[:, j * 128:(j + 1) * 128])
                            v.engine_nop().then_inc(v_kxT, 1)
                    v.wait_ge(S_("v_w1"), 2)
                    v.tensor_scalar(gsc8[:], mxc8[:], wsc[:, 0:1], None, ALU.mult)
                    v.tensor_scalar(usc8[:], mxc8[:], wsc[:, 1:2], None, ALU.mult).then_inc(v_kx, 1)

                @blk5a.tensor
                def _(t):
                    for q4 in range(4):
                        t.wait_ge(S_(f"rs_st{q4}"), 16 * 64)
                    for ti in range(NT_OWN):
                        t.wait_ge(v_kx, 3 * ti + 3)
                        for r in range(4):
                            g4 = 4 * ti + r
                            pst = ps_t5[g4 % 2]
                            if g4 >= 2:
                                t.wait_ge(v_kxT, g4 - 1)
                            ins = None
                            for j in range(4):
                                hi = r * 4 + j
                                ins = t.matmul(pst[:, j * 128:(j + 1) * 128],
                                               kxb[:, hi * 128:(hi + 1) * 128], identb[:],
                                               start=True, stop=True)
                            ins.then_inc(t_kxT, 1)

            _es7.close()
            _es8 = contextlib.ExitStack()
            wst0 = _es8.enter_context(nc.sbuf_tensor("wst0", [128, 512], F32))
            wst1 = _es8.enter_context(nc.sbuf_tensor("wst1", [128, 512], F32))
            wst2 = _es8.enter_context(nc.sbuf_tensor("wst2", [128, 512], F32))
            wst3 = _es8.enter_context(nc.sbuf_tensor("wst3", [128, 512], F32))
            wq32x0 = _es8.enter_context(nc.sbuf_tensor("wq32x0", [128, 512], F32))
            wq32x1 = _es8.enter_context(nc.sbuf_tensor("wq32x1", [128, 512], F32))
            wq32x2 = _es8.enter_context(nc.sbuf_tensor("wq32x2", [128, 512], F32))
            wq32x3 = _es8.enter_context(nc.sbuf_tensor("wq32x3", [128, 512], F32))
            wqi8a0 = _es8.enter_context(nc.sbuf_tensor("wqi8a", [128, 512], I8))
            wqi8a1 = _es8.enter_context(nc.sbuf_tensor("wqi8b", [128, 512], I8))
            wqb0 = _es8.enter_context(nc.sbuf_tensor("wqb0", [128, 512], BF16))
            wqb1 = _es8.enter_context(nc.sbuf_tensor("wqb1", [128, 512], BF16))
            wqb2 = _es8.enter_context(nc.sbuf_tensor("wqb2", [128, 512], BF16))
            wqb3 = _es8.enter_context(nc.sbuf_tensor("wqb3", [128, 512], BF16))
            hg = _es8.enter_context(nc.sbuf_tensor("hg", [128, 4 * 512], BF16))
            hu = _es8.enter_context(nc.sbuf_tensor("hu", [128, 4 * 512], BF16))
            ps_g = _es8.enter_context(nc.psum_tensor("ps_g", [128, 4 * 512], F32))
            ps_u = _es8.enter_context(nc.psum_tensor("ps_u", [128, 4 * 512], F32))
            blk5b = _es8.enter_context(nc.Block())
            if True:
                wst = [wst0, wst1, wst2, wst3]
                wq32x = [wq32x0, wq32x1, wq32x2, wq32x3]
                wqi8 = [wqi8a0, wqi8a1]
                wqb = [wqb0, wqb1, wqb2, wqb3]
                def g2glob(kg):
                    return (kg // 16) * 32 + (kg % 16)
                def u2glob(ku):
                    return (ku // 16) * 32 + 16 + (ku % 16)
                wgu_ldp = [S_("wgu_ld0"), S_("wgu_ld1")]
                wgu_up = [S_("wgu_up0"), S_("wgu_up1")]
                v_wq, s_wq = S_("v_wq"), S_("s_wq")
                t_gu, s_hg, s_hu, v_h = S_("t_gu"), S_("s_hg"), S_("s_hu"), S_("v_h")

                @blk5b.gpsimd
                def _(g):
                    g.wait_ge(S_("xp_st0"), 16 * (NT_OWN // 2))
                    g.wait_ge(S_("xp_st1"), 16 * (NT_OWN // 2))
                    g.wait_ge(S_("t_kxT"), 4 * NT_OWN)
                    g.wait_ge(S_("v_kxT"), 4 * NT_OWN)
                    if debug:
                        g.dma_start(out=tap["kx"][:, :], in_=kxT[:, :]).then_inc(S_("tapsem4"), 16)
                        S_("tapsem6").v if False else None
                    kg = 0
                    for ic in range(16):
                        for th in range(2):
                            for hi in range(16):
                                if kg >= 2:
                                    g.wait_ge(s_wq, g2glob(kg - 2) + 1)
                                g.dma_start(out=wst[2 * (kg % 2)][:],
                                            in_=wgT[hi * 128:(hi + 1) * 128,
                                                    ic * 512:(ic + 1) * 512]).then_inc(wgu_ldp[kg % 2], 16)
                                kg += 1

                @blk5b.sync
                def _(sync):
                    for q4 in range(4):
                        sync.wait_ge(S_(f"rs_st{q4}"), 16 * 64)
                    sync.wait_ge(S_("t_kxT"), 4 * NT_OWN)
                    ku = 0
                    for ic in range(16):
                        for th in range(2):
                            for hi in range(16):
                                if ku >= 2:
                                    sync.wait_ge(s_wq, u2glob(ku - 2) + 1)
                                sync.dma_start(out=wst[2 * (ku % 2) + 1][:],
                                               in_=wuT[hi * 128:(hi + 1) * 128,
                                                       ic * 512:(ic + 1) * 512]).then_inc(wgu_up[ku % 2], 16)
                                ku += 1

                @blk5b.vector
                def _(v):
                    v = _Drain(v)
                    v.wait_ge(S_("xp_st0"), 16 * (NT_OWN // 2))
                    v.wait_ge(S_("xp_st1"), 16 * (NT_OWN // 2))
                    v.wait_ge(S_("v_kx"), 3 * NT_OWN + 1)
                    k = 0
                    for ic in range(16):
                        for th in range(2):
                            for wsel in range(2):
                                for hi in range(16):
                                    v.wait_ge(s_wq, k + 1)
                                    if k >= 4:
                                        v.wait_ge(t_gu, k - 3)
                                    v.tensor_scalar(wqb[k % 4][:], wq32x[k % 4][:], 1.0, -1.0,
                                                    ALU.min, ALU.max).then_inc(v_wq, 2)
                                    k += 1
                            jj = ic * 2 + th
                            v.wait_ge(s_hu, jj + 1)
                            for q in range(4):
                                tt = th * 4 + q
                                v.tensor_mul(hslab[:, tt * DI + ic * 512:tt * DI + (ic + 1) * 512],
                                             hg[:, q * 512:(q + 1) * 512], hu[:, q * 512:(q + 1) * 512])
                            v.engine_nop().then_inc(v_h, 1)

                @blk5b.scalar
                def _(s):
                    s = _Drain(s)
                    s.wait_ge(S_("xp_st0"), 16 * (NT_OWN // 2))
                    s.wait_ge(S_("xp_st1"), 16 * (NT_OWN // 2))
                    k = 0
                    for ic in range(16):
                        for th in range(2):
                            jj = ic * 2 + th
                            for wsel in range(2):
                                for hi in range(16):
                                    kk = (jj * 2 + wsel) * 16 + hi
                                    st_idx = kk // 16 % 2
                                    stg = wst[2 * ((kk % 16 + (kk // 32) * 16) % 2) + wsel] if False else None
                                    kstream = (kk // 32) * 16 + (kk % 16)
                                    stg = wst[2 * (kstream % 2) + wsel]
                                    sem = wgu_ldp if wsel == 0 else wgu_up
                                    s.wait_ge(sem[kstream % 2], 16 * (kstream // 2 + 1))
                                    if k >= 4:
                                        s.wait_ge(v_wq, 2 * (k - 3))
                                    s.activation(wq32x[k % 4][:], stg[:], AF.Identity,
                                                 bias=mcol[:, 0:1], scale=swc[:, wsel:wsel + 1])
                                    s.activation(wq32x[k % 4][:], wq32x[k % 4][:], AF.Identity,
                                                 bias=nmcol[:, 0:1]).then_inc(s_wq, 1)
                                    k += 1
                            s.wait_ge(t_gu, jj * 32 + 16)
                            if jj >= 1:
                                s.wait_ge(v_h, jj)
                            if debug and jj == 1:
                                s.wait_ge(S_("tapsem5"), 32)
                            ins = None
                            for q in range(4):
                                tt = th * 4 + q
                                ins = s.activation(hg[:, q * 512:(q + 1) * 512], ps_g[:, q * 512:(q + 1) * 512],
                                                   AF.Silu, scale=gsc8[:, tt:tt + 1])
                            ins.then_inc(s_hg, 1)
                            s.wait_ge(t_gu, jj * 32 + 32)
                            ins = None
                            for q in range(4):
                                tt = th * 4 + q
                                ins = s.activation(hu[:, q * 512:(q + 1) * 512], ps_u[:, q * 512:(q + 1) * 512],
                                                   AF.Identity, scale=usc8[:, tt:tt + 1])
                            ins.then_inc(s_hu, 1)
                            if debug and jj == 0:
                                s.dma_start(out=tap["hg"][:, :], in_=hg[:, :]).then_inc(S_("tapsem5"), 16)
                                s.dma_start(out=tap["hu"][:, :], in_=hu[:, :]).then_inc(S_("tapsem5"), 16)

                @blk5b.tensor
                def _(t):
                    k = 0
                    for ic in range(16):
                        for th in range(2):
                            jj = ic * 2 + th
                            for wsel in range(2):
                                psx = ps_g if wsel == 0 else ps_u
                                for hi in range(16):
                                    t.wait_ge(v_wq, 2 * (k + 1))
                                    if hi == 0 and jj >= 1:
                                        t.wait_ge(s_hg if wsel == 0 else s_hu, jj)
                                    ins = None
                                    for q in range(4):
                                        tt = th * 4 + q
                                        ins = t.matmul(psx[:, q * 512:(q + 1) * 512],
                                                       kxT[:, (hi * NT_OWN + tt) * 128:(hi * NT_OWN + tt + 1) * 128],
                                                       wqb[k % 4][:],
                                                       start=(hi == 0), stop=(hi == 15))
                                    ins.then_inc(t_gu, 1)
                                    k += 1

        _es8.close()
        _es6.close()
        _es9 = contextlib.ExitStack()
        lnwT = _es9.enter_context(nc.sbuf_tensor("lnwT", [128, DI], BF16))
        lnw32 = _es9.enter_context(nc.sbuf_tensor("lnw32", [1, 512], F32))
        sq8k = _es9.enter_context(nc.sbuf_tensor("sq8k", [128, DI], BF16))
        hsq8 = _es9.enter_context(nc.sbuf_tensor("hsq8", [128, 8], F32))
        mxh8 = _es9.enter_context(nc.sbuf_tensor("mxh8", [128, 8], F32))
        rs8 = _es9.enter_context(nc.sbuf_tensor("rs8", [128, 8], F32))
        f28 = _es9.enter_context(nc.sbuf_tensor("f28", [128, 8], F32))
        dscr = _es9.enter_context(nc.sbuf_tensor("dscr", [128, 8], F32))
        sqout = _es9.enter_context(nc.sbuf_tensor("sqout", [128, DI], BF16))
        k2f = _es9.enter_context(nc.sbuf_tensor("k2f", [128, DI // 2], F32))
        ktr0 = _es9.enter_context(nc.sbuf_tensor("ktr", [128, 512], BF16))
        ktr1 = _es9.enter_context(nc.sbuf_tensor("ktr1", [128, 512], BF16))
        ps_c0 = _es9.enter_context(nc.psum_tensor("ps_c0", [128, 512], F32))
        ps_c1 = _es9.enter_context(nc.psum_tensor("ps_c1", [128, 512], F32))
        blk5c = _es9.enter_context(nc.Block())
        if True:
            ktr = [ktr0, ktr1]
            ps_c = [ps_c0, ps_c1]
            lnw_ld, t_lnw, v_lnw = S_("lnw_ld"), S_("t_lnw"), S_("v_lnw")
            s_hsq, v_hlw, v_col5, s_ln5 = S_("s_hsq"), S_("v_hlw"), S_("v_col5"), S_("s_ln5")
            v_k2, s_k2 = S_("v_k2"), S_("s_k2")
            t_k2t, v_k2t = S_("t_k2t"), S_("v_k2t")
            k2t_stp = [S_("k2t_st0"), S_("k2t_st1")]

            @blk5c.gpsimd
            def _(g):
                g.wait_ge(S_("t_gu"), 1024)
                g.wait_ge(S_("v_h"), 32)
                if debug:
                    g.wait_ge(S_("tapsem4"), 16)
                for cidx in range(16):
                    if cidx >= 1:
                        g.wait_ge(t_lnw, cidx)
                    g.dma_start(out=lnw32[:], in_=lnw_in[cidx:cidx + 1, :]).then_inc(lnw_ld, 16)

            @blk5c.tensor
            def _(t):
                for cidx in range(16):
                    t.wait_ge(lnw_ld, 16 * (cidx + 1))
                    if cidx >= 2:
                        t.wait_ge(v_lnw, cidx - 1)
                    t.matmul(ps_c[cidx % 2][:, :], onesrow[0:1, 0:128], lnw32[0:1, :],
                             start=True, stop=True).then_inc(t_lnw, 1)
                t.wait_ge(v_lnw, 16)
                for tt in range(NT_OWN):
                    t.wait_ge(v_k2, tt + 1)
                    for r in range(16):
                        g4 = tt * 16 + r
                        if g4 >= 2:
                            t.wait_ge(v_k2t, g4 - 1)
                        ins = None
                        for j in range(4):
                            ii = r * 4 + j
                            ins = t.matmul(ps_c[g4 % 2][:, j * 128:(j + 1) * 128],
                                           hslab[:, tt * DI + ii * 128:tt * DI + (ii + 1) * 128],
                                           identb[:], start=True, stop=True)
                        ins.then_inc(t_k2t, 1)

            @blk5c.vector
            def _(v):
                v = _Drain(v)
                v.wait_ge(S_("v_h"), 32)
                for cidx in range(16):
                    v.wait_ge(t_lnw, cidx + 1)
                    v.tensor_copy(lnwT[:, cidx * 512:(cidx + 1) * 512], ps_c[cidx % 2][:, :]).then_inc(v_lnw, 1)
                for tt in range(NT_OWN):
                    v.tensor_mul(sq8k[:], hslab[:, tt * DI:(tt + 1) * DI], lnwT[:])
                    v.reduce_max(mxh8[:, tt:tt + 1], sq8k[:], AX.X, apply_absolute_value=True).then_inc(v_hlw, 1)
                v.wait_ge(S_("s_hsq"), 8)
                v.tensor_scalar(rs8[:], hsq8[:], 1.0 / DI, 1e-6, ALU.mult, ALU.add).then_inc(v_col5, 1)
                v.wait_ge(s_ln5, 2)
                v.wait_ge(S_("v_w1d"), 2)
                v.tensor_mul(dscr[:], f28[:], f28[:])
                v.tensor_tensor(dscr[:], dscr[:], rs8[:], ALU.mult)
                v.tensor_scalar(dscr[:], dscr[:], -0.5, 1.5, ALU.mult, ALU.add)
                v.tensor_mul(rs8[:], dscr[:], f28[:])
                v.tensor_mul(f28[:], rs8[:], mxh8[:])
                v.tensor_scalar_max(f28[:], f28[:], 1e-5)
                v.tensor_scalar(dsc8p[:], f28[:], wsc[:, 2:3], 1.0 / 127.0, ALU.mult, ALU.mult)
                v.tensor_scalar_mul(mxh8[:], rs8[:], 127.0)
                v.reciprocal(dscr[:], f28[:])
                v.tensor_mul(f28[:], mxh8[:], dscr[:]).then_inc(v_col5, 1)
                for tt in range(NT_OWN):
                    if debug and tt == 0:
                        v.wait_ge(S_("tapsem3"), 16)
                    v.tensor_mul(sq8k[:], hslab[:, tt * DI:(tt + 1) * DI], lnwT[:])
                    for hf in range(2):
                        v.tensor_scalar(k2f[:], sq8k[:, hf * (DI // 2):(hf + 1) * (DI // 2)],
                                        f28[:, tt:tt + 1], 12582912.0, ALU.mult, ALU.add)
                        ins = v.tensor_scalar_add(hslab[:, tt * DI + hf * (DI // 2):tt * DI + (hf + 1) * (DI // 2)],
                                                  k2f[:], -12582912.0)
                    ins.then_inc(v_k2, 1)
                for tt in range(NT_OWN):
                    for r in range(16):
                        g4 = tt * 16 + r
                        v.wait_ge(t_k2t, g4 + 1)
                        if g4 >= 2:
                            v.wait_ge(k2t_stp[g4 % 2], 16 * (g4 // 2))
                        v.tensor_copy(ktr[g4 % 2][:], ps_c[g4 % 2][:, :]).then_inc(v_k2t, 1)

            @blk5c.scalar
            def _(s):
                s = _Drain(s)
                for tt in range(NT_OWN):
                    s.wait_ge(S_("v_h"), 32)
                    s.activation(sqout[:], hslab[:, tt * DI:(tt + 1) * DI], AF.Square,
                                 accum_out=hsq8[:, tt:tt + 1]).then_inc(s_hsq, 1)
                s.wait_ge(v_col5, 1)
                s.activation(f28[:], rs8[:], AF.Ln).then_inc(s_ln5, 1)
                s.activation(f28[:], f28[:], AF.Exp, scale=-0.5).then_inc(s_ln5, 1)
                s.wait_ge(S_("tapsem3"), 16) if debug else None

            @blk5c.sync
            def _(sync):
                if debug:
                    tps = S_("tapsem3")
                    sync.wait_ge(S_("s_hsq"), 8)
                    sync.dma_start(out=tap["h"][:, :], in_=hslab[:, 0:DI]).then_inc(tps, 16)
                for tt in range(NT_OWN):
                    for r in range(16):
                        g4 = tt * 16 + r
                        sync.wait_ge(v_k2t, g4 + 1)
                        dst = k2t_d[(r * 4) * 128:(r * 4 + 4) * 128,
                                    tt * 128:(tt + 1) * 128].rearrange("(n p) c -> p n c", p=128)
                        srcv = ktr[g4 % 2][:].rearrange("p (n c) -> p n c", n=4)
                        sync.dma_start(out=dst, in_=srcv).then_inc(k2t_stp[g4 % 2], 16)

        _es9.close()
    gslab = sb("gslab", [128, 8 * D], BF16)
    _es10 = contextlib.ExitStack()
    wst5a = _es10.enter_context(nc.sbuf_tensor("wst5a", [128, 512], F32))
    wst5b = _es10.enter_context(nc.sbuf_tensor("wst5b", [128, 512], F32))
    wq32dx0 = _es10.enter_context(nc.sbuf_tensor("wq32dx0", [128, 512], F32))
    wq32dx1 = _es10.enter_context(nc.sbuf_tensor("wq32dx1", [128, 512], F32))
    wqi8d0 = _es10.enter_context(nc.sbuf_tensor("wqi8d0", [128, 512], I8))
    wqi8d1 = _es10.enter_context(nc.sbuf_tensor("wqi8d1", [128, 512], I8))
    wqbd0 = _es10.enter_context(nc.sbuf_tensor("wqbd0", [128, 512], BF16))
    wqbd1 = _es10.enter_context(nc.sbuf_tensor("wqbd1", [128, 512], BF16))
    k2c0 = _es10.enter_context(nc.sbuf_tensor("k2c0", [128, 1024], BF16))
    k2c1 = _es10.enter_context(nc.sbuf_tensor("k2c1", [128, 1024], BF16))
    ps_dn0 = _es10.enter_context(nc.psum_tensor("ps_dn0", [128, 512], F32))
    ps_dn1 = _es10.enter_context(nc.psum_tensor("ps_dn1", [128, 512], F32))
    ps_dn2 = _es10.enter_context(nc.psum_tensor("ps_dn2", [128, 512], F32))
    ps_dn3 = _es10.enter_context(nc.psum_tensor("ps_dn3", [128, 512], F32))
    ps_dn4 = _es10.enter_context(nc.psum_tensor("ps_dn4", [128, 512], F32))
    ps_dn5 = _es10.enter_context(nc.psum_tensor("ps_dn5", [128, 512], F32))
    ps_dn6 = _es10.enter_context(nc.psum_tensor("ps_dn6", [128, 512], F32))
    ps_dn7 = _es10.enter_context(nc.psum_tensor("ps_dn7", [128, 512], F32))
    blk5e = _es10.enter_context(nc.Block())
    if True:
        wst5 = [wst5a, wst5b]
        wq32dx = [wq32dx0, wq32dx1]
        wqi8d = [wqi8d0, wqi8d1]
        wqbd = [wqbd0, wqbd1]
        k2c = [k2c0, k2c1]
        ps_dn = [ps_dn0, ps_dn1, ps_dn2, ps_dn3, ps_dn4, ps_dn5, ps_dn6, ps_dn7]
        wd2_ldp = [S_("wd2_ld0"), S_("wd2_ld1")]
        k2_ldp = [S_("k2_ld0"), S_("k2_ld1")]
        v_wq2, s_wq2 = S_("v_wq2"), S_("s_wq2")
        t_dn, s_g = S_("t_dn"), S_("s_g")

        @blk5e.gpsimd
        def _(g):
            g.wait_ge(S_("k2t_st0"), 16 * 64)
            g.wait_ge(S_("k2t_st1"), 16 * 64)
            k = 0
            for mc in range(4):
                for ii in range(64):
                    if k >= 2:
                        g.wait_ge(s_wq2, k - 1)
                    g.dma_start(out=wst5[k % 2][:],
                                in_=wdT[ii * 128:(ii + 1) * 128, mc * 512:(mc + 1) * 512]).then_inc(wd2_ldp[k % 2], 16)
                    k += 1

        @blk5e.sync
        def _(sync):
            sync.wait_ge(S_("k2t_st0"), 16 * 64)
            sync.wait_ge(S_("k2t_st1"), 16 * 64)
            k = 0
            for mc in range(4):
                for ii in range(64):
                    if k >= 2:
                        sync.wait_ge(t_dn, 8 * (k - 1))
                    sync.dma_start(out=k2c[k % 2][:], in_=k2t_d[ii * 128:(ii + 1) * 128, :]).then_inc(k2_ldp[k % 2], 16)
                    k += 1

        @blk5e.vector
        def _(v):
            v = _Drain(v)
            k = 0
            for mc in range(4):
                for ii in range(64):
                    v.wait_ge(s_wq2, k + 1)
                    if k >= 2:
                        v.wait_ge(t_dn, 8 * (k - 1))
                    v.tensor_scalar(wqbd[k % 2][:], wq32dx[k % 2][:], 1.0, -1.0,
                                    ALU.min, ALU.max).then_inc(v_wq2, 2)
                    k += 1

        @blk5e.scalar
        def _(s):
            s = _Drain(s)
            k = 0
            for mc in range(4):
                for ii in range(64):
                    s.wait_ge(wd2_ldp[k % 2], 16 * (k // 2 + 1))
                    if k >= 2:
                        s.wait_ge(v_wq2, 2 * (k - 1))
                    s.activation(wq32dx[k % 2][:], wst5[k % 2][:], AF.Identity,
                                 bias=mcol[:, 0:1], scale=swc[:, 2:3])
                    s.activation(wq32dx[k % 2][:], wq32dx[k % 2][:], AF.Identity,
                                 bias=nmcol[:, 0:1]).then_inc(s_wq2, 1)
                    k += 1
                if mc == 0:
                    s.wait_ge(S_("v_col5"), 2)
                s.wait_ge(t_dn, 8 * 64 * (mc + 1))
                ins = None
                for tt in range(8):
                    ins = s.activation(gslab[:, tt * D + mc * 512:tt * D + (mc + 1) * 512],
                                       ps_dn[tt][:, :], AF.Silu, scale=dsc8p[:, tt:tt + 1])
                ins.then_inc(s_g, 1)

        @blk5e.tensor
        def _(t):
            for mc in range(4):
                if mc >= 1:
                    t.wait_ge(s_g, mc)
                for ii in range(64):
                    k = mc * 64 + ii
                    t.wait_ge(v_wq2, 2 * (k + 1))
                    t.wait_ge(k2_ldp[k % 2], 16 * (k // 2 + 1))
                    ins = None
                    for tt in range(8):
                        ins = t.matmul(ps_dn[tt][:, :],
                                       k2c[k % 2][:, tt * 128:(tt + 1) * 128],
                                       wqbd[k % 2][:],
                                       start=(ii == 0), stop=(ii == 63))
                    ins.then_inc(t_dn, 8)

    _es10.close()
    _es11 = contextlib.ExitStack()
    rso = _es11.enter_context(nc.sbuf_tensor("rso", [128, D], BF16))
    xnp = _es11.enter_context(nc.sbuf_tensor("xnp", [128, D], BF16))
    xob = _es11.enter_context(nc.sbuf_tensor("xob", [128, D], F32))
    srow = _es11.enter_context(nc.sbuf_tensor("srow", [128, D], F32))
    sgrow = _es11.enter_context(nc.sbuf_tensor("sgrow", [128, D], F32))
    outb0 = _es11.enter_context(nc.sbuf_tensor("outb0", [128, D], F32))
    outb1 = _es11.enter_context(nc.sbuf_tensor("outb1", [128, D], F32))
    blk6 = _es11.enter_context(nc.Block())
    if True:
        outb = [outb0, outb1]
        rso_ld, v_out = S_("rso_ld"), S_("v_out")
        out_stp = [S_("out_st0"), S_("out_st1")]

        @blk6.sync
        def _(sync):
            sync.wait_ge(S_("g_coll"), 1)
            sync.wait_ge(S_("s_g"), 4)
            for tt in range(NT_OWN):
                if tt >= 1:
                    sync.wait_ge(v_out, 3 * tt)
                sync.dma_start(out=rso[:], in_=rs_out[tt * 128:(tt + 1) * 128, :]).then_inc(rso_ld, 16)
                sync.dma_start(out=xnp[:], in_=xn_park[tt * 128:(tt + 1) * 128, :]).then_inc(rso_ld, 16)
                sync.dma_start(out=xob[:], in_=x_own[tt * 128:(tt + 1) * 128, :]).then_inc(rso_ld, 16)
                sync.wait_ge(v_out, 3 * tt + 3)
                sync.dma_start(out=out[tt * 128:(tt + 1) * 128, :], in_=outb[tt % 2][:]).then_inc(out_stp[tt % 2], 16)
            if debug:
                tapsem = S_("tapsem2")
                sync.dma_start(out=tap["xn"][:, :], in_=xn_park[:, :]).then_inc(tapsem, 16)
                sync.dma_start(out=tap["wsc"][:, 0:3], in_=wsc[:, 0:3]).then_inc(tapsem, 16)
                sync.dma_start(out=tap["wsc"][:, 4:7], in_=swc[:, 0:3]).then_inc(tapsem, 16)
                sync.dma_start(out=tap["rs"][:, :], in_=rs_out[:, :]).then_inc(tapsem, 16)
                for tt in range(NT_OWN):
                    sync.dma_start(out=tap["g"][tt * 128:(tt + 1) * 128, :],
                                   in_=gslab[:, tt * D:(tt + 1) * D]).then_inc(tapsem, 16)
                sync.wait_ge(tapsem, 16 * 12)
            sync.wait_ge(out_stp[0], 16 * (NT_OWN // 2))
            sync.wait_ge(out_stp[1], 16 * (NT_OWN // 2))

        @blk6.vector
        def _(v):
            v = _Drain(v)
            v.wait_ge(S_("s_g"), 4)
            for tt in range(NT_OWN):
                v.wait_ge(rso_ld, 16 * (3 * tt + 3))
                v.tensor_add(srow[:], rso[:], xnp[:]).then_inc(v_out, 1)
                v.tensor_mul(sgrow[:], srow[:], gslab[:, tt * D:(tt + 1) * D]).then_inc(v_out, 1)
                if tt >= 2:
                    v.wait_ge(out_stp[tt % 2], 16 * (tt // 2))
                v.tensor_add(outb[tt % 2][:], sgrow[:], xob[:]).then_inc(v_out, 1)

    _es11.close()
    return nc, tap


def _host_constants():
    d = 64
    inv = 1.0 / (10000.0 ** (np.arange(0, d, 2, dtype=np.float64) / d))
    pos = np.arange(S, dtype=np.float64)
    ang = pos[:, None] * inv[None, :]
    cos = np.cos(ang).astype(np.float32)
    sin = np.sin(ang).astype(np.float32)
    tc = np.zeros((128, 32 * NTB), np.float32)
    ts_ = np.zeros((128, 32 * NTB), np.float32)
    for pt in range(NTB):
        tc[:, pt * 32:(pt + 1) * 32] = cos[pt * 128:(pt + 1) * 128, :]
        ts_[:, pt * 32:(pt + 1) * 32] = sin[pt * 128:(pt + 1) * 128, :]
    mk = np.zeros((128, 4 * 512), np.float32)
    for r in range(4):
        p = np.arange(128)[:, None]
        f = np.arange(512)[None, :]
        mk[:, r * 512:(r + 1) * 512] = np.where(r * 128 + p > f, -1.0e30, 0.0)
    ident = np.eye(128, dtype=np.float32)
    invc = (1.0 / (np.arange(S, dtype=np.float64) + 1.0)).astype(ml_dtypes.bfloat16).reshape(1, S)
    return tc, ts_, mk, ident, invc


def kernel(L0_attn_x, L0_attn_w_in, L0_attn_w_out, L0_attn_rms_w,
           L0_ffn_w_gate, L0_ffn_w_up, L0_ffn_w_down, L0_ffn_ln_weight):
    x = np.ascontiguousarray(np.asarray(L0_attn_x, dtype=np.float32).reshape(T_ALL, D))
    w_in = np.asarray(L0_attn_w_in, dtype=np.float32)
    w_out = np.asarray(L0_attn_w_out, dtype=np.float32)
    rms_w = np.asarray(L0_attn_rms_w, dtype=np.float32)
    wg = np.asarray(L0_ffn_w_gate, dtype=np.float32)
    wu = np.asarray(L0_ffn_w_up, dtype=np.float32)
    wd = np.asarray(L0_ffn_w_down, dtype=np.float32)
    lnw = np.asarray(L0_ffn_ln_weight, dtype=np.float32)

    tc, ts_, mk, ident, invc = _host_constants()
    wgT = np.ascontiguousarray(wg.T)
    wuT = np.ascontiguousarray(wu.T)
    wdT = np.ascontiguousarray(wd.T)
    rms_colh = np.ascontiguousarray(rms_w.reshape(16, 128).T)
    lnw_h = np.ascontiguousarray(lnw.reshape(16, 512))

    nc, _ = build_nc(debug=False)
    in_maps = []
    for g in range(8):
        rows = np.concatenate([w_in[g * 256:(g + 1) * 256],
                               w_in[2048 + g * 64:2048 + (g + 1) * 64],
                               w_in[2560 + g * 64:2560 + (g + 1) * 64]], axis=0)
        w_inT_g = np.ascontiguousarray(rows.T)
        w_outT_g = np.ascontiguousarray(w_out[:, g * 256:(g + 1) * 256].T)
        in_maps.append({
            "x_all": x,
            "x_own": np.ascontiguousarray(x[g * T_OWN:(g + 1) * T_OWN]),
            "w_inT": w_inT_g,
            "w_outT": w_outT_g,
            "rms_col": rms_colh,
            "wgT": wgT, "wuT": wuT, "wdT": wdT,
            "lnw_in": lnw_h,
            "trig_cos": tc, "trig_sin": ts_,
            "masks": mk, "ident_i": ident, "invcnt": invc,
        })
    res = run_bass_kernel_spmd(nc, in_maps, core_ids=list(range(8)))
    outs = [res.results[g]["out"] for g in range(8)]
    return np.concatenate(outs, axis=0).reshape(B, S, D).astype(np.float32)


# version 05
# speedup vs baseline: 1.1963x; 1.1531x over previous
import numpy as np
import ml_dtypes
import contextlib
import concourse.bass as bass
import concourse.mybir as mybir
from concourse.bass_utils import run_bass_kernel_spmd

F32 = mybir.dt.float32
BF16 = mybir.dt.bfloat16
I8 = mybir.dt.int8
AF = mybir.ActivationFunctionType
ALU = mybir.AluOpType
AX = mybir.AxisListType

B = 2
S = 4096
D = 2048
NTB = S // 128
NT_ALL = B * NTB
T_ALL = B * S
T_OWN = 1024
NT_OWN = 8
CC = 384
DI = 8192




_DRAIN_OPS = {
    "tensor_copy", "tensor_tensor", "tensor_add", "tensor_sub", "tensor_mul",
    "tensor_max", "tensor_scalar", "tensor_single_scalar", "tensor_scalar_add",
    "tensor_scalar_sub", "tensor_scalar_mul", "tensor_scalar_max", "tensor_scalar_min",
    "tensor_relu", "tensor_reduce", "reduce_max", "reduce_sum", "scalar_tensor_tensor",
    "tensor_tensor_scan", "memset", "activation", "copy", "mul", "add", "square",
    "sqrt", "sign", "reciprocal", "select", "bn_stats", "bn_aggr", "transpose",
    "stream_shuffle", "pool", "pool_max", "pool_avg", "copy_predicated", "memzero",
}


class _Drain:

    def __init__(self, eng):
        self._e = eng

    def __getattr__(self, n):
        f = getattr(self._e, n)
        if n in _DRAIN_OPS:
            def wrapped(*a, **kw):
                self._e.drain()
                return f(*a, **kw)
            return wrapped
        return f


def build_nc(debug=False):
    nc = bass.Bass("TRN2", target_bir_lowering=False, debug=False)

    x_all = nc.dram_tensor("x_all", [T_ALL, D], F32, kind="ExternalInput").ap()
    x_own = nc.dram_tensor("x_own", [T_OWN, D], F32, kind="ExternalInput").ap()
    w_inT = nc.dram_tensor("w_inT", [D, CC], F32, kind="ExternalInput").ap()
    w_outT = nc.dram_tensor("w_outT", [256, D], F32, kind="ExternalInput").ap()
    rms_col = nc.dram_tensor("rms_col", [128, 16], F32, kind="ExternalInput").ap()
    wgT = nc.dram_tensor("wgT", [D, DI], F32, kind="ExternalInput").ap()
    wuT = nc.dram_tensor("wuT", [D, DI], F32, kind="ExternalInput").ap()
    wdT = nc.dram_tensor("wdT", [DI, D], F32, kind="ExternalInput").ap()
    lnw_in = nc.dram_tensor("lnw_in", [16, 512], F32, kind="ExternalInput").ap()
    trig_cos = nc.dram_tensor("trig_cos", [128, 32 * NTB], F32, kind="ExternalInput").ap()
    trig_sin = nc.dram_tensor("trig_sin", [128, 32 * NTB], F32, kind="ExternalInput").ap()
    masks = nc.dram_tensor("masks", [128, 4 * 512], F32, kind="ExternalInput").ap()
    ident_i = nc.dram_tensor("ident_i", [128, 128], F32, kind="ExternalInput").ap()
    invcnt = nc.dram_tensor("invcnt", [1, S], BF16, kind="ExternalInput").ap()

    out = nc.dram_tensor("out", [T_OWN, D], F32, kind="ExternalOutput").ap()

    rs_in = nc.dram_tensor("rs_in", [T_ALL, D], BF16, kind="Internal").ap()
    rs_out = nc.dram_tensor("rs_out", [T_OWN, D], BF16, kind="Internal").ap()
    k2t_d = nc.dram_tensor("k2t_d", [DI, T_OWN], BF16, kind="Internal").ap()
    xn_park = nc.dram_tensor("xn_park", [T_OWN, D], BF16, kind="Internal").ap()

    tap = {}
    if debug:
        tap["qaug"] = nc.dram_tensor("tap_qaug", [8 * 65, S], BF16, kind="ExternalOutput").ap()
        tap["kaug"] = nc.dram_tensor("tap_kaug", [B * 65, S], BF16, kind="ExternalOutput").ap()
        tap["v"] = nc.dram_tensor("tap_v", [B * 128, NTB * 65], BF16, kind="ExternalOutput").ap()
        tap["oT"] = nc.dram_tensor("tap_oT", [2 * 128, T_ALL], BF16, kind="ExternalOutput").ap()
        tap["xn"] = nc.dram_tensor("tap_xn", [T_OWN, D], BF16, kind="ExternalOutput").ap()
        tap["g"] = nc.dram_tensor("tap_g", [T_OWN, D], BF16, kind="ExternalOutput").ap()
        tap["wsc"] = nc.dram_tensor("tap_wsc", [128, 8], F32, kind="ExternalOutput").ap()
        tap["rs"] = nc.dram_tensor("tap_rs", [T_OWN, D], BF16, kind="ExternalOutput").ap()
        tap["h"] = nc.dram_tensor("tap_h", [128, DI], BF16, kind="ExternalOutput").ap()
        tap["kx"] = nc.dram_tensor("tap_kx", [128, 16 * NT_OWN * 128], BF16, kind="ExternalOutput").ap()
        tap["hg"] = nc.dram_tensor("tap_hg", [128, 4 * 512], BF16, kind="ExternalOutput").ap()
        tap["wq"] = nc.dram_tensor("tap_wq", [128, 512], BF16, kind="ExternalOutput").ap()
        tap["hu"] = nc.dram_tensor("tap_hu", [128, 4 * 512], BF16, kind="ExternalOutput").ap()

    sb = nc.alloc_sbuf_tensor
    sems = {}

    def S_(name):
        if name not in sems:
            sems[name] = nc.alloc_semaphore(name)
        return sems[name]

    identb = sb("identb", [128, 128], BF16)
    ones_r = sb("ones_r", [1, 128], BF16)
    onesrow = sb("onesrow", [1, 512], F32)
    onescol = sb("onescol", [128, 1], F32)
    onescolb = sb("onescolb", [128, 1], BF16)
    c127 = sb("c127", [128, 1], F32)
    mcol = sb("mcol", [128, 1], F32)
    nmcol = sb("nmcol", [128, 1], F32)
    expb = sb("expb", [128, 1], F32)
    wsum = sb("wsum", [128, 192], F32)
    w1s = sb("w1s", [1, 4], F32)
    wsc = sb("wsc", [128, 4], F32)
    swc = sb("swc", [128, 4], F32)
    dsc8p = sb("dsc8p", [128, 8], F32)

    units = [(b, h, c) for b in range(B) for h in range(4) for c in range(8)]
    NU = len(units)
    grp_of = [2 * c + 2 for (_, _, c) in units]
    cum_grp = np.concatenate([[0], np.cumsum(grp_of)]).astype(int).tolist()
    NGRP = cum_grp[-1]

    _es0 = contextlib.ExitStack()
    masks_sb = _es0.enter_context(nc.sbuf_tensor("masks_sb", [128, 4 * 512], F32))
    invcnt_sb = _es0.enter_context(nc.sbuf_tensor("invcnt_sb", [1, S], BF16))
    winb = _es0.enter_context(nc.sbuf_tensor("winb", [128, 16 * CC], BF16))
    rmsc = _es0.enter_context(nc.sbuf_tensor("rmsc", [128, 16], F32))
    qa0 = _es0.enter_context(nc.sbuf_tensor("qa0", [65, S], BF16))
    qa1 = _es0.enter_context(nc.sbuf_tensor("qa1", [65, S], BF16))
    qa2 = _es0.enter_context(nc.sbuf_tensor("qa2", [65, S], BF16))
    qa3 = _es0.enter_context(nc.sbuf_tensor("qa3", [65, S], BF16))
    qa4 = _es0.enter_context(nc.sbuf_tensor("qa4", [65, S], BF16))
    qa5 = _es0.enter_context(nc.sbuf_tensor("qa5", [65, S], BF16))
    qa6 = _es0.enter_context(nc.sbuf_tensor("qa6", [65, S], BF16))
    qa7 = _es0.enter_context(nc.sbuf_tensor("qa7", [65, S], BF16))
    ka0 = _es0.enter_context(nc.sbuf_tensor("ka0", [65, S], BF16))
    ka1 = _es0.enter_context(nc.sbuf_tensor("ka1", [65, S], BF16))
    vs0 = _es0.enter_context(nc.sbuf_tensor("vs0", [128, NTB * 65], BF16))
    vs1 = _es0.enter_context(nc.sbuf_tensor("vs1", [128, NTB * 65], BF16))
    if True:
        q_aug = [qa0, qa1, qa2, qa3, qa4, qa5, qa6, qa7]
        k_aug = [ka0, ka1]
        v_sb = [vs0, vs1]

        _es1 = contextlib.ExitStack()
        trigc = _es1.enter_context(nc.sbuf_tensor("trigc", [128, 32 * NTB], F32))
        trigs = _es1.enter_context(nc.sbuf_tensor("trigs", [128, 32 * NTB], F32))
        ident32 = _es1.enter_context(nc.sbuf_tensor("ident32", [128, 128], F32))
        win32 = _es1.enter_context(nc.sbuf_tensor("win32", [128, CC], F32))
        xt0 = _es1.enter_context(nc.sbuf_tensor("xt0", [128, D], F32))
        xt1 = _es1.enter_context(nc.sbuf_tensor("xt1", [128, D], F32))
        xnt0 = _es1.enter_context(nc.sbuf_tensor("xnt0", [128, D], BF16))
        xnt1 = _es1.enter_context(nc.sbuf_tensor("xnt1", [128, D], BF16))
        xnT = _es1.enter_context(nc.sbuf_tensor("xnT", [128, 16 * 128], BF16))
        sqs = _es1.enter_context(nc.sbuf_tensor("sqs", [128, D], BF16))
        stat = _es1.enter_context(nc.sbuf_tensor("stat", [128, 24], F32))
        ropeq0 = _es1.enter_context(nc.sbuf_tensor("ropeq0", [128, 256], BF16))
        ropeq1 = _es1.enter_context(nc.sbuf_tensor("ropeq1", [128, 256], BF16))
        ropek0 = _es1.enter_context(nc.sbuf_tensor("ropek0", [128, 64], BF16))
        ropek1 = _es1.enter_context(nc.sbuf_tensor("ropek1", [128, 64], BF16))
        rsc1 = _es1.enter_context(nc.sbuf_tensor("rsc1", [128, 128], F32))
        rsc2 = _es1.enter_context(nc.sbuf_tensor("rsc2", [128, 128], F32))
        pw0 = _es1.enter_context(nc.sbuf_tensor("pw0", [128, 2048], F32))
        pw1 = _es1.enter_context(nc.sbuf_tensor("pw1", [128, 2048], F32))
        pabs = _es1.enter_context(nc.sbuf_tensor("pabs", [128, 2048], BF16))
        ps_tr0 = _es1.enter_context(nc.psum_tensor("ps_tr0", [128, 512], F32))
        ps_tr1 = _es1.enter_context(nc.psum_tensor("ps_tr1", [128, 512], F32))
        ps_y = _es1.enter_context(nc.psum_tensor("ps_y", [128, CC], F32))
        ps_qt = _es1.enter_context(nc.psum_tensor("ps_qt", [64, 512], F32))
        ps_kt = _es1.enter_context(nc.psum_tensor("ps_kt", [64, 128], F32))
        blk = _es1.enter_context(nc.Block())
        if True:
            xt = [xt0, xt1]
            xnt = [xnt0, xnt1]
            ropeq = [ropeq0, ropeq1]
            ropek = [ropek0, ropek1]
            c_ld, win_ld = S_("c_ld"), S_("win_ld")
            x_ldp = [S_("x_ld0"), S_("x_ld1")]
            v_cst, win_cv = S_("v_cst"), S_("win_cv")
            s_sq, v_st, v_red = S_("s_sq"), S_("v_st"), S_("v_red")
            t_tr, v_xnT, t_y = S_("t_tr"), S_("v_xnT"), S_("t_y")
            v_rope, t_qk, v_qkev = S_("v_rope"), S_("t_qk"), S_("v_qkev")
            gu_ldp = [S_("gu_ld0"), S_("gu_ld1")]
            s_pw = S_("s_pw")

            @blk.sync
            def _(sync):
                for dst, src in [(trigc[:], trig_cos[:]), (trigs[:], trig_sin[:]),
                                 (masks_sb[:], masks[:]), (invcnt_sb[:], invcnt[:]),
                                 (rmsc[:], rms_col[:]), (ident32[:], ident_i[:])]:
                    sync.dma_start(out=dst, in_=src).then_inc(c_ld, 16)
                for ti in range(NT_ALL):
                    if ti >= 2:
                        sync.wait_ge(s_sq, 4 * (ti - 2) + 4)
                        sync.wait_ge(v_red, ti - 1)
                    sync.dma_start(out=xt[ti % 2][:], in_=x_all[ti * 128:(ti + 1) * 128, :]).then_inc(x_ldp[ti % 2], 16)

            @blk.gpsimd
            def _(g):
                for mi in range(16):
                    if mi >= 1:
                        g.wait_ge(win_cv, mi)
                    g.dma_start(out=win32[:], in_=w_inT[mi * 128:(mi + 1) * 128, :]).then_inc(win_ld, 16)
                ch = 0
                for w_ap in (wgT, wuT):
                    for r in range(16):
                        for cc2 in range(4):
                            if ch >= 2:
                                g.wait_ge(s_pw, ch - 1)
                            g.dma_start(out=(pw0 if ch % 2 == 0 else pw1)[:],
                                        in_=w_ap[r * 128:(r + 1) * 128, cc2 * 2048:(cc2 + 1) * 2048]).then_inc(gu_ldp[ch % 2], 16)
                            ch += 1

            @blk.vector
            def _(v):
                v = _Drain(v)
                v.wait_ge(c_ld, 16 * 6)
                v.memset(ones_r[:], 1.0)
                v.memset(onesrow[:], 1.0)
                v.memset(onescol[:], 1.0)
                v.memset(onescolb[:], 1.0)
                v.memset(c127[:], 127.0)
                v.memset(mcol[:], 12582912.0)
                v.memset(nmcol[:], -12582912.0)
                v.memset(expb[:], -40.0)
                v.tensor_copy(identb[:], ident32[:])
                for b in range(B):
                    v.memset(k_aug[b][64:65, :], 1.0)
                    v.memset(v_sb[b].rearrange("p (n c) -> p n c", c=65)[:, :, 64:65], 1.0)
                v.memset(wsum[:], 0.0).then_inc(v_cst, 1)
                for mi in range(16):
                    v.wait_ge(win_ld, 16 * (mi + 1))
                    v.tensor_scalar(winb[:, mi * CC:(mi + 1) * CC], win32[:],
                                    rmsc[:, mi:mi + 1], None, ALU.mult).then_inc(win_cv, 1)

                for ti in range(NT_ALL):
                    xb = xt[ti % 2]
                    o = 12 * (ti % 2)
                    v.wait_ge(x_ldp[ti % 2], 16 * (ti // 2 + 1))
                    v.reduce_sum(stat[:, o + 0:o + 1], xb[:], AX.X).then_inc(v_red, 1)
                    v.wait_ge(s_sq, 4 * ti + 1)
                    v.tensor_scalar_mul(stat[:, o + 2:o + 3], stat[:, o + 0:o + 1], 1.0 / D)
                    v.tensor_scalar_mul(stat[:, o + 3:o + 4], stat[:, o + 1:o + 2], 1.0 / D)
                    v.tensor_scalar(stat[:, o + 4:o + 5], stat[:, o + 2:o + 3], stat[:, o + 2:o + 3], None, ALU.mult)
                    v.tensor_sub(stat[:, o + 5:o + 6], stat[:, o + 3:o + 4], stat[:, o + 4:o + 5])
                    v.tensor_scalar_add(stat[:, o + 6:o + 7], stat[:, o + 5:o + 6], 1e-5)
                    v.scalar_tensor_tensor(stat[:, o + 7:o + 8], stat[:, o + 6:o + 7], 1e-6,
                                           stat[:, o + 5:o + 6], ALU.mult, ALU.add).then_inc(v_st, 1)
                    v.wait_ge(s_sq, 4 * ti + 3)
                    v.tensor_scalar(stat[:, o + 11:o + 12], stat[:, o + 10:o + 11], stat[:, o + 10:o + 11], None, ALU.mult)
                    v.scalar_tensor_tensor(stat[:, o + 11:o + 12], stat[:, o + 11:o + 12], -0.5,
                                           stat[:, o + 7:o + 8], ALU.mult, ALU.mult)
                    v.tensor_scalar_add(stat[:, o + 11:o + 12], stat[:, o + 11:o + 12], 1.5)
                    v.tensor_mul(stat[:, o + 10:o + 11], stat[:, o + 11:o + 12], stat[:, o + 10:o + 11])
                    v.tensor_scalar(stat[:, o + 11:o + 12], stat[:, o + 2:o + 3],
                                    stat[:, o + 10:o + 11], -1.0, ALU.mult, ALU.mult).then_inc(v_st, 1)

                    for r in range(4):
                        g4 = 4 * ti + r
                        pst = [ps_tr0, ps_tr1][g4 % 2]
                        v.wait_ge(t_tr, g4 + 1)
                        v.tensor_copy(xnT[:, r * 512:(r + 1) * 512], pst[:]).then_inc(v_xnT, 1)

                    v.wait_ge(t_y, ti + 1)
                    b = ti // NTB
                    pt = ti % NTB
                    cos4 = trigc[:, pt * 32:(pt + 1) * 32].unsqueeze(1).to_broadcast([128, 4, 32])
                    sin4 = trigs[:, pt * 32:(pt + 1) * 32].unsqueeze(1).to_broadcast([128, 4, 32])
                    ya = ps_y[:, 0:256].rearrange("p (h x) -> p h x", h=4)
                    rqv = ropeq[ti % 2][:].rearrange("p (h x) -> p h x", h=4)
                    r1 = rsc1[:].rearrange("p (h x) -> p h x", h=4)[:, :, 0:32]
                    r2 = rsc2[:].rearrange("p (h x) -> p h x", h=4)[:, :, 0:32]
                    v.tensor_tensor(r1, ya[:, :, 0:32], cos4, ALU.mult)
                    v.tensor_tensor(r2, ya[:, :, 32:64], sin4, ALU.mult)
                    v.tensor_sub(rqv[:, :, 0:32], r1, r2)
                    v.tensor_tensor(r1, ya[:, :, 32:64], cos4, ALU.mult)
                    v.tensor_tensor(r2, ya[:, :, 0:32], sin4, ALU.mult)
                    v.tensor_add(rqv[:, :, 32:64], r1, r2)
                    cos1 = trigc[:, pt * 32:(pt + 1) * 32]
                    sin1 = trigs[:, pt * 32:(pt + 1) * 32]
                    rk = ropek[ti % 2]
                    v.tensor_tensor(rsc1[:, 0:32], ps_y[:, 256:288], cos1, ALU.mult)
                    v.tensor_tensor(rsc2[:, 0:32], ps_y[:, 288:320], sin1, ALU.mult)
                    v.tensor_sub(rk[:, 0:32], rsc1[:, 0:32], rsc2[:, 0:32])
                    v.tensor_tensor(rsc1[:, 0:32], ps_y[:, 288:320], cos1, ALU.mult)
                    v.tensor_tensor(rsc2[:, 0:32], ps_y[:, 256:288], sin1, ALU.mult)
                    v.tensor_add(rk[:, 32:64], rsc1[:, 0:32], rsc2[:, 0:32])
                    v.tensor_copy(v_sb[b][:, pt * 65:pt * 65 + 64], ps_y[:, 320:384]).then_inc(v_rope, 1)

                    v.wait_ge(t_qk, ti + 1)
                    for h in range(4):
                        v.tensor_copy(q_aug[b * 4 + h][0:64, pt * 128:(pt + 1) * 128],
                                      ps_qt[0:64, h * 128:(h + 1) * 128])
                    v.tensor_copy(k_aug[b][0:64, pt * 128:(pt + 1) * 128], ps_kt[:, :]).then_inc(v_qkev, 1)

            @blk.scalar
            def _(s):
                s = _Drain(s)
                s.wait_ge(v_cst, 1)
                ch = 0
                for ti in range(NT_ALL):
                    xb = xt[ti % 2]
                    o = 12 * (ti % 2)
                    s.wait_ge(x_ldp[ti % 2], 16 * (ti // 2 + 1))
                    s.activation(sqs[:], xb[:], AF.Square, accum_out=stat[:, o + 1:o + 2]).then_inc(s_sq, 1)
                    s.wait_ge(v_st, 2 * ti + 1)
                    s.activation(stat[:, o + 9:o + 10], stat[:, o + 7:o + 8], AF.Ln).then_inc(s_sq, 1)
                    s.activation(stat[:, o + 10:o + 11], stat[:, o + 9:o + 10], AF.Exp, scale=-0.5).then_inc(s_sq, 1)
                    s.wait_ge(v_st, 2 * ti + 2)
                    if ti >= 2:
                        s.wait_ge(t_tr, 4 * (ti - 1))
                    s.activation(xnt[ti % 2][:], xb[:], AF.Identity,
                                 bias=stat[:, o + 11:o + 12], scale=stat[:, o + 10:o + 11]).then_inc(s_sq, 1)
                    for _2 in range(2):
                        if ch < 128:
                            s.wait_ge(gu_ldp[ch % 2], 16 * (ch // 2 + 1))
                            s.activation(pabs[:], (pw0 if ch % 2 == 0 else pw1)[:], AF.Abs,
                                         accum_out=wsum[:, ch:ch + 1]).then_inc(s_pw, 1)
                            ch += 1

            @blk.tensor
            def _(t):
                t.wait_ge(v_cst, 1)
                t.wait_ge(win_cv, 16)
                for ti in range(NT_ALL):
                    t.wait_ge(s_sq, 4 * ti + 4)
                    for r in range(4):
                        g4 = 4 * ti + r
                        pst = [ps_tr0, ps_tr1][g4 % 2]
                        if g4 >= 2:
                            t.wait_ge(v_xnT, g4 - 1)
                        ins = None
                        for j in range(4):
                            mi = r * 4 + j
                            ins = t.matmul(pst[:, j * 128:(j + 1) * 128],
                                           xnt[ti % 2][:, mi * 128:(mi + 1) * 128], identb[:],
                                           start=True, stop=True)
                        ins.then_inc(t_tr, 1)
                    t.wait_ge(v_xnT, 4 * ti + 4)
                    if ti >= 1:
                        t.wait_ge(v_rope, ti)
                    ins = None
                    for mi in range(16):
                        ins = t.matmul(ps_y[:], xnT[:, mi * 128:(mi + 1) * 128],
                                       winb[:, mi * CC:(mi + 1) * CC], start=(mi == 0), stop=(mi == 15))
                    ins.then_inc(t_y, 1)
                    t.wait_ge(v_rope, ti + 1)
                    if ti >= 1:
                        t.wait_ge(v_qkev, ti)
                    for h in range(4):
                        t.matmul(ps_qt[0:64, h * 128:(h + 1) * 128],
                                 ropeq[ti % 2][:, h * 64:(h + 1) * 64], identb[:],
                                 start=True, stop=True)
                    t.matmul(ps_kt[:, :], ropek[ti % 2][:], identb[:],
                             start=True, stop=True).then_inc(t_qk, 1)

        _es1.close()
        _es2 = contextlib.ExitStack()
        kpre = _es2.enter_context(nc.sbuf_tensor("kpre", [64, S], F32))
        dotbf = _es2.enter_context(nc.sbuf_tensor("dotbf", [64, S], BF16))
        wred = _es2.enter_context(nc.sbuf_tensor("wred", [128, 2], F32))
        ps_d0 = _es2.enter_context(nc.psum_tensor("ps_d0", [1, 512], F32))
        ps_d1 = _es2.enter_context(nc.psum_tensor("ps_d1", [1, 512], F32))
        ps_w1 = _es2.enter_context(nc.psum_tensor("ps_w1", [1, 2], F32))
        ps_bc = _es2.enter_context(nc.psum_tensor("ps_bc", [128, 2], F32))
        blk2 = _es2.enter_context(nc.Block())
        if True:
            v_scan, t_dot, v_aug = S_("v_scan"), S_("t_dot"), S_("v_aug")
            v_wred, t_w1, v_w1 = S_("v_wred"), S_("t_w1"), S_("v_w1")

            @blk2.vector
            def _(v):
                v = _Drain(v)
                v.wait_ge(S_("s_pw"), 128)
                v.wait_ge(S_("t_qk"), NT_ALL)
                v.reduce_sum(wred[:, 0:1], wsum[:, 0:64], AX.X)
                v.reduce_sum(wred[:, 1:2], wsum[:, 64:128], AX.X).then_inc(v_wred, 2)
                v.wait_ge(t_w1, 2)
                v.tensor_scalar_mul(w1s[:, 0:2], ps_w1[0:1, 0:2], 1.0 / (D * DI))
                v.tensor_scalar_max(w1s[:, 0:2], w1s[:, 0:2], 1e-5).then_inc(v_w1, 1)
                v.wait_ge(t_w1, 4)
                v.tensor_copy(wsc[:, 0:2], ps_bc[:, 0:2])
                v.reciprocal(swc[:, 0:2], wsc[:, 0:2]).then_inc(v_w1, 1)

                v.wait_ge(S_("v_qkev"), NT_ALL)
                for b in range(B):
                    v.tensor_tensor_scan(kpre[:], onescol[0:64, 0:1].to_broadcast([64, S]),
                                         k_aug[b][0:64, :], 0.0, ALU.mult, ALU.add).then_inc(v_scan, 1)
                    for h in range(4):
                        i = b * 4 + h
                        if i > 0:
                            v.wait_ge(t_dot, i * 8)
                        v.tensor_mul(dotbf[:], q_aug[i][0:64, :], kpre[:]).then_inc(v_scan, 1)
                        for c in range(8):
                            v.wait_ge(t_dot, i * 8 + c + 1)
                            psd = [ps_d0, ps_d1][c % 2]
                            v.scalar_tensor_tensor(q_aug[i][64:65, c * 512:(c + 1) * 512],
                                                   psd[0:1, :], -1.0,
                                                   invcnt_sb[0:1, c * 512:(c + 1) * 512],
                                                   ALU.mult, ALU.mult).then_inc(v_aug, 1)

            @blk2.tensor
            def _(t):
                t.wait_ge(v_wred, 2)
                t.matmul(ps_w1[0:1, 0:1], wred[:, 0:1], onescol[:, 0:1], start=True, stop=True)
                t.matmul(ps_w1[0:1, 1:2], wred[:, 1:2], onescol[:, 0:1], start=True, stop=True).then_inc(t_w1, 2)
                t.wait_ge(v_w1, 1)
                t.matmul(ps_bc[:, 0:1], onesrow[0:1, 0:128], w1s[0:1, 0:1], start=True, stop=True)
                t.matmul(ps_bc[:, 1:2], onesrow[0:1, 0:128], w1s[0:1, 1:2], start=True, stop=True).then_inc(t_w1, 2)
                for b in range(B):
                    for h in range(4):
                        i = b * 4 + h
                        t.wait_ge(S_("v_scan"), b + i + 2)
                        for c in range(8):
                            psd = [ps_d0, ps_d1][c % 2]
                            if c >= 2:
                                t.wait_ge(S_("v_aug"), i * 8 + c - 1)
                            t.matmul(psd[0:1, :], onescolb[0:64, 0:1], dotbf[:, c * 512:(c + 1) * 512],
                                     start=True, stop=True).then_inc(t_dot, 1)

        _es2.close()
        _es3 = contextlib.ExitStack()
        oT0 = _es3.enter_context(nc.sbuf_tensor("oT0", [128, T_ALL], BF16))
        oT1 = _es3.enter_context(nc.sbuf_tensor("oT1", [128, T_ALL], BF16))
        if True:
            oT = [oT0, oT1]
            _es4 = contextlib.ExitStack()
            pt0 = _es4.enter_context(nc.sbuf_tensor("pt0", [128, 1024], BF16))
            pt1 = _es4.enter_context(nc.sbuf_tensor("pt1", [128, 1024], BF16))
            rden = _es4.enter_context(nc.sbuf_tensor("rden", [1, 512], BF16))
            rdenf = _es4.enter_context(nc.sbuf_tensor("rdenf", [1, 512], F32))
            sce = _es4.enter_context(nc.sbuf_tensor("sce", [64, 512], BF16))
            tmpo0 = _es4.enter_context(nc.sbuf_tensor("tmpo0", [64, 512], BF16))
            tmpo1 = _es4.enter_context(nc.sbuf_tensor("tmpo1", [64, 512], BF16))
            pwd0 = _es4.enter_context(nc.sbuf_tensor("pwd0", [128, 2048], F32))
            pwd1 = _es4.enter_context(nc.sbuf_tensor("pwd1", [128, 2048], F32))
            ps_s0 = _es4.enter_context(nc.psum_tensor("ps_s0", [128, 1024], F32))
            ps_s1 = _es4.enter_context(nc.psum_tensor("ps_s1", [128, 1024], F32))
            ps_o0 = _es4.enter_context(nc.psum_tensor("ps_o0", [65, 512], F32))
            ps_o1 = _es4.enter_context(nc.psum_tensor("ps_o1", [65, 512], F32))
            ps_e = _es4.enter_context(nc.psum_tensor("ps_e", [64, 512], F32))
            ps_wb = _es4.enter_context(nc.psum_tensor("ps_wb", [128, 2], F32))
            blk3 = _es4.enter_context(nc.Block())
            if True:
                ptile = [pt0, pt1]
                tmpo = [tmpo0, tmpo1]
                ps_s = [ps_s0, ps_s1]
                ps_o = [ps_o0, ps_o1]
                t_qk3, v_msk, s_exp = S_("t_qk3"), S_("v_msk"), S_("s_exp")
                t_av, v_den1, t_ex = S_("t_av"), S_("v_den1"), S_("t_ex")
                v_den2 = S_("v_den2")
                g_crossp = [S_("g_cross0"), S_("g_cross1")]
                wd_ldp = [S_("wd_ld0"), S_("wd_ld1")]
                v_pwd = S_("v_pwd")
                v_wredd, t_w1d, v_w1d = S_("v_wredd"), S_("t_w1d"), S_("v_w1d")

                @blk3.gpsimd
                def _(g):
                    g.wait_ge(S_("v_aug"), 64)
                    g.wait_ge(S_("t_dot"), 64)
                    for u, (b, h, c) in enumerate(units):
                        ch = u
                        if ch < 64:
                            if ch >= 2:
                                g.wait_ge(v_pwd, ch - 1)
                            g.dma_start(out=(pwd0 if ch % 2 == 0 else pwd1)[:],
                                        in_=wdT[ch * 128:(ch + 1) * 128, :]).then_inc(wd_ldp[ch % 2], 16)
                        g.wait_ge(v_den2, u + 1)
                        g.dma_start(out=oT[h // 2][(h % 2) * 64:(h % 2) * 64 + 64,
                                                   b * S + c * 512:b * S + (c + 1) * 512],
                                    in_=tmpo[u % 2][:, :]).then_inc(g_crossp[u % 2], 16)

                @blk3.tensor
                def _(t):
                    t.wait_ge(S_("v_aug"), 64)
                    for u, (b, h, c) in enumerate(units):
                        i = b * 4 + h
                        ng = 2 * c + 2

                        def qk(g2):
                            gg = cum_grp[u] + g2
                            if gg >= 2:
                                t.wait_ge(s_exp, gg - 1)
                            ins = None
                            for j in range(2):
                                kvt = 2 * g2 + j
                                ins = t.matmul(ps_s[gg % 2][:, j * 512:(j + 1) * 512],
                                               k_aug[b][:, kvt * 128:(kvt + 1) * 128],
                                               q_aug[i][:, c * 512:(c + 1) * 512],
                                               start=True, stop=True)
                            ins.then_inc(t_qk3, 1)

                        def av(g2):
                            gg = cum_grp[u] + g2
                            t.wait_ge(s_exp, gg + 1)
                            if g2 == 0 and u >= 2:
                                t.wait_ge(v_den2, u - 1)
                            ins = None
                            for j in range(2):
                                kvt = 2 * g2 + j
                                ins = t.matmul(ps_o[u % 2][:, :],
                                               v_sb[b][:, kvt * 65:(kvt + 1) * 65],
                                               ptile[gg % 2][:, j * 512:(j + 1) * 512],
                                               start=(kvt == 0), stop=(kvt == 2 * ng - 1))
                            ins.then_inc(t_av, 1)

                        for g2 in range(ng):
                            qk(g2)
                            if g2 >= 1:
                                av(g2 - 1)
                        av(ng - 1)
                        t.wait_ge(v_den1, u + 1)
                        t.matmul(ps_e[:, :], ones_r[0:1, 0:64], rden[0:1, :],
                                 start=True, stop=True).then_inc(t_ex, 1)
                    t.wait_ge(v_wredd, 1)
                    t.matmul(ps_wb[0:1, 0:1], wsc[:, 3:4], onescol[:, 0:1], start=True, stop=True).then_inc(t_w1d, 1)
                    t.wait_ge(v_w1d, 1)
                    t.matmul(ps_wb[:, 1:2], onesrow[0:1, 0:128], w1s[0:1, 2:3], start=True, stop=True).then_inc(t_w1d, 1)

                @blk3.vector
                def _(v):
                    v = _Drain(v)
                    wd_ch = 0
                    for u, (b, h, c) in enumerate(units):
                        ng = 2 * c + 2
                        for dg in range(2):
                            g2 = ng - 2 + dg
                            gg = cum_grp[u] + g2
                            v.wait_ge(t_qk3, gg + 1)
                            for j in range(2):
                                kvt = 2 * g2 + j
                                r = kvt - 4 * c
                                v.tensor_add(ps_s[gg % 2][:, j * 512:(j + 1) * 512],
                                             ps_s[gg % 2][:, j * 512:(j + 1) * 512],
                                             masks_sb[:, r * 512:(r + 1) * 512])
                            v.engine_nop().then_inc(v_msk, 1)
                        v.wait_ge(t_av, cum_grp[u + 1])
                        v.reciprocal(rdenf[0:1, :], ps_o[u % 2][64:65, :])
                        v.tensor_copy(rden[0:1, :], rdenf[0:1, :]).then_inc(v_den1, 1)
                        v.wait_ge(t_ex, u + 1)
                        v.tensor_copy(sce[:, :], ps_e[:, :])
                        if u >= 2:
                            v.wait_ge(g_crossp[u % 2], 16 * (u // 2))
                        v.tensor_mul(tmpo[u % 2][:, :], ps_o[u % 2][0:64, :], sce[:, :]).then_inc(v_den2, 1)
                        if wd_ch < 64:
                            v.wait_ge(wd_ldp[wd_ch % 2], 16 * (wd_ch // 2 + 1))
                            v.reduce_sum(wsum[:, 128 + wd_ch:129 + wd_ch],
                                         (pwd0 if wd_ch % 2 == 0 else pwd1)[:], AX.X,
                                         apply_absolute_value=True).then_inc(v_pwd, 1)
                            wd_ch += 1
                    v.reduce_sum(wsc[:, 3:4], wsum[:, 128:192], AX.X).then_inc(v_wredd, 1)
                    v.wait_ge(t_w1d, 1)
                    v.tensor_scalar_mul(w1s[:, 2:3], ps_wb[0:1, 0:1], 1.0 / (D * DI))
                    v.tensor_scalar_max(w1s[:, 2:3], w1s[:, 2:3], 1e-5).then_inc(v_w1d, 1)
                    v.wait_ge(t_w1d, 2)
                    v.tensor_copy(wsc[:, 2:3], ps_wb[:, 1:2])
                    v.reciprocal(swc[:, 2:3], wsc[:, 2:3]).then_inc(v_w1d, 1)

                @blk3.scalar
                def _(s):
                    s = _Drain(s)
                    for u, (b, h, c) in enumerate(units):
                        ng = 2 * c + 2
                        for g2 in range(ng):
                            gg = cum_grp[u] + g2
                            if g2 >= ng - 2:
                                s.wait_ge(v_msk, 2 * u + (g2 - (ng - 2)) + 1)
                            else:
                                s.wait_ge(t_qk3, gg + 1)
                            if gg >= 2:
                                s.wait_ge(t_av, gg - 1)
                            s.activation(ptile[gg % 2][:, :], ps_s[gg % 2][:, :], AF.Exp,
                                         bias=expb[:, 0:1], scale=0.125).then_inc(s_exp, 1)

            _es4.close()
            _es5 = contextlib.ExitStack()
            wo32 = _es5.enter_context(nc.sbuf_tensor("wo32", [128, 512], F32))
            woutb = _es5.enter_context(nc.sbuf_tensor("woutb", [128, 2 * D], BF16))
            rst = _es5.enter_context(nc.sbuf_tensor("rst", [128, 4 * 512], BF16))
            ps_w0 = _es5.enter_context(nc.psum_tensor("ps_w0", [128, 512], F32))
            ps_w1b = _es5.enter_context(nc.psum_tensor("ps_w1b", [128, 512], F32))
            blk4 = _es5.enter_context(nc.Block())
            if True:
                wo_ld, wo_cv, t_wo = S_("wo_ld"), S_("wo_cv"), S_("t_wo")
                v_rs, g_coll = S_("v_rs"), S_("g_coll")
                rs_stp = [S_("rs_st0"), S_("rs_st1"), S_("rs_st2"), S_("rs_st3")]
                ps_w = [ps_w0, ps_w1b]

                @blk4.gpsimd
                def _(g):
                    g.wait_ge(S_("g_cross0"), 16 * (NU // 2))
                    g.wait_ge(S_("g_cross1"), 16 * (NU // 2))
                    for j in range(8):
                        ct, mc = j // 4, j % 4
                        if j >= 1:
                            g.wait_ge(wo_cv, j)
                        g.dma_start(out=wo32[:],
                                    in_=w_outT[ct * 128:(ct + 1) * 128, mc * 512:(mc + 1) * 512]).then_inc(wo_ld, 16)
                    for q4 in range(4):
                        g.wait_ge(rs_stp[q4], 16 * 64)
                    g.collective_compute(
                        "ReduceScatter", ALU.add,
                        replica_groups=[list(range(8))],
                        ins=[rs_in[:, :]],
                        outs=[rs_out[:, :]],
                    ).then_inc(g_coll, 1)

                @blk4.vector
                def _(v):
                    v = _Drain(v)
                    v.wait_ge(S_("g_cross0"), 16 * (NU // 2))
                    v.wait_ge(S_("g_cross1"), 16 * (NU // 2))
                    for j in range(8):
                        v.wait_ge(wo_ld, 16 * (j + 1))
                        ct, mc = j // 4, j % 4
                        v.tensor_copy(woutb[:, ct * D + mc * 512:ct * D + (mc + 1) * 512],
                                      wo32[:]).then_inc(wo_cv, 1)
                    for idx in range(256):
                        v.wait_ge(t_wo, idx + 1)
                        if idx >= 4:
                            v.wait_ge(rs_stp[idx % 4], 16 * (idx // 4))
                        v.tensor_copy(rst[:, (idx % 4) * 512:(idx % 4 + 1) * 512],
                                      ps_w[idx % 2][:, :]).then_inc(v_rs, 1)

                @blk4.tensor
                def _(t):
                    t.wait_ge(wo_cv, 8)
                    t.wait_ge(S_("g_cross0"), 16 * (NU // 2))
                    t.wait_ge(S_("g_cross1"), 16 * (NU // 2))
                    for idx in range(256):
                        tt, mc = idx // 4, idx % 4
                        if idx >= 2:
                            t.wait_ge(v_rs, idx - 1)
                        ins = None
                        for ct in range(2):
                            ins = t.matmul(ps_w[idx % 2][:, :], oT[ct][:, tt * 128:(tt + 1) * 128],
                                           woutb[:, ct * D + mc * 512:ct * D + (mc + 1) * 512],
                                           start=(ct == 0), stop=(ct == 1))
                        ins.then_inc(t_wo, 1)

                @blk4.sync
                def _(sync):
                    for idx in range(256):
                        tt, mc = idx // 4, idx % 4
                        sync.wait_ge(v_rs, idx + 1)
                        sync.dma_start(out=rs_in[tt * 128:(tt + 1) * 128, mc * 512:(mc + 1) * 512],
                                       in_=rst[:, (idx % 4) * 512:(idx % 4 + 1) * 512]).then_inc(rs_stp[idx % 4], 16)

                if debug:
                    @blk4.scalar
                    def _(s):
                        s = _Drain(s)
                        s.wait_ge(S_("g_cross0"), 16 * (NU // 2))
                        s.wait_ge(S_("g_cross1"), 16 * (NU // 2))
                        for i in range(8):
                            s.dma_start(out=tap["qaug"][i * 65:(i + 1) * 65, :], in_=q_aug[i][:, :]).then_inc(S_("tapsem"), 16)
                        for b in range(B):
                            s.dma_start(out=tap["kaug"][b * 65:(b + 1) * 65, :], in_=k_aug[b][:, :]).then_inc(S_("tapsem"), 16)
                            s.dma_start(out=tap["v"][b * 128:(b + 1) * 128, :], in_=v_sb[b][:, :]).then_inc(S_("tapsem"), 16)
                        for p in range(2):
                            s.dma_start(out=tap["oT"][p * 128:(p + 1) * 128, :], in_=oT[p][:, :]).then_inc(S_("tapsem"), 16)
                        s.wait_ge(S_("tapsem"), 16 * 14)

    _es5.close()
    _es3.close()
    _es0.close()
    with nc.sbuf_tensor("hslab", [128, 8 * DI], BF16) as hslab:
        _es6 = contextlib.ExitStack()
        kxT = _es6.enter_context(nc.sbuf_tensor("kxT", [128, 16 * NT_OWN * 128], BF16))
        mxc8 = _es6.enter_context(nc.sbuf_tensor("mxc8", [128, 8], F32))
        gsc8 = _es6.enter_context(nc.sbuf_tensor("gsc8", [128, 8], F32))
        usc8 = _es6.enter_context(nc.sbuf_tensor("usc8", [128, 8], F32))
        if True:
            _es7 = contextlib.ExitStack()
            xo0 = _es7.enter_context(nc.sbuf_tensor("xo0", [128, D], F32))
            xo1 = _es7.enter_context(nc.sbuf_tensor("xo1", [128, D], F32))
            xno0 = _es7.enter_context(nc.sbuf_tensor("xno0", [128, D], BF16))
            xno1 = _es7.enter_context(nc.sbuf_tensor("xno1", [128, D], BF16))
            sq5 = _es7.enter_context(nc.sbuf_tensor("sq5", [128, D], BF16))
            st5 = _es7.enter_context(nc.sbuf_tensor("st5", [128, 24], F32))
            kxf = _es7.enter_context(nc.sbuf_tensor("kxf", [128, D], F32))
            kxb = _es7.enter_context(nc.sbuf_tensor("kxb", [128, D], BF16))
            ps_t50 = _es7.enter_context(nc.psum_tensor("ps_t50", [128, 512], F32))
            ps_t51 = _es7.enter_context(nc.psum_tensor("ps_t51", [128, 512], F32))
            blk5a = _es7.enter_context(nc.Block())
            if True:
                xo = [xo0, xo1]
                xno = [xno0, xno1]
                ps_t5 = [ps_t50, ps_t51]
                xo_ldp = [S_("xo_ld0"), S_("xo_ld1")]
                s_sq5, v_st5, v_red5 = S_("s_sq5"), S_("v_st5"), S_("v_red5")
                v_kx, t_kxT, v_kxT = S_("v_kx"), S_("t_kxT"), S_("v_kxT")
                xp_stp = [S_("xp_st0"), S_("xp_st1")]

                @blk5a.sync
                def _(sync):
                    for q4 in range(4):
                        sync.wait_ge(S_(f"rs_st{q4}"), 16 * 64)
                    def park(j):
                        sync.wait_ge(v_kx, 3 * j + 2)
                        sync.dma_start(out=xn_park[j * 128:(j + 1) * 128, :], in_=xno[j % 2][:]).then_inc(xp_stp[j % 2], 16)
                    for ti in range(NT_OWN):
                        if ti >= 2:
                            sync.wait_ge(s_sq5, 4 * (ti - 2) + 4)
                            sync.wait_ge(v_red5, ti - 1)
                        sync.dma_start(out=xo[ti % 2][:], in_=x_own[ti * 128:(ti + 1) * 128, :]).then_inc(xo_ldp[ti % 2], 16)
                        if ti >= 2:
                            park(ti - 2)
                    park(NT_OWN - 2)
                    park(NT_OWN - 1)

                @blk5a.scalar
                def _(s):
                    s = _Drain(s)
                    for q4 in range(4):
                        s.wait_ge(S_(f"rs_st{q4}"), 16 * 64)
                    for ti in range(NT_OWN):
                        xb = xo[ti % 2]
                        o = 12 * (ti % 2)
                        s.wait_ge(xo_ldp[ti % 2], 16 * (ti // 2 + 1))
                        s.activation(sq5[:], xb[:], AF.Square, accum_out=st5[:, o + 1:o + 2]).then_inc(s_sq5, 1)
                        s.wait_ge(v_st5, 2 * ti + 1)
                        s.activation(st5[:, o + 9:o + 10], st5[:, o + 6:o + 7], AF.Ln).then_inc(s_sq5, 1)
                        s.activation(st5[:, o + 10:o + 11], st5[:, o + 9:o + 10], AF.Exp, scale=-0.5).then_inc(s_sq5, 1)
                        s.wait_ge(v_st5, 2 * ti + 2)
                        if ti >= 2:
                            s.wait_ge(v_kx, 3 * (ti - 2) + 3)
                            s.wait_ge(xp_stp[ti % 2], 16 * (ti // 2))
                        s.activation(xno[ti % 2][:], xb[:], AF.Identity,
                                     bias=st5[:, o + 11:o + 12], scale=st5[:, o + 10:o + 11]).then_inc(s_sq5, 1)

                @blk5a.vector
                def _(v):
                    v = _Drain(v)
                    for q4 in range(4):
                        v.wait_ge(S_(f"rs_st{q4}"), 16 * 64)
                    for ti in range(NT_OWN):
                        xb = xo[ti % 2]
                        o = 12 * (ti % 2)
                        v.wait_ge(xo_ldp[ti % 2], 16 * (ti // 2 + 1))
                        v.reduce_sum(st5[:, o + 0:o + 1], xb[:], AX.X).then_inc(v_red5, 1)
                        v.wait_ge(s_sq5, 4 * ti + 1)
                        v.tensor_scalar_mul(st5[:, o + 2:o + 3], st5[:, o + 0:o + 1], 1.0 / D)
                        v.tensor_scalar_mul(st5[:, o + 3:o + 4], st5[:, o + 1:o + 2], 1.0 / D)
                        v.tensor_scalar(st5[:, o + 4:o + 5], st5[:, o + 2:o + 3], st5[:, o + 2:o + 3], None, ALU.mult)
                        v.tensor_sub(st5[:, o + 5:o + 6], st5[:, o + 3:o + 4], st5[:, o + 4:o + 5])
                        v.tensor_scalar_add(st5[:, o + 6:o + 7], st5[:, o + 5:o + 6], 1e-5).then_inc(v_st5, 1)
                        v.wait_ge(s_sq5, 4 * ti + 3)
                        v.tensor_scalar(st5[:, o + 11:o + 12], st5[:, o + 10:o + 11], st5[:, o + 10:o + 11], None, ALU.mult)
                        v.scalar_tensor_tensor(st5[:, o + 11:o + 12], st5[:, o + 11:o + 12], -0.5,
                                               st5[:, o + 6:o + 7], ALU.mult, ALU.mult)
                        v.tensor_scalar_add(st5[:, o + 11:o + 12], st5[:, o + 11:o + 12], 1.5)
                        v.tensor_mul(st5[:, o + 10:o + 11], st5[:, o + 11:o + 12], st5[:, o + 10:o + 11])
                        v.tensor_scalar(st5[:, o + 11:o + 12], st5[:, o + 2:o + 3],
                                        st5[:, o + 10:o + 11], -1.0, ALU.mult, ALU.mult).then_inc(v_st5, 1)
                        v.wait_ge(s_sq5, 4 * ti + 4)
                        v.reduce_max(st5[:, o + 4:o + 5], xno[ti % 2][:], AX.X, apply_absolute_value=True)
                        v.tensor_scalar_max(st5[:, o + 4:o + 5], st5[:, o + 4:o + 5], 1e-5)
                        v.reciprocal(st5[:, o + 5:o + 6], st5[:, o + 4:o + 5])
                        v.tensor_scalar_mul(st5[:, o + 5:o + 6], st5[:, o + 5:o + 6], 127.0)
                        v.tensor_scalar_mul(mxc8[:, ti:ti + 1], st5[:, o + 4:o + 5], 1.0 / 127.0).then_inc(v_kx, 1)
                        if ti >= 1:
                            v.wait_ge(t_kxT, 4 * ti)
                        v.tensor_scalar(kxf[:], xno[ti % 2][:], st5[:, o + 5:o + 6], 12582912.0,
                                        ALU.mult, ALU.add).then_inc(v_kx, 1)
                        v.tensor_scalar_add(kxb[:], kxf[:], -12582912.0).then_inc(v_kx, 1)
                        for r in range(4):
                            g4 = 4 * ti + r
                            v.wait_ge(t_kxT, g4 + 1)
                            pst = ps_t5[g4 % 2]
                            for j in range(4):
                                hi = r * 4 + j
                                v.tensor_copy(kxT[:, (hi * NT_OWN + ti) * 128:(hi * NT_OWN + ti + 1) * 128],
                                              pst[:, j * 128:(j + 1) * 128])
                            v.engine_nop().then_inc(v_kxT, 1)
                    v.wait_ge(S_("v_w1"), 2)
                    v.tensor_scalar(gsc8[:], mxc8[:], wsc[:, 0:1], None, ALU.mult)
                    v.tensor_scalar(usc8[:], mxc8[:], wsc[:, 1:2], None, ALU.mult).then_inc(v_kx, 1)

                @blk5a.tensor
                def _(t):
                    for q4 in range(4):
                        t.wait_ge(S_(f"rs_st{q4}"), 16 * 64)
                    for ti in range(NT_OWN):
                        t.wait_ge(v_kx, 3 * ti + 3)
                        for r in range(4):
                            g4 = 4 * ti + r
                            pst = ps_t5[g4 % 2]
                            if g4 >= 2:
                                t.wait_ge(v_kxT, g4 - 1)
                            ins = None
                            for j in range(4):
                                hi = r * 4 + j
                                ins = t.matmul(pst[:, j * 128:(j + 1) * 128],
                                               kxb[:, hi * 128:(hi + 1) * 128], identb[:],
                                               start=True, stop=True)
                            ins.then_inc(t_kxT, 1)

            _es7.close()
            _es8 = contextlib.ExitStack()
            wst0 = _es8.enter_context(nc.sbuf_tensor("wst0", [128, 512], F32))
            wst1 = _es8.enter_context(nc.sbuf_tensor("wst1", [128, 512], F32))
            wst2 = _es8.enter_context(nc.sbuf_tensor("wst2", [128, 512], F32))
            wst3 = _es8.enter_context(nc.sbuf_tensor("wst3", [128, 512], F32))
            wq32x0 = _es8.enter_context(nc.sbuf_tensor("wq32x0", [128, 512], F32))
            wq32x1 = _es8.enter_context(nc.sbuf_tensor("wq32x1", [128, 512], F32))
            wq32x2 = _es8.enter_context(nc.sbuf_tensor("wq32x2", [128, 512], F32))
            wq32x3 = _es8.enter_context(nc.sbuf_tensor("wq32x3", [128, 512], F32))
            wqi8a0 = _es8.enter_context(nc.sbuf_tensor("wqi8a", [128, 512], I8))
            wqi8a1 = _es8.enter_context(nc.sbuf_tensor("wqi8b", [128, 512], I8))
            wqb0 = _es8.enter_context(nc.sbuf_tensor("wqb0", [128, 512], BF16))
            wqb1 = _es8.enter_context(nc.sbuf_tensor("wqb1", [128, 512], BF16))
            wqb2 = _es8.enter_context(nc.sbuf_tensor("wqb2", [128, 512], BF16))
            wqb3 = _es8.enter_context(nc.sbuf_tensor("wqb3", [128, 512], BF16))
            hg = _es8.enter_context(nc.sbuf_tensor("hg", [128, 8 * 512], BF16))
            hu = _es8.enter_context(nc.sbuf_tensor("hu", [128, 8 * 512], BF16))
            ps_g = _es8.enter_context(nc.psum_tensor("ps_g", [128, 4 * 512], F32))
            ps_u = _es8.enter_context(nc.psum_tensor("ps_u", [128, 4 * 512], F32))
            blk5b = _es8.enter_context(nc.Block())
            if True:
                wst = [wst0, wst1, wst2, wst3]
                wq32x = [wq32x0, wq32x1, wq32x2, wq32x3]
                wqi8 = [wqi8a0, wqi8a1]
                wqb = [wqb0, wqb1, wqb2, wqb3]
                def g2glob(kg):
                    return (kg // 16) * 32 + (kg % 16)
                def u2glob(ku):
                    return (ku // 16) * 32 + 16 + (ku % 16)
                wgu_ldp = [S_("wgu_ld0"), S_("wgu_ld1")]
                wgu_up = [S_("wgu_up0"), S_("wgu_up1")]
                v_wq, s_wq = S_("v_wq"), S_("s_wq")
                t_gu, s_hg, s_hu, v_h = S_("t_gu"), S_("s_hg"), S_("s_hu"), S_("v_h")

                def s2glob(kq, wsel):
                    return (kq // 16) * 32 + wsel * 16 + (kq % 16)

                def bank(tt):
                    return (ps_g if tt < 4 else ps_u)[:, (tt % 4) * 512:(tt % 4 + 1) * 512]

                @blk5b.gpsimd
                def _(g):
                    g.wait_ge(S_("xp_st0"), 16 * (NT_OWN // 2))
                    g.wait_ge(S_("xp_st1"), 16 * (NT_OWN // 2))
                    g.wait_ge(S_("t_kxT"), 4 * NT_OWN)
                    g.wait_ge(S_("v_kxT"), 4 * NT_OWN)
                    if debug:
                        g.dma_start(out=tap["kx"][:, :], in_=kxT[:, :]).then_inc(S_("tapsem4"), 16)
                    kg = 0
                    for ic in range(16):
                        for hi in range(16):
                            if kg >= 2:
                                g.wait_ge(s_wq, s2glob(kg - 2, 0) + 1)
                            g.dma_start(out=wst[2 * (kg % 2)][:],
                                        in_=wgT[hi * 128:(hi + 1) * 128,
                                                ic * 512:(ic + 1) * 512]).then_inc(wgu_ldp[kg % 2], 16)
                            kg += 1

                @blk5b.sync
                def _(sync):
                    for q4 in range(4):
                        sync.wait_ge(S_(f"rs_st{q4}"), 16 * 64)
                    sync.wait_ge(S_("t_kxT"), 4 * NT_OWN)
                    ku = 0
                    for ic in range(16):
                        for hi in range(16):
                            if ku >= 2:
                                sync.wait_ge(s_wq, s2glob(ku - 2, 1) + 1)
                            sync.dma_start(out=wst[2 * (ku % 2) + 1][:],
                                           in_=wuT[hi * 128:(hi + 1) * 128,
                                                   ic * 512:(ic + 1) * 512]).then_inc(wgu_up[ku % 2], 16)
                            ku += 1

                @blk5b.vector
                def _(v):
                    v = _Drain(v)
                    v.wait_ge(S_("xp_st0"), 16 * (NT_OWN // 2))
                    v.wait_ge(S_("xp_st1"), 16 * (NT_OWN // 2))
                    v.wait_ge(S_("v_kx"), 3 * NT_OWN + 1)
                    k = 0
                    for ic in range(16):
                        if ic >= 1:
                            v.wait_ge(s_hu, 8 * ic)
                            for tt in range(8):
                                v.tensor_mul(hslab[:, tt * DI + (ic - 1) * 512:tt * DI + ic * 512],
                                             hg[:, tt * 512:(tt + 1) * 512], hu[:, tt * 512:(tt + 1) * 512])
                            v.engine_nop().then_inc(v_h, 1)
                        for wsel in range(2):
                            for hi in range(16):
                                v.wait_ge(s_wq, k + 1)
                                if k >= 4:
                                    v.wait_ge(t_gu, k - 3)
                                v.tensor_scalar(wqb[k % 4][:], wq32x[k % 4][:], 1.0, -1.0,
                                                ALU.min, ALU.max).then_inc(v_wq, 1)
                                k += 1
                    v.wait_ge(s_hu, 8 * 16)
                    for tt in range(8):
                        v.tensor_mul(hslab[:, tt * DI + 15 * 512:tt * DI + 16 * 512],
                                     hg[:, tt * 512:(tt + 1) * 512], hu[:, tt * 512:(tt + 1) * 512])
                    v.engine_nop().then_inc(v_h, 1)

                @blk5b.scalar
                def _(s):
                    s = _Drain(s)
                    s.wait_ge(S_("xp_st0"), 16 * (NT_OWN // 2))
                    s.wait_ge(S_("xp_st1"), 16 * (NT_OWN // 2))
                    k = 0
                    for ic in range(16):
                        for wsel in range(2):
                            for hi in range(16):
                                kq = ic * 16 + hi
                                stg = wst[2 * (kq % 2) + wsel]
                                sem = wgu_ldp if wsel == 0 else wgu_up
                                s.wait_ge(sem[kq % 2], 16 * (kq // 2 + 1))
                                if k >= 4:
                                    s.wait_ge(v_wq, k - 3)
                                s.activation(wq32x[k % 4][:], stg[:], AF.Identity,
                                             bias=mcol[:, 0:1], scale=swc[:, wsel:wsel + 1])
                                s.activation(wq32x[k % 4][:], wq32x[k % 4][:], AF.Identity,
                                             bias=nmcol[:, 0:1]).then_inc(s_wq, 1)
                                k += 1
                            if wsel == 0:
                                s.wait_ge(t_gu, ic * 32 + 16)
                                s.wait_ge(v_h, ic)
                                if debug and ic == 1:
                                    s.wait_ge(S_("tapsem5"), 32)
                                for tt in range(8):
                                    s.activation(hg[:, tt * 512:(tt + 1) * 512], bank(tt),
                                                 AF.Silu, scale=gsc8[:, tt:tt + 1]).then_inc(s_hg, 1)
                        s.wait_ge(t_gu, ic * 32 + 32)
                        for tt in range(8):
                            s.activation(hu[:, tt * 512:(tt + 1) * 512], bank(tt),
                                         AF.Identity, scale=usc8[:, tt:tt + 1]).then_inc(s_hu, 1)
                        if debug and ic == 0:
                            s.dma_start(out=tap["hg"][:, :], in_=hg[:, 0:2048]).then_inc(S_("tapsem5"), 16)
                            s.dma_start(out=tap["hu"][:, :], in_=hu[:, 0:2048]).then_inc(S_("tapsem5"), 16)

                @blk5b.tensor
                def _(t):
                    k = 0
                    for ic in range(16):
                        for wsel in range(2):
                            for hi in range(16):
                                t.wait_ge(v_wq, k + 1)
                                ins = None
                                for tt in range(8):
                                    if hi == 0:
                                        if wsel == 0:
                                            if ic >= 1:
                                                t.wait_ge(s_hu, 8 * (ic - 1) + tt + 1)
                                        else:
                                            t.wait_ge(s_hg, 8 * ic + tt + 1)
                                    ins = t.matmul(bank(tt),
                                                   kxT[:, (hi * NT_OWN + tt) * 128:(hi * NT_OWN + tt + 1) * 128],
                                                   wqb[k % 4][:],
                                                   start=(hi == 0), stop=(hi == 15))
                                ins.then_inc(t_gu, 1)
                                k += 1

        _es8.close()
        _es6.close()
        _es9 = contextlib.ExitStack()
        lnwT = _es9.enter_context(nc.sbuf_tensor("lnwT", [128, DI], BF16))
        lnw32 = _es9.enter_context(nc.sbuf_tensor("lnw32", [1, 512], F32))
        sq8k = _es9.enter_context(nc.sbuf_tensor("sq8k", [128, DI], BF16))
        hsq8 = _es9.enter_context(nc.sbuf_tensor("hsq8", [128, 8], F32))
        mxh8 = _es9.enter_context(nc.sbuf_tensor("mxh8", [128, 8], F32))
        rs8 = _es9.enter_context(nc.sbuf_tensor("rs8", [128, 8], F32))
        f28 = _es9.enter_context(nc.sbuf_tensor("f28", [128, 8], F32))
        dscr = _es9.enter_context(nc.sbuf_tensor("dscr", [128, 8], F32))
        sqout = _es9.enter_context(nc.sbuf_tensor("sqout", [128, DI], BF16))
        k2f = _es9.enter_context(nc.sbuf_tensor("k2f", [128, DI // 2], F32))
        ktr0 = _es9.enter_context(nc.sbuf_tensor("ktr", [128, 512], BF16))
        ktr1 = _es9.enter_context(nc.sbuf_tensor("ktr1", [128, 512], BF16))
        ps_c0 = _es9.enter_context(nc.psum_tensor("ps_c0", [128, 512], F32))
        ps_c1 = _es9.enter_context(nc.psum_tensor("ps_c1", [128, 512], F32))
        blk5c = _es9.enter_context(nc.Block())
        if True:
            ktr = [ktr0, ktr1]
            ps_c = [ps_c0, ps_c1]
            lnw_ld, t_lnw, v_lnw = S_("lnw_ld"), S_("t_lnw"), S_("v_lnw")
            s_hsq, v_hlw, v_col5, s_ln5 = S_("s_hsq"), S_("v_hlw"), S_("v_col5"), S_("s_ln5")
            v_k2, s_k2 = S_("v_k2"), S_("s_k2")
            t_k2t, v_k2t = S_("t_k2t"), S_("v_k2t")
            k2t_stp = [S_("k2t_st0"), S_("k2t_st1")]

            @blk5c.gpsimd
            def _(g):
                g.wait_ge(S_("t_gu"), 512)
                g.wait_ge(S_("v_h"), 16)
                if debug:
                    g.wait_ge(S_("tapsem4"), 16)
                for cidx in range(16):
                    if cidx >= 1:
                        g.wait_ge(t_lnw, cidx)
                    g.dma_start(out=lnw32[:], in_=lnw_in[cidx:cidx + 1, :]).then_inc(lnw_ld, 16)

            @blk5c.tensor
            def _(t):
                for cidx in range(16):
                    t.wait_ge(lnw_ld, 16 * (cidx + 1))
                    if cidx >= 2:
                        t.wait_ge(v_lnw, cidx - 1)
                    t.matmul(ps_c[cidx % 2][:, :], onesrow[0:1, 0:128], lnw32[0:1, :],
                             start=True, stop=True).then_inc(t_lnw, 1)
                t.wait_ge(v_lnw, 16)
                for tt in range(NT_OWN):
                    t.wait_ge(v_k2, tt + 1)
                    for r in range(16):
                        g4 = tt * 16 + r
                        if g4 >= 2:
                            t.wait_ge(v_k2t, g4 - 1)
                        ins = None
                        for j in range(4):
                            ii = r * 4 + j
                            ins = t.matmul(ps_c[g4 % 2][:, j * 128:(j + 1) * 128],
                                           hslab[:, tt * DI + ii * 128:tt * DI + (ii + 1) * 128],
                                           identb[:], start=True, stop=True)
                        ins.then_inc(t_k2t, 1)

            @blk5c.vector
            def _(v):
                v = _Drain(v)
                v.wait_ge(S_("v_h"), 16)
                for cidx in range(16):
                    v.wait_ge(t_lnw, cidx + 1)
                    v.tensor_copy(lnwT[:, cidx * 512:(cidx + 1) * 512], ps_c[cidx % 2][:, :]).then_inc(v_lnw, 1)
                for tt in range(NT_OWN):
                    v.tensor_mul(sq8k[:], hslab[:, tt * DI:(tt + 1) * DI], lnwT[:])
                    v.reduce_max(mxh8[:, tt:tt + 1], sq8k[:], AX.X, apply_absolute_value=True).then_inc(v_hlw, 1)
                v.wait_ge(S_("s_hsq"), 8)
                v.tensor_scalar(rs8[:], hsq8[:], 1.0 / DI, 1e-6, ALU.mult, ALU.add).then_inc(v_col5, 1)
                v.wait_ge(s_ln5, 2)
                v.wait_ge(S_("v_w1d"), 2)
                v.tensor_mul(dscr[:], f28[:], f28[:])
                v.tensor_tensor(dscr[:], dscr[:], rs8[:], ALU.mult)
                v.tensor_scalar(dscr[:], dscr[:], -0.5, 1.5, ALU.mult, ALU.add)
                v.tensor_mul(rs8[:], dscr[:], f28[:])
                v.tensor_mul(f28[:], rs8[:], mxh8[:])
                v.tensor_scalar_max(f28[:], f28[:], 1e-5)
                v.tensor_scalar(dsc8p[:], f28[:], wsc[:, 2:3], 1.0 / 127.0, ALU.mult, ALU.mult)
                v.tensor_scalar_mul(mxh8[:], rs8[:], 127.0)
                v.reciprocal(dscr[:], f28[:])
                v.tensor_mul(f28[:], mxh8[:], dscr[:]).then_inc(v_col5, 1)
                for tt in range(NT_OWN):
                    if debug and tt == 0:
                        v.wait_ge(S_("tapsem3"), 16)
                    v.tensor_mul(sq8k[:], hslab[:, tt * DI:(tt + 1) * DI], lnwT[:])
                    for hf in range(2):
                        v.tensor_scalar(k2f[:], sq8k[:, hf * (DI // 2):(hf + 1) * (DI // 2)],
                                        f28[:, tt:tt + 1], 12582912.0, ALU.mult, ALU.add)
                        ins = v.tensor_scalar_add(hslab[:, tt * DI + hf * (DI // 2):tt * DI + (hf + 1) * (DI // 2)],
                                                  k2f[:], -12582912.0)
                    ins.then_inc(v_k2, 1)
                for tt in range(NT_OWN):
                    for r in range(16):
                        g4 = tt * 16 + r
                        v.wait_ge(t_k2t, g4 + 1)
                        if g4 >= 2:
                            v.wait_ge(k2t_stp[g4 % 2], 16 * (g4 // 2))
                        v.tensor_copy(ktr[g4 % 2][:], ps_c[g4 % 2][:, :]).then_inc(v_k2t, 1)

            @blk5c.scalar
            def _(s):
                s = _Drain(s)
                for tt in range(NT_OWN):
                    s.wait_ge(S_("v_h"), 16)
                    s.activation(sqout[:], hslab[:, tt * DI:(tt + 1) * DI], AF.Square,
                                 accum_out=hsq8[:, tt:tt + 1]).then_inc(s_hsq, 1)
                s.wait_ge(v_col5, 1)
                s.activation(f28[:], rs8[:], AF.Ln).then_inc(s_ln5, 1)
                s.activation(f28[:], f28[:], AF.Exp, scale=-0.5).then_inc(s_ln5, 1)
                s.wait_ge(S_("tapsem3"), 16) if debug else None

            @blk5c.sync
            def _(sync):
                if debug:
                    tps = S_("tapsem3")
                    sync.wait_ge(S_("s_hsq"), 8)
                    sync.dma_start(out=tap["h"][:, :], in_=hslab[:, 0:DI]).then_inc(tps, 16)
                for tt in range(NT_OWN):
                    for r in range(16):
                        g4 = tt * 16 + r
                        sync.wait_ge(v_k2t, g4 + 1)
                        dst = k2t_d[(r * 4) * 128:(r * 4 + 4) * 128,
                                    tt * 128:(tt + 1) * 128].rearrange("(n p) c -> p n c", p=128)
                        srcv = ktr[g4 % 2][:].rearrange("p (n c) -> p n c", n=4)
                        sync.dma_start(out=dst, in_=srcv).then_inc(k2t_stp[g4 % 2], 16)

        _es9.close()
    gslab = sb("gslab", [128, 8 * D], BF16)
    _es10 = contextlib.ExitStack()
    wst5a = _es10.enter_context(nc.sbuf_tensor("wst5a", [128, 512], F32))
    wst5b = _es10.enter_context(nc.sbuf_tensor("wst5b", [128, 512], F32))
    wq32dx0 = _es10.enter_context(nc.sbuf_tensor("wq32dx0", [128, 512], F32))
    wq32dx1 = _es10.enter_context(nc.sbuf_tensor("wq32dx1", [128, 512], F32))
    wqi8d0 = _es10.enter_context(nc.sbuf_tensor("wqi8d0", [128, 512], I8))
    wqi8d1 = _es10.enter_context(nc.sbuf_tensor("wqi8d1", [128, 512], I8))
    wqbd0 = _es10.enter_context(nc.sbuf_tensor("wqbd0", [128, 512], BF16))
    wqbd1 = _es10.enter_context(nc.sbuf_tensor("wqbd1", [128, 512], BF16))
    k2c0 = _es10.enter_context(nc.sbuf_tensor("k2c0", [128, 1024], BF16))
    k2c1 = _es10.enter_context(nc.sbuf_tensor("k2c1", [128, 1024], BF16))
    ps_dn0 = _es10.enter_context(nc.psum_tensor("ps_dn0", [128, 512], F32))
    ps_dn1 = _es10.enter_context(nc.psum_tensor("ps_dn1", [128, 512], F32))
    ps_dn2 = _es10.enter_context(nc.psum_tensor("ps_dn2", [128, 512], F32))
    ps_dn3 = _es10.enter_context(nc.psum_tensor("ps_dn3", [128, 512], F32))
    ps_dn4 = _es10.enter_context(nc.psum_tensor("ps_dn4", [128, 512], F32))
    ps_dn5 = _es10.enter_context(nc.psum_tensor("ps_dn5", [128, 512], F32))
    ps_dn6 = _es10.enter_context(nc.psum_tensor("ps_dn6", [128, 512], F32))
    ps_dn7 = _es10.enter_context(nc.psum_tensor("ps_dn7", [128, 512], F32))
    blk5e = _es10.enter_context(nc.Block())
    if True:
        wst5 = [wst5a, wst5b]
        wq32dx = [wq32dx0, wq32dx1]
        wqi8d = [wqi8d0, wqi8d1]
        wqbd = [wqbd0, wqbd1]
        k2c = [k2c0, k2c1]
        ps_dn = [ps_dn0, ps_dn1, ps_dn2, ps_dn3, ps_dn4, ps_dn5, ps_dn6, ps_dn7]
        wd2_ldp = [S_("wd2_ld0"), S_("wd2_ld1")]
        k2_ldp = [S_("k2_ld0"), S_("k2_ld1")]
        v_wq2, s_wq2 = S_("v_wq2"), S_("s_wq2")
        t_dn, s_g = S_("t_dn"), S_("s_g")

        @blk5e.gpsimd
        def _(g):
            g.wait_ge(S_("k2t_st0"), 16 * 64)
            g.wait_ge(S_("k2t_st1"), 16 * 64)
            k = 0
            for mc in range(4):
                for ii in range(64):
                    if k >= 2:
                        g.wait_ge(s_wq2, k - 1)
                    g.dma_start(out=wst5[k % 2][:],
                                in_=wdT[ii * 128:(ii + 1) * 128, mc * 512:(mc + 1) * 512]).then_inc(wd2_ldp[k % 2], 16)
                    k += 1

        @blk5e.sync
        def _(sync):
            sync.wait_ge(S_("k2t_st0"), 16 * 64)
            sync.wait_ge(S_("k2t_st1"), 16 * 64)
            k = 0
            for mc in range(4):
                for ii in range(64):
                    if k >= 2:
                        sync.wait_ge(t_dn, 8 * (k - 1))
                    sync.dma_start(out=k2c[k % 2][:], in_=k2t_d[ii * 128:(ii + 1) * 128, :]).then_inc(k2_ldp[k % 2], 16)
                    k += 1

        @blk5e.vector
        def _(v):
            v = _Drain(v)
            k = 0
            for mc in range(4):
                for ii in range(64):
                    v.wait_ge(s_wq2, k + 1)
                    if k >= 2:
                        v.wait_ge(t_dn, 8 * (k - 1))
                    v.tensor_scalar(wqbd[k % 2][:], wq32dx[k % 2][:], 1.0, -1.0,
                                    ALU.min, ALU.max).then_inc(v_wq2, 2)
                    k += 1

        @blk5e.scalar
        def _(s):
            s = _Drain(s)
            k = 0
            for mc in range(4):
                for ii in range(64):
                    s.wait_ge(wd2_ldp[k % 2], 16 * (k // 2 + 1))
                    if k >= 2:
                        s.wait_ge(v_wq2, 2 * (k - 1))
                    s.activation(wq32dx[k % 2][:], wst5[k % 2][:], AF.Identity,
                                 bias=mcol[:, 0:1], scale=swc[:, 2:3])
                    s.activation(wq32dx[k % 2][:], wq32dx[k % 2][:], AF.Identity,
                                 bias=nmcol[:, 0:1]).then_inc(s_wq2, 1)
                    k += 1
                if mc == 0:
                    s.wait_ge(S_("v_col5"), 2)
                s.wait_ge(t_dn, 8 * 64 * (mc + 1))
                ins = None
                for tt in range(8):
                    ins = s.activation(gslab[:, tt * D + mc * 512:tt * D + (mc + 1) * 512],
                                       ps_dn[tt][:, :], AF.Silu, scale=dsc8p[:, tt:tt + 1])
                ins.then_inc(s_g, 1)

        @blk5e.tensor
        def _(t):
            for mc in range(4):
                if mc >= 1:
                    t.wait_ge(s_g, mc)
                for ii in range(64):
                    k = mc * 64 + ii
                    t.wait_ge(v_wq2, 2 * (k + 1))
                    t.wait_ge(k2_ldp[k % 2], 16 * (k // 2 + 1))
                    ins = None
                    for tt in range(8):
                        ins = t.matmul(ps_dn[tt][:, :],
                                       k2c[k % 2][:, tt * 128:(tt + 1) * 128],
                                       wqbd[k % 2][:],
                                       start=(ii == 0), stop=(ii == 63))
                    ins.then_inc(t_dn, 8)

    _es10.close()
    _es11 = contextlib.ExitStack()
    rso = _es11.enter_context(nc.sbuf_tensor("rso", [128, D], BF16))
    xnp = _es11.enter_context(nc.sbuf_tensor("xnp", [128, D], BF16))
    xob = _es11.enter_context(nc.sbuf_tensor("xob", [128, D], F32))
    srow = _es11.enter_context(nc.sbuf_tensor("srow", [128, D], F32))
    sgrow = _es11.enter_context(nc.sbuf_tensor("sgrow", [128, D], F32))
    outb0 = _es11.enter_context(nc.sbuf_tensor("outb0", [128, D], F32))
    outb1 = _es11.enter_context(nc.sbuf_tensor("outb1", [128, D], F32))
    blk6 = _es11.enter_context(nc.Block())
    if True:
        outb = [outb0, outb1]
        rso_ld, v_out = S_("rso_ld"), S_("v_out")
        out_stp = [S_("out_st0"), S_("out_st1")]

        @blk6.sync
        def _(sync):
            sync.wait_ge(S_("g_coll"), 1)
            sync.wait_ge(S_("s_g"), 4)
            for tt in range(NT_OWN):
                if tt >= 1:
                    sync.wait_ge(v_out, 3 * tt)
                sync.dma_start(out=rso[:], in_=rs_out[tt * 128:(tt + 1) * 128, :]).then_inc(rso_ld, 16)
                sync.dma_start(out=xnp[:], in_=xn_park[tt * 128:(tt + 1) * 128, :]).then_inc(rso_ld, 16)
                sync.dma_start(out=xob[:], in_=x_own[tt * 128:(tt + 1) * 128, :]).then_inc(rso_ld, 16)
                sync.wait_ge(v_out, 3 * tt + 3)
                sync.dma_start(out=out[tt * 128:(tt + 1) * 128, :], in_=outb[tt % 2][:]).then_inc(out_stp[tt % 2], 16)
            if debug:
                tapsem = S_("tapsem2")
                sync.dma_start(out=tap["xn"][:, :], in_=xn_park[:, :]).then_inc(tapsem, 16)
                sync.dma_start(out=tap["wsc"][:, 0:3], in_=wsc[:, 0:3]).then_inc(tapsem, 16)
                sync.dma_start(out=tap["wsc"][:, 4:7], in_=swc[:, 0:3]).then_inc(tapsem, 16)
                sync.dma_start(out=tap["rs"][:, :], in_=rs_out[:, :]).then_inc(tapsem, 16)
                for tt in range(NT_OWN):
                    sync.dma_start(out=tap["g"][tt * 128:(tt + 1) * 128, :],
                                   in_=gslab[:, tt * D:(tt + 1) * D]).then_inc(tapsem, 16)
                sync.wait_ge(tapsem, 16 * 12)
            sync.wait_ge(out_stp[0], 16 * (NT_OWN // 2))
            sync.wait_ge(out_stp[1], 16 * (NT_OWN // 2))

        @blk6.vector
        def _(v):
            v = _Drain(v)
            v.wait_ge(S_("s_g"), 4)
            for tt in range(NT_OWN):
                v.wait_ge(rso_ld, 16 * (3 * tt + 3))
                v.tensor_add(srow[:], rso[:], xnp[:]).then_inc(v_out, 1)
                v.tensor_mul(sgrow[:], srow[:], gslab[:, tt * D:(tt + 1) * D]).then_inc(v_out, 1)
                if tt >= 2:
                    v.wait_ge(out_stp[tt % 2], 16 * (tt // 2))
                v.tensor_add(outb[tt % 2][:], sgrow[:], xob[:]).then_inc(v_out, 1)

    _es11.close()
    return nc, tap


def _host_constants():
    d = 64
    inv = 1.0 / (10000.0 ** (np.arange(0, d, 2, dtype=np.float64) / d))
    pos = np.arange(S, dtype=np.float64)
    ang = pos[:, None] * inv[None, :]
    cos = np.cos(ang).astype(np.float32)
    sin = np.sin(ang).astype(np.float32)
    tc = np.zeros((128, 32 * NTB), np.float32)
    ts_ = np.zeros((128, 32 * NTB), np.float32)
    for pt in range(NTB):
        tc[:, pt * 32:(pt + 1) * 32] = cos[pt * 128:(pt + 1) * 128, :]
        ts_[:, pt * 32:(pt + 1) * 32] = sin[pt * 128:(pt + 1) * 128, :]
    mk = np.zeros((128, 4 * 512), np.float32)
    for r in range(4):
        p = np.arange(128)[:, None]
        f = np.arange(512)[None, :]
        mk[:, r * 512:(r + 1) * 512] = np.where(r * 128 + p > f, -1.0e30, 0.0)
    ident = np.eye(128, dtype=np.float32)
    invc = (1.0 / (np.arange(S, dtype=np.float64) + 1.0)).astype(ml_dtypes.bfloat16).reshape(1, S)
    return tc, ts_, mk, ident, invc


def kernel(L0_attn_x, L0_attn_w_in, L0_attn_w_out, L0_attn_rms_w,
           L0_ffn_w_gate, L0_ffn_w_up, L0_ffn_w_down, L0_ffn_ln_weight):
    x = np.ascontiguousarray(np.asarray(L0_attn_x, dtype=np.float32).reshape(T_ALL, D))
    w_in = np.asarray(L0_attn_w_in, dtype=np.float32)
    w_out = np.asarray(L0_attn_w_out, dtype=np.float32)
    rms_w = np.asarray(L0_attn_rms_w, dtype=np.float32)
    wg = np.asarray(L0_ffn_w_gate, dtype=np.float32)
    wu = np.asarray(L0_ffn_w_up, dtype=np.float32)
    wd = np.asarray(L0_ffn_w_down, dtype=np.float32)
    lnw = np.asarray(L0_ffn_ln_weight, dtype=np.float32)

    tc, ts_, mk, ident, invc = _host_constants()
    wgT = np.ascontiguousarray(wg.T)
    wuT = np.ascontiguousarray(wu.T)
    wdT = np.ascontiguousarray(wd.T)
    rms_colh = np.ascontiguousarray(rms_w.reshape(16, 128).T)
    lnw_h = np.ascontiguousarray(lnw.reshape(16, 512))

    nc, _ = build_nc(debug=False)
    in_maps = []
    for g in range(8):
        rows = np.concatenate([w_in[g * 256:(g + 1) * 256],
                               w_in[2048 + g * 64:2048 + (g + 1) * 64],
                               w_in[2560 + g * 64:2560 + (g + 1) * 64]], axis=0)
        w_inT_g = np.ascontiguousarray(rows.T)
        w_outT_g = np.ascontiguousarray(w_out[:, g * 256:(g + 1) * 256].T)
        in_maps.append({
            "x_all": x,
            "x_own": np.ascontiguousarray(x[g * T_OWN:(g + 1) * T_OWN]),
            "w_inT": w_inT_g,
            "w_outT": w_outT_g,
            "rms_col": rms_colh,
            "wgT": wgT, "wuT": wuT, "wdT": wdT,
            "lnw_in": lnw_h,
            "trig_cos": tc, "trig_sin": ts_,
            "masks": mk, "ident_i": ident, "invcnt": invc,
        })
    res = run_bass_kernel_spmd(nc, in_maps, core_ids=list(range(8)))
    outs = [res.results[g]["out"] for g in range(8)]
    return np.concatenate(outs, axis=0).reshape(B, S, D).astype(np.float32)


# version 06
# speedup vs baseline: 1.2654x; 1.0578x over previous
import numpy as np
import ml_dtypes
import contextlib
import concourse.bass as bass
import concourse.mybir as mybir
from concourse.bass_utils import run_bass_kernel_spmd

F32 = mybir.dt.float32
BF16 = mybir.dt.bfloat16
I8 = mybir.dt.int8
AF = mybir.ActivationFunctionType
ALU = mybir.AluOpType
AX = mybir.AxisListType

B = 2
S = 4096
D = 2048
NTB = S // 128
NT_ALL = B * NTB
T_ALL = B * S
T_OWN = 1024
NT_OWN = 8
CC = 384
DI = 8192




_DRAIN_OPS = {
    "tensor_copy", "tensor_tensor", "tensor_add", "tensor_sub", "tensor_mul",
    "tensor_max", "tensor_scalar", "tensor_single_scalar", "tensor_scalar_add",
    "tensor_scalar_sub", "tensor_scalar_mul", "tensor_scalar_max", "tensor_scalar_min",
    "tensor_relu", "tensor_reduce", "reduce_max", "reduce_sum", "scalar_tensor_tensor",
    "tensor_tensor_scan", "memset", "activation", "copy", "mul", "add", "square",
    "sqrt", "sign", "reciprocal", "select", "bn_stats", "bn_aggr", "transpose",
    "stream_shuffle", "pool", "pool_max", "pool_avg", "copy_predicated", "memzero",
}


class _Drain:

    def __init__(self, eng):
        self._e = eng

    def __getattr__(self, n):
        f = getattr(self._e, n)
        if n in _DRAIN_OPS:
            def wrapped(*a, **kw):
                self._e.drain()
                return f(*a, **kw)
            return wrapped
        return f


def build_nc(debug=False):
    nc = bass.Bass("TRN2", target_bir_lowering=False, debug=False)

    x_all = nc.dram_tensor("x_all", [T_ALL, D], F32, kind="ExternalInput").ap()
    x_own = nc.dram_tensor("x_own", [T_OWN, D], F32, kind="ExternalInput").ap()
    w_inT = nc.dram_tensor("w_inT", [D, CC], F32, kind="ExternalInput").ap()
    w_outT = nc.dram_tensor("w_outT", [256, D], F32, kind="ExternalInput").ap()
    rms_col = nc.dram_tensor("rms_col", [128, 16], F32, kind="ExternalInput").ap()
    wgT = nc.dram_tensor("wgT", [D, DI], F32, kind="ExternalInput").ap()
    wuT = nc.dram_tensor("wuT", [D, DI], F32, kind="ExternalInput").ap()
    wdT = nc.dram_tensor("wdT", [DI, D], F32, kind="ExternalInput").ap()
    lnw_in = nc.dram_tensor("lnw_in", [16, 512], F32, kind="ExternalInput").ap()
    trig_cos = nc.dram_tensor("trig_cos", [128, 32 * NTB], F32, kind="ExternalInput").ap()
    trig_sin = nc.dram_tensor("trig_sin", [128, 32 * NTB], F32, kind="ExternalInput").ap()
    masks = nc.dram_tensor("masks", [128, 4 * 512], F32, kind="ExternalInput").ap()
    ident_i = nc.dram_tensor("ident_i", [128, 128], F32, kind="ExternalInput").ap()
    invcnt = nc.dram_tensor("invcnt", [1, S], BF16, kind="ExternalInput").ap()

    out = nc.dram_tensor("out", [T_OWN, D], F32, kind="ExternalOutput").ap()

    rs_in = nc.dram_tensor("rs_in", [T_ALL, D], BF16, kind="Internal").ap()
    rs_out = nc.dram_tensor("rs_out", [T_OWN, D], BF16, kind="Internal").ap()
    k2t_d = nc.dram_tensor("k2t_d", [DI, T_OWN], BF16, kind="Internal").ap()
    xn_park = nc.dram_tensor("xn_park", [T_OWN, D], BF16, kind="Internal").ap()

    tap = {}
    if debug:
        tap["qaug"] = nc.dram_tensor("tap_qaug", [8 * 65, S], BF16, kind="ExternalOutput").ap()
        tap["kaug"] = nc.dram_tensor("tap_kaug", [B * 65, S], BF16, kind="ExternalOutput").ap()
        tap["v"] = nc.dram_tensor("tap_v", [B * 128, NTB * 65], BF16, kind="ExternalOutput").ap()
        tap["oT"] = nc.dram_tensor("tap_oT", [2 * 128, T_ALL], BF16, kind="ExternalOutput").ap()
        tap["xn"] = nc.dram_tensor("tap_xn", [T_OWN, D], BF16, kind="ExternalOutput").ap()
        tap["g"] = nc.dram_tensor("tap_g", [T_OWN, D], BF16, kind="ExternalOutput").ap()
        tap["wsc"] = nc.dram_tensor("tap_wsc", [128, 8], F32, kind="ExternalOutput").ap()
        tap["rs"] = nc.dram_tensor("tap_rs", [T_OWN, D], BF16, kind="ExternalOutput").ap()
        tap["h"] = nc.dram_tensor("tap_h", [128, DI], BF16, kind="ExternalOutput").ap()
        tap["kx"] = nc.dram_tensor("tap_kx", [128, 16 * NT_OWN * 128], BF16, kind="ExternalOutput").ap()
        tap["hg"] = nc.dram_tensor("tap_hg", [128, 4 * 512], BF16, kind="ExternalOutput").ap()
        tap["wq"] = nc.dram_tensor("tap_wq", [128, 512], BF16, kind="ExternalOutput").ap()
        tap["hu"] = nc.dram_tensor("tap_hu", [128, 4 * 512], BF16, kind="ExternalOutput").ap()

    sb = nc.alloc_sbuf_tensor
    sems = {}

    def S_(name):
        if name not in sems:
            sems[name] = nc.alloc_semaphore(name)
        return sems[name]

    identb = sb("identb", [128, 128], BF16)
    ones_r = sb("ones_r", [1, 128], BF16)
    onesrow = sb("onesrow", [1, 512], F32)
    onescol = sb("onescol", [128, 1], F32)
    onescolb = sb("onescolb", [128, 1], BF16)
    c127 = sb("c127", [128, 1], F32)
    mcol = sb("mcol", [128, 1], F32)
    nmcol = sb("nmcol", [128, 1], F32)
    expb = sb("expb", [128, 1], F32)
    wsum = sb("wsum", [128, 192], F32)
    w1s = sb("w1s", [1, 4], F32)
    wsc = sb("wsc", [128, 4], F32)
    swc = sb("swc", [128, 4], F32)
    dsc8p = sb("dsc8p", [128, 8], F32)

    units = [(b, h, c) for b in range(B) for h in range(4) for c in range(8)]
    NU = len(units)
    grp_of = [2 * c + 2 for (_, _, c) in units]
    cum_grp = np.concatenate([[0], np.cumsum(grp_of)]).astype(int).tolist()
    NGRP = cum_grp[-1]

    _es0 = contextlib.ExitStack()
    masks_sb = _es0.enter_context(nc.sbuf_tensor("masks_sb", [128, 4 * 512], F32))
    invcnt_sb = _es0.enter_context(nc.sbuf_tensor("invcnt_sb", [1, S], BF16))
    winb = _es0.enter_context(nc.sbuf_tensor("winb", [128, 16 * CC], BF16))
    rmsc = _es0.enter_context(nc.sbuf_tensor("rmsc", [128, 16], F32))
    qa0 = _es0.enter_context(nc.sbuf_tensor("qa0", [65, S], BF16))
    qa1 = _es0.enter_context(nc.sbuf_tensor("qa1", [65, S], BF16))
    qa2 = _es0.enter_context(nc.sbuf_tensor("qa2", [65, S], BF16))
    qa3 = _es0.enter_context(nc.sbuf_tensor("qa3", [65, S], BF16))
    qa4 = _es0.enter_context(nc.sbuf_tensor("qa4", [65, S], BF16))
    qa5 = _es0.enter_context(nc.sbuf_tensor("qa5", [65, S], BF16))
    qa6 = _es0.enter_context(nc.sbuf_tensor("qa6", [65, S], BF16))
    qa7 = _es0.enter_context(nc.sbuf_tensor("qa7", [65, S], BF16))
    ka0 = _es0.enter_context(nc.sbuf_tensor("ka0", [65, S], BF16))
    ka1 = _es0.enter_context(nc.sbuf_tensor("ka1", [65, S], BF16))
    vs0 = _es0.enter_context(nc.sbuf_tensor("vs0", [128, NTB * 65], BF16))
    vs1 = _es0.enter_context(nc.sbuf_tensor("vs1", [128, NTB * 65], BF16))
    if True:
        q_aug = [qa0, qa1, qa2, qa3, qa4, qa5, qa6, qa7]
        k_aug = [ka0, ka1]
        v_sb = [vs0, vs1]

        _es1 = contextlib.ExitStack()
        trigc = _es1.enter_context(nc.sbuf_tensor("trigc", [128, 32 * NTB], F32))
        trigs = _es1.enter_context(nc.sbuf_tensor("trigs", [128, 32 * NTB], F32))
        ident32 = _es1.enter_context(nc.sbuf_tensor("ident32", [128, 128], F32))
        win32 = _es1.enter_context(nc.sbuf_tensor("win32", [128, CC], F32))
        xt0 = _es1.enter_context(nc.sbuf_tensor("xt0", [128, D], F32))
        xt1 = _es1.enter_context(nc.sbuf_tensor("xt1", [128, D], F32))
        xt2 = _es1.enter_context(nc.sbuf_tensor("xt2", [128, D], F32))
        xt3 = _es1.enter_context(nc.sbuf_tensor("xt3", [128, D], F32))
        xnt0 = _es1.enter_context(nc.sbuf_tensor("xnt0", [128, D], BF16))
        xnt1 = _es1.enter_context(nc.sbuf_tensor("xnt1", [128, D], BF16))
        xnT = _es1.enter_context(nc.sbuf_tensor("xnT", [128, 16 * 128], BF16))
        sqs = _es1.enter_context(nc.sbuf_tensor("sqs", [128, D], BF16))
        stat = _es1.enter_context(nc.sbuf_tensor("stat", [128, 56], F32))
        ropeq0 = _es1.enter_context(nc.sbuf_tensor("ropeq0", [128, 256], BF16))
        ropeq1 = _es1.enter_context(nc.sbuf_tensor("ropeq1", [128, 256], BF16))
        ropek0 = _es1.enter_context(nc.sbuf_tensor("ropek0", [128, 64], BF16))
        ropek1 = _es1.enter_context(nc.sbuf_tensor("ropek1", [128, 64], BF16))
        rsc1 = _es1.enter_context(nc.sbuf_tensor("rsc1", [128, 128], F32))
        rsc2 = _es1.enter_context(nc.sbuf_tensor("rsc2", [128, 128], F32))
        pw0 = _es1.enter_context(nc.sbuf_tensor("pw0", [128, 2048], F32))
        pw1 = _es1.enter_context(nc.sbuf_tensor("pw1", [128, 2048], F32))
        pabs = _es1.enter_context(nc.sbuf_tensor("pabs", [128, 2048], BF16))
        ps_tr0 = _es1.enter_context(nc.psum_tensor("ps_tr0", [128, 512], F32))
        ps_tr1 = _es1.enter_context(nc.psum_tensor("ps_tr1", [128, 512], F32))
        ps_y = _es1.enter_context(nc.psum_tensor("ps_y", [128, CC], F32))
        ps_qt = _es1.enter_context(nc.psum_tensor("ps_qt", [64, 512], F32))
        ps_kt = _es1.enter_context(nc.psum_tensor("ps_kt", [64, 128], F32))
        blk = _es1.enter_context(nc.Block())
        if True:
            xt = [xt0, xt1, xt2, xt3]
            xnt = [xnt0, xnt1]
            ropeq = [ropeq0, ropeq1]
            ropek = [ropek0, ropek1]
            c_ld, win_ld = S_("c_ld"), S_("win_ld")
            x_ldp = [S_("x_ld0"), S_("x_ld1"), S_("x_ld2"), S_("x_ld3")]
            s_le, s_xn = S_("s_le"), S_("s_xn")
            v_cst, win_cv = S_("v_cst"), S_("win_cv")
            s_sq, v_st, v_red = S_("s_sq"), S_("v_st"), S_("v_red")
            t_tr, v_xnT, t_y = S_("t_tr"), S_("v_xnT"), S_("t_y")
            v_rope, t_qk, v_qkev = S_("v_rope"), S_("t_qk"), S_("v_qkev")
            gu_ldp = [S_("gu_ld0"), S_("gu_ld1")]
            s_pw = S_("s_pw")

            @blk.sync
            def _(sync):
                for dst, src in [(trigc[:], trig_cos[:]), (trigs[:], trig_sin[:]),
                                 (masks_sb[:], masks[:]), (invcnt_sb[:], invcnt[:]),
                                 (rmsc[:], rms_col[:]), (ident32[:], ident_i[:])]:
                    sync.dma_start(out=dst, in_=src).then_inc(c_ld, 16)
                for ti in range(NT_ALL):
                    if ti >= 4:
                        sync.wait_ge(s_xn, ti - 3)
                        sync.wait_ge(v_red, ti - 3)
                    sync.dma_start(out=xt[ti % 4][:], in_=x_all[ti * 128:(ti + 1) * 128, :]).then_inc(x_ldp[ti % 4], 16)

            @blk.gpsimd
            def _(g):
                for mi in range(16):
                    if mi >= 1:
                        g.wait_ge(win_cv, mi)
                    g.dma_start(out=win32[:], in_=w_inT[mi * 128:(mi + 1) * 128, :]).then_inc(win_ld, 16)
                ch = 0
                for w_ap in (wgT, wuT):
                    for r in range(16):
                        for cc2 in range(4):
                            if ch >= 2:
                                g.wait_ge(s_pw, ch - 1)
                            g.dma_start(out=(pw0 if ch % 2 == 0 else pw1)[:],
                                        in_=w_ap[r * 128:(r + 1) * 128, cc2 * 2048:(cc2 + 1) * 2048]).then_inc(gu_ldp[ch % 2], 16)
                            ch += 1

            @blk.vector
            def _(v):
                v = _Drain(v)
                v.wait_ge(c_ld, 16 * 6)
                v.memset(ones_r[:], 1.0)
                v.memset(onesrow[:], 1.0)
                v.memset(onescol[:], 1.0)
                v.memset(onescolb[:], 1.0)
                v.memset(c127[:], 127.0)
                v.memset(mcol[:], 12582912.0)
                v.memset(nmcol[:], -12582912.0)
                v.memset(expb[:], -40.0)
                v.tensor_copy(identb[:], ident32[:])
                for b in range(B):
                    v.memset(k_aug[b][64:65, :], 1.0)
                    v.memset(v_sb[b].rearrange("p (n c) -> p n c", c=65)[:, :, 64:65], 1.0)
                v.memset(wsum[:], 0.0).then_inc(v_cst, 1)
                for mi in range(16):
                    v.wait_ge(win_ld, 16 * (mi + 1))
                    v.tensor_scalar(winb[:, mi * CC:(mi + 1) * CC], win32[:],
                                    rmsc[:, mi:mi + 1], None, ALU.mult).then_inc(win_cv, 1)

                for ti in range(NT_ALL):
                    Q = ti // 4
                    BQ = 28 * (Q % 2)
                    if ti % 4 == 0:
                        for tj in range(4):
                            v.wait_ge(x_ldp[tj], 16 * (Q + 1))
                            v.reduce_sum(stat[:, BQ + tj:BQ + tj + 1], xt[tj][:], AX.X).then_inc(v_red, 1)
                        v.wait_ge(s_sq, 4 * (Q + 1))
                        mu = stat[:, BQ + 8:BQ + 12]
                        t2 = stat[:, BQ + 12:BQ + 16]
                        rec = stat[:, BQ + 16:BQ + 20]
                        c0 = stat[:, BQ + 20:BQ + 24]
                        nb = stat[:, BQ + 24:BQ + 28]
                        v.tensor_scalar_mul(mu, stat[:, BQ + 0:BQ + 4], 1.0 / D)
                        v.tensor_scalar_mul(t2, stat[:, BQ + 4:BQ + 8], 1.0 / D)
                        v.tensor_mul(rec, mu, mu)
                        v.tensor_sub(t2, t2, rec)
                        v.tensor_scalar_add(rec, t2, 1e-5)
                        v.scalar_tensor_tensor(t2, rec, 1e-6, t2, ALU.mult, ALU.add)
                        v.reciprocal(rec, t2).then_inc(v_st, 1)
                        v.wait_ge(s_le, Q + 1)
                        v.tensor_mul(nb, c0, c0)
                        v.scalar_tensor_tensor(nb, nb, -0.5, t2, ALU.mult, ALU.mult)
                        v.tensor_scalar_add(nb, nb, 1.5)
                        v.tensor_mul(c0, nb, c0)
                        v.scalar_tensor_tensor(nb, mu, -1.0, c0, ALU.mult, ALU.mult).then_inc(v_st, 1)

                    for r in range(4):
                        g4 = 4 * ti + r
                        pst = [ps_tr0, ps_tr1][g4 % 2]
                        v.wait_ge(t_tr, g4 + 1)
                        v.tensor_copy(xnT[:, r * 512:(r + 1) * 512], pst[:]).then_inc(v_xnT, 1)

                    v.wait_ge(t_y, ti + 1)
                    b = ti // NTB
                    pt = ti % NTB
                    cos4 = trigc[:, pt * 32:(pt + 1) * 32].unsqueeze(1).to_broadcast([128, 4, 32])
                    sin4 = trigs[:, pt * 32:(pt + 1) * 32].unsqueeze(1).to_broadcast([128, 4, 32])
                    ya = ps_y[:, 0:256].rearrange("p (h x) -> p h x", h=4)
                    rqv = ropeq[ti % 2][:].rearrange("p (h x) -> p h x", h=4)
                    r1 = rsc1[:].rearrange("p (h x) -> p h x", h=4)[:, :, 0:32]
                    r2 = rsc2[:].rearrange("p (h x) -> p h x", h=4)[:, :, 0:32]
                    v.tensor_tensor(r1, ya[:, :, 0:32], cos4, ALU.mult)
                    v.tensor_tensor(r2, ya[:, :, 32:64], sin4, ALU.mult)
                    v.tensor_sub(rqv[:, :, 0:32], r1, r2)
                    v.tensor_tensor(r1, ya[:, :, 32:64], cos4, ALU.mult)
                    v.tensor_tensor(r2, ya[:, :, 0:32], sin4, ALU.mult)
                    v.tensor_add(rqv[:, :, 32:64], r1, r2)
                    cos1 = trigc[:, pt * 32:(pt + 1) * 32]
                    sin1 = trigs[:, pt * 32:(pt + 1) * 32]
                    rk = ropek[ti % 2]
                    v.tensor_tensor(rsc1[:, 0:32], ps_y[:, 256:288], cos1, ALU.mult)
                    v.tensor_tensor(rsc2[:, 0:32], ps_y[:, 288:320], sin1, ALU.mult)
                    v.tensor_sub(rk[:, 0:32], rsc1[:, 0:32], rsc2[:, 0:32])
                    v.tensor_tensor(rsc1[:, 0:32], ps_y[:, 288:320], cos1, ALU.mult)
                    v.tensor_tensor(rsc2[:, 0:32], ps_y[:, 256:288], sin1, ALU.mult)
                    v.tensor_add(rk[:, 32:64], rsc1[:, 0:32], rsc2[:, 0:32])
                    v.tensor_copy(v_sb[b][:, pt * 65:pt * 65 + 64], ps_y[:, 320:384]).then_inc(v_rope, 1)

                    v.wait_ge(t_qk, ti + 1)
                    for h in range(4):
                        v.tensor_copy(q_aug[b * 4 + h][0:64, pt * 128:(pt + 1) * 128],
                                      ps_qt[0:64, h * 128:(h + 1) * 128])
                    v.tensor_copy(k_aug[b][0:64, pt * 128:(pt + 1) * 128], ps_kt[:, :]).then_inc(v_qkev, 1)

            @blk.scalar
            def _(s):
                s = _Drain(s)
                s.wait_ge(v_cst, 1)
                ch = 0
                for Q in range(NT_ALL // 4):
                    BQ = 28 * (Q % 2)
                    for tj in range(4):
                        s.wait_ge(x_ldp[tj], 16 * (Q + 1))
                        s.activation(sqs[:], xt[tj][:], AF.Square,
                                     accum_out=stat[:, BQ + 4 + tj:BQ + 5 + tj]).then_inc(s_sq, 1)
                    s.wait_ge(v_st, 2 * Q + 1)
                    s.activation(stat[:, BQ + 20:BQ + 24], stat[:, BQ + 16:BQ + 20], AF.Ln)
                    s.activation(stat[:, BQ + 20:BQ + 24], stat[:, BQ + 20:BQ + 24], AF.Exp,
                                 scale=-0.5).then_inc(s_le, 1)
                    s.wait_ge(v_st, 2 * Q + 2)
                    for tj in range(4):
                        ti = 4 * Q + tj
                        if ti >= 2:
                            s.wait_ge(t_tr, 4 * (ti - 1))
                        s.activation(xnt[ti % 2][:], xt[tj][:], AF.Identity,
                                     bias=stat[:, BQ + 24 + tj:BQ + 25 + tj],
                                     scale=stat[:, BQ + 20 + tj:BQ + 21 + tj]).then_inc(s_xn, 1)
                        for _2 in range(2):
                            if ch < 128:
                                s.wait_ge(gu_ldp[ch % 2], 16 * (ch // 2 + 1))
                                s.activation(pabs[:], (pw0 if ch % 2 == 0 else pw1)[:], AF.Abs,
                                             accum_out=wsum[:, ch:ch + 1]).then_inc(s_pw, 1)
                                ch += 1

            @blk.tensor
            def _(t):
                t.wait_ge(v_cst, 1)
                t.wait_ge(win_cv, 16)
                for ti in range(NT_ALL):
                    t.wait_ge(s_xn, ti + 1)
                    for r in range(4):
                        g4 = 4 * ti + r
                        pst = [ps_tr0, ps_tr1][g4 % 2]
                        if g4 >= 2:
                            t.wait_ge(v_xnT, g4 - 1)
                        ins = None
                        for j in range(4):
                            mi = r * 4 + j
                            ins = t.matmul(pst[:, j * 128:(j + 1) * 128],
                                           xnt[ti % 2][:, mi * 128:(mi + 1) * 128], identb[:],
                                           start=True, stop=True)
                        ins.then_inc(t_tr, 1)
                    t.wait_ge(v_xnT, 4 * ti + 4)
                    if ti >= 1:
                        t.wait_ge(v_rope, ti)
                    ins = None
                    for mi in range(16):
                        ins = t.matmul(ps_y[:], xnT[:, mi * 128:(mi + 1) * 128],
                                       winb[:, mi * CC:(mi + 1) * CC], start=(mi == 0), stop=(mi == 15))
                    ins.then_inc(t_y, 1)
                    t.wait_ge(v_rope, ti + 1)
                    if ti >= 1:
                        t.wait_ge(v_qkev, ti)
                    for h in range(4):
                        t.matmul(ps_qt[0:64, h * 128:(h + 1) * 128],
                                 ropeq[ti % 2][:, h * 64:(h + 1) * 64], identb[:],
                                 start=True, stop=True)
                    t.matmul(ps_kt[:, :], ropek[ti % 2][:], identb[:],
                             start=True, stop=True).then_inc(t_qk, 1)

        _es1.close()
        _es2 = contextlib.ExitStack()
        kpre = _es2.enter_context(nc.sbuf_tensor("kpre", [64, S], F32))
        dotbf = _es2.enter_context(nc.sbuf_tensor("dotbf", [64, S], BF16))
        wred = _es2.enter_context(nc.sbuf_tensor("wred", [128, 2], F32))
        ps_d0 = _es2.enter_context(nc.psum_tensor("ps_d0", [1, 512], F32))
        ps_d1 = _es2.enter_context(nc.psum_tensor("ps_d1", [1, 512], F32))
        ps_w1 = _es2.enter_context(nc.psum_tensor("ps_w1", [1, 2], F32))
        ps_bc = _es2.enter_context(nc.psum_tensor("ps_bc", [128, 2], F32))
        blk2 = _es2.enter_context(nc.Block())
        if True:
            v_scan, t_dot, v_aug = S_("v_scan"), S_("t_dot"), S_("v_aug")
            v_wred, t_w1, v_w1 = S_("v_wred"), S_("t_w1"), S_("v_w1")

            @blk2.vector
            def _(v):
                v = _Drain(v)
                v.wait_ge(S_("s_pw"), 128)
                v.wait_ge(S_("t_qk"), NT_ALL)
                v.reduce_sum(wred[:, 0:1], wsum[:, 0:64], AX.X)
                v.reduce_sum(wred[:, 1:2], wsum[:, 64:128], AX.X).then_inc(v_wred, 2)
                v.wait_ge(t_w1, 2)
                v.tensor_scalar_mul(w1s[:, 0:2], ps_w1[0:1, 0:2], 1.0 / (D * DI))
                v.tensor_scalar_max(w1s[:, 0:2], w1s[:, 0:2], 1e-5).then_inc(v_w1, 1)
                v.wait_ge(t_w1, 4)
                v.tensor_copy(wsc[:, 0:2], ps_bc[:, 0:2])
                v.reciprocal(swc[:, 0:2], wsc[:, 0:2]).then_inc(v_w1, 1)

                v.wait_ge(S_("v_qkev"), NT_ALL)
                for b in range(B):
                    v.tensor_tensor_scan(kpre[:], onescol[0:64, 0:1].to_broadcast([64, S]),
                                         k_aug[b][0:64, :], 0.0, ALU.mult, ALU.add).then_inc(v_scan, 1)
                    for h in range(4):
                        i = b * 4 + h
                        if i > 0:
                            v.wait_ge(t_dot, i * 8)
                        v.tensor_mul(dotbf[:], q_aug[i][0:64, :], kpre[:]).then_inc(v_scan, 1)
                        for c in range(8):
                            v.wait_ge(t_dot, i * 8 + c + 1)
                            psd = [ps_d0, ps_d1][c % 2]
                            v.scalar_tensor_tensor(q_aug[i][64:65, c * 512:(c + 1) * 512],
                                                   psd[0:1, :], -1.0,
                                                   invcnt_sb[0:1, c * 512:(c + 1) * 512],
                                                   ALU.mult, ALU.mult).then_inc(v_aug, 1)

            @blk2.tensor
            def _(t):
                t.wait_ge(v_wred, 2)
                t.matmul(ps_w1[0:1, 0:1], wred[:, 0:1], onescol[:, 0:1], start=True, stop=True)
                t.matmul(ps_w1[0:1, 1:2], wred[:, 1:2], onescol[:, 0:1], start=True, stop=True).then_inc(t_w1, 2)
                t.wait_ge(v_w1, 1)
                t.matmul(ps_bc[:, 0:1], onesrow[0:1, 0:128], w1s[0:1, 0:1], start=True, stop=True)
                t.matmul(ps_bc[:, 1:2], onesrow[0:1, 0:128], w1s[0:1, 1:2], start=True, stop=True).then_inc(t_w1, 2)
                for b in range(B):
                    for h in range(4):
                        i = b * 4 + h
                        t.wait_ge(S_("v_scan"), b + i + 2)
                        for c in range(8):
                            psd = [ps_d0, ps_d1][c % 2]
                            if c >= 2:
                                t.wait_ge(S_("v_aug"), i * 8 + c - 1)
                            t.matmul(psd[0:1, :], onescolb[0:64, 0:1], dotbf[:, c * 512:(c + 1) * 512],
                                     start=True, stop=True).then_inc(t_dot, 1)

        _es2.close()
        _es3 = contextlib.ExitStack()
        oT0 = _es3.enter_context(nc.sbuf_tensor("oT0", [128, T_ALL], BF16))
        oT1 = _es3.enter_context(nc.sbuf_tensor("oT1", [128, T_ALL], BF16))
        if True:
            oT = [oT0, oT1]
            _es4 = contextlib.ExitStack()
            pt0 = _es4.enter_context(nc.sbuf_tensor("pt0", [128, 1024], BF16))
            pt1 = _es4.enter_context(nc.sbuf_tensor("pt1", [128, 1024], BF16))
            rden = _es4.enter_context(nc.sbuf_tensor("rden", [1, 512], BF16))
            rdenf = _es4.enter_context(nc.sbuf_tensor("rdenf", [1, 512], F32))
            sce = _es4.enter_context(nc.sbuf_tensor("sce", [64, 512], BF16))
            tmpo0 = _es4.enter_context(nc.sbuf_tensor("tmpo0", [64, 512], BF16))
            tmpo1 = _es4.enter_context(nc.sbuf_tensor("tmpo1", [64, 512], BF16))
            pwd0 = _es4.enter_context(nc.sbuf_tensor("pwd0", [128, 2048], F32))
            pwd1 = _es4.enter_context(nc.sbuf_tensor("pwd1", [128, 2048], F32))
            ps_s0 = _es4.enter_context(nc.psum_tensor("ps_s0", [128, 1024], F32))
            ps_s1 = _es4.enter_context(nc.psum_tensor("ps_s1", [128, 1024], F32))
            ps_o0 = _es4.enter_context(nc.psum_tensor("ps_o0", [65, 512], F32))
            ps_o1 = _es4.enter_context(nc.psum_tensor("ps_o1", [65, 512], F32))
            ps_e = _es4.enter_context(nc.psum_tensor("ps_e", [64, 512], F32))
            ps_wb = _es4.enter_context(nc.psum_tensor("ps_wb", [128, 2], F32))
            blk3 = _es4.enter_context(nc.Block())
            if True:
                ptile = [pt0, pt1]
                tmpo = [tmpo0, tmpo1]
                ps_s = [ps_s0, ps_s1]
                ps_o = [ps_o0, ps_o1]
                t_qk3, v_msk, s_exp = S_("t_qk3"), S_("v_msk"), S_("s_exp")
                t_av, v_den1, t_ex = S_("t_av"), S_("v_den1"), S_("t_ex")
                v_den2 = S_("v_den2")
                g_crossp = [S_("g_cross0"), S_("g_cross1")]
                wd_ldp = [S_("wd_ld0"), S_("wd_ld1")]
                v_pwd = S_("v_pwd")
                v_wredd, t_w1d, v_w1d = S_("v_wredd"), S_("t_w1d"), S_("v_w1d")

                @blk3.gpsimd
                def _(g):
                    g.wait_ge(S_("v_aug"), 64)
                    g.wait_ge(S_("t_dot"), 64)
                    for u, (b, h, c) in enumerate(units):
                        ch = u
                        if ch < 64:
                            if ch >= 2:
                                g.wait_ge(v_pwd, ch - 1)
                            g.dma_start(out=(pwd0 if ch % 2 == 0 else pwd1)[:],
                                        in_=wdT[ch * 128:(ch + 1) * 128, :]).then_inc(wd_ldp[ch % 2], 16)
                        g.wait_ge(v_den2, u + 1)
                        g.dma_start(out=oT[h // 2][(h % 2) * 64:(h % 2) * 64 + 64,
                                                   b * S + c * 512:b * S + (c + 1) * 512],
                                    in_=tmpo[u % 2][:, :]).then_inc(g_crossp[u % 2], 16)

                @blk3.tensor
                def _(t):
                    t.wait_ge(S_("v_aug"), 64)
                    for u, (b, h, c) in enumerate(units):
                        i = b * 4 + h
                        ng = 2 * c + 2

                        def qk(g2):
                            gg = cum_grp[u] + g2
                            if gg >= 2:
                                t.wait_ge(s_exp, gg - 1)
                            ins = None
                            for j in range(2):
                                kvt = 2 * g2 + j
                                ins = t.matmul(ps_s[gg % 2][:, j * 512:(j + 1) * 512],
                                               k_aug[b][:, kvt * 128:(kvt + 1) * 128],
                                               q_aug[i][:, c * 512:(c + 1) * 512],
                                               start=True, stop=True)
                            ins.then_inc(t_qk3, 1)

                        def av(g2):
                            gg = cum_grp[u] + g2
                            t.wait_ge(s_exp, gg + 1)
                            if g2 == 0 and u >= 2:
                                t.wait_ge(v_den2, u - 1)
                            ins = None
                            for j in range(2):
                                kvt = 2 * g2 + j
                                ins = t.matmul(ps_o[u % 2][:, :],
                                               v_sb[b][:, kvt * 65:(kvt + 1) * 65],
                                               ptile[gg % 2][:, j * 512:(j + 1) * 512],
                                               start=(kvt == 0), stop=(kvt == 2 * ng - 1))
                            ins.then_inc(t_av, 1)

                        for g2 in range(ng):
                            qk(g2)
                            if g2 >= 1:
                                av(g2 - 1)
                        av(ng - 1)
                        t.wait_ge(v_den1, u + 1)
                        t.matmul(ps_e[:, :], ones_r[0:1, 0:64], rden[0:1, :],
                                 start=True, stop=True).then_inc(t_ex, 1)
                    t.wait_ge(v_wredd, 1)
                    t.matmul(ps_wb[0:1, 0:1], wsc[:, 3:4], onescol[:, 0:1], start=True, stop=True).then_inc(t_w1d, 1)
                    t.wait_ge(v_w1d, 1)
                    t.matmul(ps_wb[:, 1:2], onesrow[0:1, 0:128], w1s[0:1, 2:3], start=True, stop=True).then_inc(t_w1d, 1)

                @blk3.vector
                def _(v):
                    v = _Drain(v)
                    wd_ch = 0
                    for u, (b, h, c) in enumerate(units):
                        ng = 2 * c + 2
                        for dg in range(2):
                            g2 = ng - 2 + dg
                            gg = cum_grp[u] + g2
                            v.wait_ge(t_qk3, gg + 1)
                            for j in range(2):
                                kvt = 2 * g2 + j
                                r = kvt - 4 * c
                                v.tensor_add(ps_s[gg % 2][:, j * 512:(j + 1) * 512],
                                             ps_s[gg % 2][:, j * 512:(j + 1) * 512],
                                             masks_sb[:, r * 512:(r + 1) * 512])
                            v.engine_nop().then_inc(v_msk, 1)
                        v.wait_ge(t_av, cum_grp[u + 1])
                        v.reciprocal(rdenf[0:1, :], ps_o[u % 2][64:65, :])
                        v.tensor_copy(rden[0:1, :], rdenf[0:1, :]).then_inc(v_den1, 1)
                        v.wait_ge(t_ex, u + 1)
                        v.tensor_copy(sce[:, :], ps_e[:, :])
                        if u >= 2:
                            v.wait_ge(g_crossp[u % 2], 16 * (u // 2))
                        v.tensor_mul(tmpo[u % 2][:, :], ps_o[u % 2][0:64, :], sce[:, :]).then_inc(v_den2, 1)
                        if wd_ch < 64:
                            v.wait_ge(wd_ldp[wd_ch % 2], 16 * (wd_ch // 2 + 1))
                            v.reduce_sum(wsum[:, 128 + wd_ch:129 + wd_ch],
                                         (pwd0 if wd_ch % 2 == 0 else pwd1)[:], AX.X,
                                         apply_absolute_value=True).then_inc(v_pwd, 1)
                            wd_ch += 1
                    v.reduce_sum(wsc[:, 3:4], wsum[:, 128:192], AX.X).then_inc(v_wredd, 1)
                    v.wait_ge(t_w1d, 1)
                    v.tensor_scalar_mul(w1s[:, 2:3], ps_wb[0:1, 0:1], 1.0 / (D * DI))
                    v.tensor_scalar_max(w1s[:, 2:3], w1s[:, 2:3], 1e-5).then_inc(v_w1d, 1)
                    v.wait_ge(t_w1d, 2)
                    v.tensor_copy(wsc[:, 2:3], ps_wb[:, 1:2])
                    v.reciprocal(swc[:, 2:3], wsc[:, 2:3]).then_inc(v_w1d, 1)

                @blk3.scalar
                def _(s):
                    s = _Drain(s)
                    for u, (b, h, c) in enumerate(units):
                        ng = 2 * c + 2
                        for g2 in range(ng):
                            gg = cum_grp[u] + g2
                            if g2 >= ng - 2:
                                s.wait_ge(v_msk, 2 * u + (g2 - (ng - 2)) + 1)
                            else:
                                s.wait_ge(t_qk3, gg + 1)
                            if gg >= 2:
                                s.wait_ge(t_av, gg - 1)
                            s.activation(ptile[gg % 2][:, :], ps_s[gg % 2][:, :], AF.Exp,
                                         bias=expb[:, 0:1], scale=0.125).then_inc(s_exp, 1)

            _es4.close()
            _es5 = contextlib.ExitStack()
            wo32 = _es5.enter_context(nc.sbuf_tensor("wo32", [128, 512], F32))
            woutb = _es5.enter_context(nc.sbuf_tensor("woutb", [128, 2 * D], BF16))
            rst = _es5.enter_context(nc.sbuf_tensor("rst", [128, 4 * 512], BF16))
            ps_w0 = _es5.enter_context(nc.psum_tensor("ps_w0", [128, 512], F32))
            ps_w1b = _es5.enter_context(nc.psum_tensor("ps_w1b", [128, 512], F32))
            blk4 = _es5.enter_context(nc.Block())
            if True:
                wo_ld, wo_cv, t_wo = S_("wo_ld"), S_("wo_cv"), S_("t_wo")
                v_rs, g_coll = S_("v_rs"), S_("g_coll")
                rs_stp = [S_("rs_st0"), S_("rs_st1"), S_("rs_st2"), S_("rs_st3")]
                ps_w = [ps_w0, ps_w1b]

                @blk4.gpsimd
                def _(g):
                    g.wait_ge(S_("g_cross0"), 16 * (NU // 2))
                    g.wait_ge(S_("g_cross1"), 16 * (NU // 2))
                    for j in range(8):
                        ct, mc = j // 4, j % 4
                        if j >= 1:
                            g.wait_ge(wo_cv, j)
                        g.dma_start(out=wo32[:],
                                    in_=w_outT[ct * 128:(ct + 1) * 128, mc * 512:(mc + 1) * 512]).then_inc(wo_ld, 16)
                    for q4 in range(4):
                        g.wait_ge(rs_stp[q4], 16 * 64)
                    g.collective_compute(
                        "ReduceScatter", ALU.add,
                        replica_groups=[list(range(8))],
                        ins=[rs_in[:, :]],
                        outs=[rs_out[:, :]],
                    ).then_inc(g_coll, 1)

                @blk4.vector
                def _(v):
                    v = _Drain(v)
                    v.wait_ge(S_("g_cross0"), 16 * (NU // 2))
                    v.wait_ge(S_("g_cross1"), 16 * (NU // 2))
                    for j in range(8):
                        v.wait_ge(wo_ld, 16 * (j + 1))
                        ct, mc = j // 4, j % 4
                        v.tensor_copy(woutb[:, ct * D + mc * 512:ct * D + (mc + 1) * 512],
                                      wo32[:]).then_inc(wo_cv, 1)
                    for idx in range(256):
                        v.wait_ge(t_wo, idx + 1)
                        if idx >= 4:
                            v.wait_ge(rs_stp[idx % 4], 16 * (idx // 4))
                        v.tensor_copy(rst[:, (idx % 4) * 512:(idx % 4 + 1) * 512],
                                      ps_w[idx % 2][:, :]).then_inc(v_rs, 1)

                @blk4.tensor
                def _(t):
                    t.wait_ge(wo_cv, 8)
                    t.wait_ge(S_("g_cross0"), 16 * (NU // 2))
                    t.wait_ge(S_("g_cross1"), 16 * (NU // 2))
                    for idx in range(256):
                        tt, mc = idx // 4, idx % 4
                        if idx >= 2:
                            t.wait_ge(v_rs, idx - 1)
                        ins = None
                        for ct in range(2):
                            ins = t.matmul(ps_w[idx % 2][:, :], oT[ct][:, tt * 128:(tt + 1) * 128],
                                           woutb[:, ct * D + mc * 512:ct * D + (mc + 1) * 512],
                                           start=(ct == 0), stop=(ct == 1))
                        ins.then_inc(t_wo, 1)

                @blk4.sync
                def _(sync):
                    for idx in range(256):
                        tt, mc = idx // 4, idx % 4
                        sync.wait_ge(v_rs, idx + 1)
                        sync.dma_start(out=rs_in[tt * 128:(tt + 1) * 128, mc * 512:(mc + 1) * 512],
                                       in_=rst[:, (idx % 4) * 512:(idx % 4 + 1) * 512]).then_inc(rs_stp[idx % 4], 16)

                if debug:
                    @blk4.scalar
                    def _(s):
                        s = _Drain(s)
                        s.wait_ge(S_("g_cross0"), 16 * (NU // 2))
                        s.wait_ge(S_("g_cross1"), 16 * (NU // 2))
                        for i in range(8):
                            s.dma_start(out=tap["qaug"][i * 65:(i + 1) * 65, :], in_=q_aug[i][:, :]).then_inc(S_("tapsem"), 16)
                        for b in range(B):
                            s.dma_start(out=tap["kaug"][b * 65:(b + 1) * 65, :], in_=k_aug[b][:, :]).then_inc(S_("tapsem"), 16)
                            s.dma_start(out=tap["v"][b * 128:(b + 1) * 128, :], in_=v_sb[b][:, :]).then_inc(S_("tapsem"), 16)
                        for p in range(2):
                            s.dma_start(out=tap["oT"][p * 128:(p + 1) * 128, :], in_=oT[p][:, :]).then_inc(S_("tapsem"), 16)
                        s.wait_ge(S_("tapsem"), 16 * 14)

    _es5.close()
    _es3.close()
    _es0.close()
    with nc.sbuf_tensor("hslab", [128, 8 * DI], BF16) as hslab:
        _es6 = contextlib.ExitStack()
        kxT = _es6.enter_context(nc.sbuf_tensor("kxT", [128, 16 * NT_OWN * 128], BF16))
        mxc8 = _es6.enter_context(nc.sbuf_tensor("mxc8", [128, 8], F32))
        gsc8 = _es6.enter_context(nc.sbuf_tensor("gsc8", [128, 8], F32))
        usc8 = _es6.enter_context(nc.sbuf_tensor("usc8", [128, 8], F32))
        if True:
            _es7 = contextlib.ExitStack()
            xo0 = _es7.enter_context(nc.sbuf_tensor("xo0", [128, D], F32))
            xo1 = _es7.enter_context(nc.sbuf_tensor("xo1", [128, D], F32))
            xno0 = _es7.enter_context(nc.sbuf_tensor("xno0", [128, D], BF16))
            xno1 = _es7.enter_context(nc.sbuf_tensor("xno1", [128, D], BF16))
            sq5 = _es7.enter_context(nc.sbuf_tensor("sq5", [128, D], BF16))
            st5 = _es7.enter_context(nc.sbuf_tensor("st5", [128, 24], F32))
            kxf = _es7.enter_context(nc.sbuf_tensor("kxf", [128, D], F32))
            kxb = _es7.enter_context(nc.sbuf_tensor("kxb", [128, D], BF16))
            ps_t50 = _es7.enter_context(nc.psum_tensor("ps_t50", [128, 512], F32))
            ps_t51 = _es7.enter_context(nc.psum_tensor("ps_t51", [128, 512], F32))
            blk5a = _es7.enter_context(nc.Block())
            if True:
                xo = [xo0, xo1]
                xno = [xno0, xno1]
                ps_t5 = [ps_t50, ps_t51]
                xo_ldp = [S_("xo_ld0"), S_("xo_ld1")]
                s_sq5, v_st5, v_red5 = S_("s_sq5"), S_("v_st5"), S_("v_red5")
                v_kx, t_kxT, v_kxT = S_("v_kx"), S_("t_kxT"), S_("v_kxT")
                xp_stp = [S_("xp_st0"), S_("xp_st1")]

                @blk5a.sync
                def _(sync):
                    for q4 in range(4):
                        sync.wait_ge(S_(f"rs_st{q4}"), 16 * 64)
                    def park(j):
                        sync.wait_ge(v_kx, 3 * j + 2)
                        sync.dma_start(out=xn_park[j * 128:(j + 1) * 128, :], in_=xno[j % 2][:]).then_inc(xp_stp[j % 2], 16)
                    for ti in range(NT_OWN):
                        if ti >= 2:
                            sync.wait_ge(s_sq5, 4 * (ti - 2) + 4)
                            sync.wait_ge(v_red5, ti - 1)
                        sync.dma_start(out=xo[ti % 2][:], in_=x_own[ti * 128:(ti + 1) * 128, :]).then_inc(xo_ldp[ti % 2], 16)
                        if ti >= 2:
                            park(ti - 2)
                    park(NT_OWN - 2)
                    park(NT_OWN - 1)

                @blk5a.scalar
                def _(s):
                    s = _Drain(s)
                    for q4 in range(4):
                        s.wait_ge(S_(f"rs_st{q4}"), 16 * 64)
                    for ti in range(NT_OWN):
                        xb = xo[ti % 2]
                        o = 12 * (ti % 2)
                        s.wait_ge(xo_ldp[ti % 2], 16 * (ti // 2 + 1))
                        s.activation(sq5[:], xb[:], AF.Square, accum_out=st5[:, o + 1:o + 2]).then_inc(s_sq5, 1)
                        s.wait_ge(v_st5, 2 * ti + 1)
                        s.activation(st5[:, o + 9:o + 10], st5[:, o + 6:o + 7], AF.Ln).then_inc(s_sq5, 1)
                        s.activation(st5[:, o + 10:o + 11], st5[:, o + 9:o + 10], AF.Exp, scale=-0.5).then_inc(s_sq5, 1)
                        s.wait_ge(v_st5, 2 * ti + 2)
                        if ti >= 2:
                            s.wait_ge(v_kx, 3 * (ti - 2) + 3)
                            s.wait_ge(xp_stp[ti % 2], 16 * (ti // 2))
                        s.activation(xno[ti % 2][:], xb[:], AF.Identity,
                                     bias=st5[:, o + 11:o + 12], scale=st5[:, o + 10:o + 11]).then_inc(s_sq5, 1)

                @blk5a.vector
                def _(v):
                    v = _Drain(v)
                    for q4 in range(4):
                        v.wait_ge(S_(f"rs_st{q4}"), 16 * 64)
                    for ti in range(NT_OWN):
                        xb = xo[ti % 2]
                        o = 12 * (ti % 2)
                        v.wait_ge(xo_ldp[ti % 2], 16 * (ti // 2 + 1))
                        v.reduce_sum(st5[:, o + 0:o + 1], xb[:], AX.X).then_inc(v_red5, 1)
                        v.wait_ge(s_sq5, 4 * ti + 1)
                        v.tensor_scalar_mul(st5[:, o + 2:o + 3], st5[:, o + 0:o + 1], 1.0 / D)
                        v.tensor_scalar_mul(st5[:, o + 3:o + 4], st5[:, o + 1:o + 2], 1.0 / D)
                        v.tensor_scalar(st5[:, o + 4:o + 5], st5[:, o + 2:o + 3], st5[:, o + 2:o + 3], None, ALU.mult)
                        v.tensor_sub(st5[:, o + 5:o + 6], st5[:, o + 3:o + 4], st5[:, o + 4:o + 5])
                        v.tensor_scalar_add(st5[:, o + 6:o + 7], st5[:, o + 5:o + 6], 1e-5).then_inc(v_st5, 1)
                        v.wait_ge(s_sq5, 4 * ti + 3)
                        v.tensor_scalar(st5[:, o + 11:o + 12], st5[:, o + 10:o + 11], st5[:, o + 10:o + 11], None, ALU.mult)
                        v.scalar_tensor_tensor(st5[:, o + 11:o + 12], st5[:, o + 11:o + 12], -0.5,
                                               st5[:, o + 6:o + 7], ALU.mult, ALU.mult)
                        v.tensor_scalar_add(st5[:, o + 11:o + 12], st5[:, o + 11:o + 12], 1.5)
                        v.tensor_mul(st5[:, o + 10:o + 11], st5[:, o + 11:o + 12], st5[:, o + 10:o + 11])
                        v.tensor_scalar(st5[:, o + 11:o + 12], st5[:, o + 2:o + 3],
                                        st5[:, o + 10:o + 11], -1.0, ALU.mult, ALU.mult).then_inc(v_st5, 1)
                        v.wait_ge(s_sq5, 4 * ti + 4)
                        v.reduce_max(st5[:, o + 4:o + 5], xno[ti % 2][:], AX.X, apply_absolute_value=True)
                        v.tensor_scalar_max(st5[:, o + 4:o + 5], st5[:, o + 4:o + 5], 1e-5)
                        v.reciprocal(st5[:, o + 5:o + 6], st5[:, o + 4:o + 5])
                        v.tensor_scalar_mul(st5[:, o + 5:o + 6], st5[:, o + 5:o + 6], 127.0)
                        v.tensor_scalar_mul(mxc8[:, ti:ti + 1], st5[:, o + 4:o + 5], 1.0 / 127.0).then_inc(v_kx, 1)
                        if ti >= 1:
                            v.wait_ge(t_kxT, 4 * ti)
                        v.tensor_scalar(kxf[:], xno[ti % 2][:], st5[:, o + 5:o + 6], 12582912.0,
                                        ALU.mult, ALU.add).then_inc(v_kx, 1)
                        v.tensor_scalar_add(kxb[:], kxf[:], -12582912.0).then_inc(v_kx, 1)
                        for r in range(4):
                            g4 = 4 * ti + r
                            v.wait_ge(t_kxT, g4 + 1)
                            pst = ps_t5[g4 % 2]
                            for j in range(4):
                                hi = r * 4 + j
                                v.tensor_copy(kxT[:, (hi * NT_OWN + ti) * 128:(hi * NT_OWN + ti + 1) * 128],
                                              pst[:, j * 128:(j + 1) * 128])
                            v.engine_nop().then_inc(v_kxT, 1)
                    v.wait_ge(S_("v_w1"), 2)
                    v.tensor_scalar(gsc8[:], mxc8[:], wsc[:, 0:1], None, ALU.mult)
                    v.tensor_scalar(usc8[:], mxc8[:], wsc[:, 1:2], None, ALU.mult).then_inc(v_kx, 1)

                @blk5a.tensor
                def _(t):
                    for q4 in range(4):
                        t.wait_ge(S_(f"rs_st{q4}"), 16 * 64)
                    for ti in range(NT_OWN):
                        t.wait_ge(v_kx, 3 * ti + 3)
                        for r in range(4):
                            g4 = 4 * ti + r
                            pst = ps_t5[g4 % 2]
                            if g4 >= 2:
                                t.wait_ge(v_kxT, g4 - 1)
                            ins = None
                            for j in range(4):
                                hi = r * 4 + j
                                ins = t.matmul(pst[:, j * 128:(j + 1) * 128],
                                               kxb[:, hi * 128:(hi + 1) * 128], identb[:],
                                               start=True, stop=True)
                            ins.then_inc(t_kxT, 1)

            _es7.close()
            _es8 = contextlib.ExitStack()
            wst0 = _es8.enter_context(nc.sbuf_tensor("wst0", [128, 512], F32))
            wst1 = _es8.enter_context(nc.sbuf_tensor("wst1", [128, 512], F32))
            wst2 = _es8.enter_context(nc.sbuf_tensor("wst2", [128, 512], F32))
            wst3 = _es8.enter_context(nc.sbuf_tensor("wst3", [128, 512], F32))
            wq32x0 = _es8.enter_context(nc.sbuf_tensor("wq32x0", [128, 512], F32))
            wq32x1 = _es8.enter_context(nc.sbuf_tensor("wq32x1", [128, 512], F32))
            wq32x2 = _es8.enter_context(nc.sbuf_tensor("wq32x2", [128, 512], F32))
            wq32x3 = _es8.enter_context(nc.sbuf_tensor("wq32x3", [128, 512], F32))
            wqi8a0 = _es8.enter_context(nc.sbuf_tensor("wqi8a", [128, 512], I8))
            wqi8a1 = _es8.enter_context(nc.sbuf_tensor("wqi8b", [128, 512], I8))
            wqb0 = _es8.enter_context(nc.sbuf_tensor("wqb0", [128, 512], BF16))
            wqb1 = _es8.enter_context(nc.sbuf_tensor("wqb1", [128, 512], BF16))
            wqb2 = _es8.enter_context(nc.sbuf_tensor("wqb2", [128, 512], BF16))
            wqb3 = _es8.enter_context(nc.sbuf_tensor("wqb3", [128, 512], BF16))
            hg = _es8.enter_context(nc.sbuf_tensor("hg", [128, 8 * 512], BF16))
            hu = _es8.enter_context(nc.sbuf_tensor("hu", [128, 8 * 512], BF16))
            ps_g = _es8.enter_context(nc.psum_tensor("ps_g", [128, 4 * 512], F32))
            ps_u = _es8.enter_context(nc.psum_tensor("ps_u", [128, 4 * 512], F32))
            blk5b = _es8.enter_context(nc.Block())
            if True:
                wst = [wst0, wst1, wst2, wst3]
                wq32x = [wq32x0, wq32x1, wq32x2, wq32x3]
                wqi8 = [wqi8a0, wqi8a1]
                wqb = [wqb0, wqb1, wqb2, wqb3]
                def g2glob(kg):
                    return (kg // 16) * 32 + (kg % 16)
                def u2glob(ku):
                    return (ku // 16) * 32 + 16 + (ku % 16)
                wgu_ldp = [S_("wgu_ld0"), S_("wgu_ld1")]
                wgu_up = [S_("wgu_up0"), S_("wgu_up1")]
                v_wq, s_wq = S_("v_wq"), S_("s_wq")
                t_gu, s_hg, s_hu, v_h = S_("t_gu"), S_("s_hg"), S_("s_hu"), S_("v_h")

                def s2glob(kq, wsel):
                    return (kq // 16) * 32 + wsel * 16 + (kq % 16)

                def bank(tt):
                    return (ps_g if tt < 4 else ps_u)[:, (tt % 4) * 512:(tt % 4 + 1) * 512]

                @blk5b.gpsimd
                def _(g):
                    g.wait_ge(S_("xp_st0"), 16 * (NT_OWN // 2))
                    g.wait_ge(S_("xp_st1"), 16 * (NT_OWN // 2))
                    g.wait_ge(S_("t_kxT"), 4 * NT_OWN)
                    g.wait_ge(S_("v_kxT"), 4 * NT_OWN)
                    kg = 0
                    for ic in range(16):
                        for hi in range(16):
                            if kg >= 2:
                                g.wait_ge(s_wq, s2glob(kg - 2, 0) + 1)
                            g.dma_start(out=wst[2 * (kg % 2)][:],
                                        in_=wgT[hi * 128:(hi + 1) * 128,
                                                ic * 512:(ic + 1) * 512]).then_inc(wgu_ldp[kg % 2], 16)
                            kg += 1

                @blk5b.sync
                def _(sync):
                    for q4 in range(4):
                        sync.wait_ge(S_(f"rs_st{q4}"), 16 * 64)
                    sync.wait_ge(S_("t_kxT"), 4 * NT_OWN)
                    ku = 0
                    for ic in range(16):
                        for hi in range(16):
                            if ku >= 2:
                                sync.wait_ge(s_wq, s2glob(ku - 2, 1) + 1)
                            sync.dma_start(out=wst[2 * (ku % 2) + 1][:],
                                           in_=wuT[hi * 128:(hi + 1) * 128,
                                                   ic * 512:(ic + 1) * 512]).then_inc(wgu_up[ku % 2], 16)
                            ku += 1

                @blk5b.vector
                def _(v):
                    v = _Drain(v)
                    v.wait_ge(S_("xp_st0"), 16 * (NT_OWN // 2))
                    v.wait_ge(S_("xp_st1"), 16 * (NT_OWN // 2))
                    v.wait_ge(S_("v_kx"), 3 * NT_OWN + 1)
                    k = 0
                    for ic in range(16):
                        if ic >= 1:
                            v.wait_ge(s_hu, 8 * ic)
                            for tt in range(8):
                                v.tensor_mul(hslab[:, tt * DI + (ic - 1) * 512:tt * DI + ic * 512],
                                             hg[:, tt * 512:(tt + 1) * 512], hu[:, tt * 512:(tt + 1) * 512])
                            v.engine_nop().then_inc(v_h, 1)
                        for wsel in range(2):
                            for hi in range(16):
                                v.wait_ge(s_wq, k + 1)
                                if k >= 4:
                                    v.wait_ge(t_gu, k - 3)
                                v.tensor_scalar(wqb[k % 4][:], wq32x[k % 4][:], 1.0, -1.0,
                                                ALU.min, ALU.max).then_inc(v_wq, 1)
                                k += 1
                    v.wait_ge(s_hu, 8 * 16)
                    for tt in range(8):
                        v.tensor_mul(hslab[:, tt * DI + 15 * 512:tt * DI + 16 * 512],
                                     hg[:, tt * 512:(tt + 1) * 512], hu[:, tt * 512:(tt + 1) * 512])
                    v.engine_nop().then_inc(v_h, 1)

                @blk5b.scalar
                def _(s):
                    s = _Drain(s)
                    s.wait_ge(S_("xp_st0"), 16 * (NT_OWN // 2))
                    s.wait_ge(S_("xp_st1"), 16 * (NT_OWN // 2))
                    k = 0
                    for ic in range(16):
                        for wsel in range(2):
                            for hi in range(16):
                                kq = ic * 16 + hi
                                stg = wst[2 * (kq % 2) + wsel]
                                sem = wgu_ldp if wsel == 0 else wgu_up
                                s.wait_ge(sem[kq % 2], 16 * (kq // 2 + 1))
                                if k >= 4:
                                    s.wait_ge(v_wq, k - 3)
                                s.activation(wq32x[k % 4][:], stg[:], AF.Identity,
                                             bias=mcol[:, 0:1], scale=swc[:, wsel:wsel + 1])
                                s.activation(wq32x[k % 4][:], wq32x[k % 4][:], AF.Identity,
                                             bias=nmcol[:, 0:1]).then_inc(s_wq, 1)
                                k += 1
                            if wsel == 0:
                                s.wait_ge(t_gu, ic * 32 + 16)
                                s.wait_ge(v_h, ic)
                                if debug and ic == 1:
                                    s.wait_ge(S_("tapsem"), 256)
                                for tt in range(8):
                                    s.activation(hg[:, tt * 512:(tt + 1) * 512], bank(tt),
                                                 AF.Silu, scale=gsc8[:, tt:tt + 1]).then_inc(s_hg, 1)
                        s.wait_ge(t_gu, ic * 32 + 32)
                        for tt in range(8):
                            s.activation(hu[:, tt * 512:(tt + 1) * 512], bank(tt),
                                         AF.Identity, scale=usc8[:, tt:tt + 1]).then_inc(s_hu, 1)
                        if debug and ic == 0:
                            s.dma_start(out=tap["hg"][:, :], in_=hg[:, 0:2048]).then_inc(S_("tapsem"), 16)
                            s.dma_start(out=tap["hu"][:, :], in_=hu[:, 0:2048]).then_inc(S_("tapsem"), 16)

                @blk5b.tensor
                def _(t):
                    k = 0
                    for ic in range(16):
                        for wsel in range(2):
                            for hi in range(16):
                                t.wait_ge(v_wq, k + 1)
                                ins = None
                                for tt in range(8):
                                    if hi == 0:
                                        if wsel == 0:
                                            if ic >= 1:
                                                t.wait_ge(s_hu, 8 * (ic - 1) + tt + 1)
                                        else:
                                            t.wait_ge(s_hg, 8 * ic + tt + 1)
                                    ins = t.matmul(bank(tt),
                                                   kxT[:, (hi * NT_OWN + tt) * 128:(hi * NT_OWN + tt + 1) * 128],
                                                   wqb[k % 4][:],
                                                   start=(hi == 0), stop=(hi == 15))
                                ins.then_inc(t_gu, 1)
                                k += 1

        _es8.close()
        _es6.close()
        _es9 = contextlib.ExitStack()
        lnwT = _es9.enter_context(nc.sbuf_tensor("lnwT", [128, DI], BF16))
        lnw32 = _es9.enter_context(nc.sbuf_tensor("lnw32", [1, 512], F32))
        sq8k = _es9.enter_context(nc.sbuf_tensor("sq8k", [128, DI], BF16))
        hsq8 = _es9.enter_context(nc.sbuf_tensor("hsq8", [128, 8], F32))
        mxh8 = _es9.enter_context(nc.sbuf_tensor("mxh8", [128, 8], F32))
        rs8 = _es9.enter_context(nc.sbuf_tensor("rs8", [128, 8], F32))
        f28 = _es9.enter_context(nc.sbuf_tensor("f28", [128, 8], F32))
        dscr = _es9.enter_context(nc.sbuf_tensor("dscr", [128, 8], F32))
        sqout = _es9.enter_context(nc.sbuf_tensor("sqout", [128, DI], BF16))
        k2f = _es9.enter_context(nc.sbuf_tensor("k2f", [128, DI // 2], F32))
        ktr0 = _es9.enter_context(nc.sbuf_tensor("ktr", [128, 512], BF16))
        ktr1 = _es9.enter_context(nc.sbuf_tensor("ktr1", [128, 512], BF16))
        ps_c0 = _es9.enter_context(nc.psum_tensor("ps_c0", [128, 512], F32))
        ps_c1 = _es9.enter_context(nc.psum_tensor("ps_c1", [128, 512], F32))
        blk5c = _es9.enter_context(nc.Block())
        if True:
            ktr = [ktr0, ktr1]
            ps_c = [ps_c0, ps_c1]
            lnw_ld, t_lnw, v_lnw = S_("lnw_ld"), S_("t_lnw"), S_("v_lnw")
            s_hsq, v_hlw, v_col5, s_ln5 = S_("s_hsq"), S_("v_hlw"), S_("v_col5"), S_("s_ln5")
            v_k2, s_k2 = S_("v_k2"), S_("s_k2")
            t_k2t, v_k2t = S_("t_k2t"), S_("v_k2t")
            k2t_stp = [S_("k2t_st0"), S_("k2t_st1")]

            @blk5c.gpsimd
            def _(g):
                g.wait_ge(S_("t_gu"), 512)
                g.wait_ge(S_("v_h"), 16)
                if debug:
                    g.wait_ge(S_("tapsem"), 224)
                for cidx in range(16):
                    if cidx >= 1:
                        g.wait_ge(t_lnw, cidx)
                    g.dma_start(out=lnw32[:], in_=lnw_in[cidx:cidx + 1, :]).then_inc(lnw_ld, 16)

            @blk5c.tensor
            def _(t):
                for cidx in range(16):
                    t.wait_ge(lnw_ld, 16 * (cidx + 1))
                    if cidx >= 2:
                        t.wait_ge(v_lnw, cidx - 1)
                    t.matmul(ps_c[cidx % 2][:, :], onesrow[0:1, 0:128], lnw32[0:1, :],
                             start=True, stop=True).then_inc(t_lnw, 1)
                t.wait_ge(v_lnw, 16)
                for tt in range(NT_OWN):
                    t.wait_ge(v_k2, tt + 1)
                    for r in range(16):
                        g4 = tt * 16 + r
                        if g4 >= 2:
                            t.wait_ge(v_k2t, g4 - 1)
                        ins = None
                        for j in range(4):
                            ii = r * 4 + j
                            ins = t.matmul(ps_c[g4 % 2][:, j * 128:(j + 1) * 128],
                                           hslab[:, tt * DI + ii * 128:tt * DI + (ii + 1) * 128],
                                           identb[:], start=True, stop=True)
                        ins.then_inc(t_k2t, 1)

            @blk5c.vector
            def _(v):
                v = _Drain(v)
                v.wait_ge(S_("v_h"), 16)
                for cidx in range(16):
                    v.wait_ge(t_lnw, cidx + 1)
                    v.tensor_copy(lnwT[:, cidx * 512:(cidx + 1) * 512], ps_c[cidx % 2][:, :]).then_inc(v_lnw, 1)
                for tt in range(NT_OWN):
                    v.tensor_mul(sq8k[:], hslab[:, tt * DI:(tt + 1) * DI], lnwT[:])
                    v.reduce_max(mxh8[:, tt:tt + 1], sq8k[:], AX.X, apply_absolute_value=True).then_inc(v_hlw, 1)
                v.wait_ge(S_("s_hsq"), 8)
                v.tensor_scalar(rs8[:], hsq8[:], 1.0 / DI, 1e-6, ALU.mult, ALU.add).then_inc(v_col5, 1)
                v.wait_ge(s_ln5, 2)
                v.wait_ge(S_("v_w1d"), 2)
                v.tensor_mul(dscr[:], f28[:], f28[:])
                v.tensor_tensor(dscr[:], dscr[:], rs8[:], ALU.mult)
                v.tensor_scalar(dscr[:], dscr[:], -0.5, 1.5, ALU.mult, ALU.add)
                v.tensor_mul(rs8[:], dscr[:], f28[:])
                v.tensor_mul(f28[:], rs8[:], mxh8[:])
                v.tensor_scalar_max(f28[:], f28[:], 1e-5)
                v.tensor_scalar(dsc8p[:], f28[:], wsc[:, 2:3], 1.0 / 127.0, ALU.mult, ALU.mult)
                v.tensor_scalar_mul(mxh8[:], rs8[:], 127.0)
                v.reciprocal(dscr[:], f28[:])
                v.tensor_mul(f28[:], mxh8[:], dscr[:]).then_inc(v_col5, 1)
                for tt in range(NT_OWN):
                    if debug and tt == 0:
                        v.wait_ge(S_("tapsem"), 272)
                    v.tensor_mul(sq8k[:], hslab[:, tt * DI:(tt + 1) * DI], lnwT[:])
                    for hf in range(2):
                        v.tensor_scalar(k2f[:], sq8k[:, hf * (DI // 2):(hf + 1) * (DI // 2)],
                                        f28[:, tt:tt + 1], 12582912.0, ALU.mult, ALU.add)
                        ins = v.tensor_scalar_add(hslab[:, tt * DI + hf * (DI // 2):tt * DI + (hf + 1) * (DI // 2)],
                                                  k2f[:], -12582912.0)
                    ins.then_inc(v_k2, 1)
                for tt in range(NT_OWN):
                    for r in range(16):
                        g4 = tt * 16 + r
                        v.wait_ge(t_k2t, g4 + 1)
                        if g4 >= 2:
                            v.wait_ge(k2t_stp[g4 % 2], 16 * (g4 // 2))
                        v.tensor_copy(ktr[g4 % 2][:], ps_c[g4 % 2][:, :]).then_inc(v_k2t, 1)

            @blk5c.scalar
            def _(s):
                s = _Drain(s)
                for tt in range(NT_OWN):
                    s.wait_ge(S_("v_h"), 16)
                    s.activation(sqout[:], hslab[:, tt * DI:(tt + 1) * DI], AF.Square,
                                 accum_out=hsq8[:, tt:tt + 1]).then_inc(s_hsq, 1)
                s.wait_ge(v_col5, 1)
                s.activation(f28[:], rs8[:], AF.Ln).then_inc(s_ln5, 1)
                s.activation(f28[:], f28[:], AF.Exp, scale=-0.5).then_inc(s_ln5, 1)
                s.wait_ge(S_("tapsem"), 272) if debug else None

            @blk5c.sync
            def _(sync):
                if debug:
                    sync.wait_ge(S_("s_hsq"), 8)
                    sync.dma_start(out=tap["h"][:, :], in_=hslab[:, 0:DI]).then_inc(S_("tapsem"), 16)
                for tt in range(NT_OWN):
                    for r in range(16):
                        g4 = tt * 16 + r
                        sync.wait_ge(v_k2t, g4 + 1)
                        dst = k2t_d[(r * 4) * 128:(r * 4 + 4) * 128,
                                    tt * 128:(tt + 1) * 128].rearrange("(n p) c -> p n c", p=128)
                        srcv = ktr[g4 % 2][:].rearrange("p (n c) -> p n c", n=4)
                        sync.dma_start(out=dst, in_=srcv).then_inc(k2t_stp[g4 % 2], 16)

        _es9.close()
    gslab = sb("gslab", [128, 8 * D], BF16)
    _es10 = contextlib.ExitStack()
    wst5a = _es10.enter_context(nc.sbuf_tensor("wst5a", [128, 512], F32))
    wst5b = _es10.enter_context(nc.sbuf_tensor("wst5b", [128, 512], F32))
    wq32dx0 = _es10.enter_context(nc.sbuf_tensor("wq32dx0", [128, 512], F32))
    wq32dx1 = _es10.enter_context(nc.sbuf_tensor("wq32dx1", [128, 512], F32))
    wqi8d0 = _es10.enter_context(nc.sbuf_tensor("wqi8d0", [128, 512], I8))
    wqi8d1 = _es10.enter_context(nc.sbuf_tensor("wqi8d1", [128, 512], I8))
    wqbd0 = _es10.enter_context(nc.sbuf_tensor("wqbd0", [128, 512], BF16))
    wqbd1 = _es10.enter_context(nc.sbuf_tensor("wqbd1", [128, 512], BF16))
    k2c0 = _es10.enter_context(nc.sbuf_tensor("k2c0", [128, 1024], BF16))
    k2c1 = _es10.enter_context(nc.sbuf_tensor("k2c1", [128, 1024], BF16))
    ps_dn0 = _es10.enter_context(nc.psum_tensor("ps_dn0", [128, 512], F32))
    ps_dn1 = _es10.enter_context(nc.psum_tensor("ps_dn1", [128, 512], F32))
    ps_dn2 = _es10.enter_context(nc.psum_tensor("ps_dn2", [128, 512], F32))
    ps_dn3 = _es10.enter_context(nc.psum_tensor("ps_dn3", [128, 512], F32))
    ps_dn4 = _es10.enter_context(nc.psum_tensor("ps_dn4", [128, 512], F32))
    ps_dn5 = _es10.enter_context(nc.psum_tensor("ps_dn5", [128, 512], F32))
    ps_dn6 = _es10.enter_context(nc.psum_tensor("ps_dn6", [128, 512], F32))
    ps_dn7 = _es10.enter_context(nc.psum_tensor("ps_dn7", [128, 512], F32))
    blk5e = _es10.enter_context(nc.Block())
    if True:
        wst5 = [wst5a, wst5b]
        wq32dx = [wq32dx0, wq32dx1]
        wqi8d = [wqi8d0, wqi8d1]
        wqbd = [wqbd0, wqbd1]
        k2c = [k2c0, k2c1]
        ps_dn = [ps_dn0, ps_dn1, ps_dn2, ps_dn3, ps_dn4, ps_dn5, ps_dn6, ps_dn7]
        wd2_ldp = [S_("wd2_ld0"), S_("wd2_ld1")]
        k2_ldp = [S_("k2_ld0"), S_("k2_ld1")]
        v_wq2, s_wq2 = S_("v_wq2"), S_("s_wq2")
        t_dn, s_g = S_("t_dn"), S_("s_g")

        @blk5e.gpsimd
        def _(g):
            g.wait_ge(S_("k2t_st0"), 16 * 64)
            g.wait_ge(S_("k2t_st1"), 16 * 64)
            k = 0
            for mc in range(4):
                for ii in range(64):
                    if k >= 2:
                        g.wait_ge(s_wq2, k - 1)
                    g.dma_start(out=wst5[k % 2][:],
                                in_=wdT[ii * 128:(ii + 1) * 128, mc * 512:(mc + 1) * 512]).then_inc(wd2_ldp[k % 2], 16)
                    k += 1

        @blk5e.sync
        def _(sync):
            sync.wait_ge(S_("k2t_st0"), 16 * 64)
            sync.wait_ge(S_("k2t_st1"), 16 * 64)
            k = 0
            for mc in range(4):
                for ii in range(64):
                    if k >= 2:
                        sync.wait_ge(t_dn, 8 * (k - 1))
                    sync.dma_start(out=k2c[k % 2][:], in_=k2t_d[ii * 128:(ii + 1) * 128, :]).then_inc(k2_ldp[k % 2], 16)
                    k += 1

        @blk5e.vector
        def _(v):
            v = _Drain(v)
            k = 0
            for mc in range(4):
                for ii in range(64):
                    v.wait_ge(s_wq2, k + 1)
                    if k >= 2:
                        v.wait_ge(t_dn, 8 * (k - 1))
                    v.tensor_scalar(wqbd[k % 2][:], wq32dx[k % 2][:], 1.0, -1.0,
                                    ALU.min, ALU.max).then_inc(v_wq2, 2)
                    k += 1

        @blk5e.scalar
        def _(s):
            s = _Drain(s)
            k = 0
            for mc in range(4):
                for ii in range(64):
                    s.wait_ge(wd2_ldp[k % 2], 16 * (k // 2 + 1))
                    if k >= 2:
                        s.wait_ge(v_wq2, 2 * (k - 1))
                    s.activation(wq32dx[k % 2][:], wst5[k % 2][:], AF.Identity,
                                 bias=mcol[:, 0:1], scale=swc[:, 2:3])
                    s.activation(wq32dx[k % 2][:], wq32dx[k % 2][:], AF.Identity,
                                 bias=nmcol[:, 0:1]).then_inc(s_wq2, 1)
                    k += 1
                if mc == 0:
                    s.wait_ge(S_("v_col5"), 2)
                s.wait_ge(t_dn, 8 * 64 * (mc + 1))
                ins = None
                for tt in range(8):
                    ins = s.activation(gslab[:, tt * D + mc * 512:tt * D + (mc + 1) * 512],
                                       ps_dn[tt][:, :], AF.Silu, scale=dsc8p[:, tt:tt + 1])
                ins.then_inc(s_g, 1)

        @blk5e.tensor
        def _(t):
            for mc in range(4):
                if mc >= 1:
                    t.wait_ge(s_g, mc)
                for ii in range(64):
                    k = mc * 64 + ii
                    t.wait_ge(v_wq2, 2 * (k + 1))
                    t.wait_ge(k2_ldp[k % 2], 16 * (k // 2 + 1))
                    ins = None
                    for tt in range(8):
                        ins = t.matmul(ps_dn[tt][:, :],
                                       k2c[k % 2][:, tt * 128:(tt + 1) * 128],
                                       wqbd[k % 2][:],
                                       start=(ii == 0), stop=(ii == 63))
                    ins.then_inc(t_dn, 8)

    _es10.close()
    _es11 = contextlib.ExitStack()
    rso = _es11.enter_context(nc.sbuf_tensor("rso", [128, D], BF16))
    xnp = _es11.enter_context(nc.sbuf_tensor("xnp", [128, D], BF16))
    xob = _es11.enter_context(nc.sbuf_tensor("xob", [128, D], F32))
    srow = _es11.enter_context(nc.sbuf_tensor("srow", [128, D], F32))
    sgrow = _es11.enter_context(nc.sbuf_tensor("sgrow", [128, D], F32))
    outb0 = _es11.enter_context(nc.sbuf_tensor("outb0", [128, D], F32))
    outb1 = _es11.enter_context(nc.sbuf_tensor("outb1", [128, D], F32))
    blk6 = _es11.enter_context(nc.Block())
    if True:
        outb = [outb0, outb1]
        rso_ld, v_out = S_("rso_ld"), S_("v_out")
        out_stp = [S_("out_st0"), S_("out_st1")]

        @blk6.sync
        def _(sync):
            sync.wait_ge(S_("g_coll"), 1)
            sync.wait_ge(S_("s_g"), 4)
            for tt in range(NT_OWN):
                if tt >= 1:
                    sync.wait_ge(v_out, 3 * tt)
                sync.dma_start(out=rso[:], in_=rs_out[tt * 128:(tt + 1) * 128, :]).then_inc(rso_ld, 16)
                sync.dma_start(out=xnp[:], in_=xn_park[tt * 128:(tt + 1) * 128, :]).then_inc(rso_ld, 16)
                sync.dma_start(out=xob[:], in_=x_own[tt * 128:(tt + 1) * 128, :]).then_inc(rso_ld, 16)
                sync.wait_ge(v_out, 3 * tt + 3)
                sync.dma_start(out=out[tt * 128:(tt + 1) * 128, :], in_=outb[tt % 2][:]).then_inc(out_stp[tt % 2], 16)
            if debug:
                tapsem = S_("tapsem")
                sync.dma_start(out=tap["xn"][:, :], in_=xn_park[:, :]).then_inc(tapsem, 16)
                sync.dma_start(out=tap["wsc"][:, 0:3], in_=wsc[:, 0:3]).then_inc(tapsem, 16)
                sync.dma_start(out=tap["wsc"][:, 4:7], in_=swc[:, 0:3]).then_inc(tapsem, 16)
                sync.dma_start(out=tap["rs"][:, :], in_=rs_out[:, :]).then_inc(tapsem, 16)
                for tt in range(NT_OWN):
                    sync.dma_start(out=tap["g"][tt * 128:(tt + 1) * 128, :],
                                   in_=gslab[:, tt * D:(tt + 1) * D]).then_inc(tapsem, 16)
                sync.wait_ge(tapsem, 272 + 16 * 12)
            sync.wait_ge(out_stp[0], 16 * (NT_OWN // 2))
            sync.wait_ge(out_stp[1], 16 * (NT_OWN // 2))

        @blk6.vector
        def _(v):
            v = _Drain(v)
            v.wait_ge(S_("s_g"), 4)
            for tt in range(NT_OWN):
                v.wait_ge(rso_ld, 16 * (3 * tt + 3))
                v.tensor_add(srow[:], rso[:], xnp[:]).then_inc(v_out, 1)
                v.tensor_mul(sgrow[:], srow[:], gslab[:, tt * D:(tt + 1) * D]).then_inc(v_out, 1)
                if tt >= 2:
                    v.wait_ge(out_stp[tt % 2], 16 * (tt // 2))
                v.tensor_add(outb[tt % 2][:], sgrow[:], xob[:]).then_inc(v_out, 1)

    _es11.close()
    return nc, tap


def _host_constants():
    d = 64
    inv = 1.0 / (10000.0 ** (np.arange(0, d, 2, dtype=np.float64) / d))
    pos = np.arange(S, dtype=np.float64)
    ang = pos[:, None] * inv[None, :]
    cos = np.cos(ang).astype(np.float32)
    sin = np.sin(ang).astype(np.float32)
    tc = np.zeros((128, 32 * NTB), np.float32)
    ts_ = np.zeros((128, 32 * NTB), np.float32)
    for pt in range(NTB):
        tc[:, pt * 32:(pt + 1) * 32] = cos[pt * 128:(pt + 1) * 128, :]
        ts_[:, pt * 32:(pt + 1) * 32] = sin[pt * 128:(pt + 1) * 128, :]
    mk = np.zeros((128, 4 * 512), np.float32)
    for r in range(4):
        p = np.arange(128)[:, None]
        f = np.arange(512)[None, :]
        mk[:, r * 512:(r + 1) * 512] = np.where(r * 128 + p > f, -1.0e30, 0.0)
    ident = np.eye(128, dtype=np.float32)
    invc = (1.0 / (np.arange(S, dtype=np.float64) + 1.0)).astype(ml_dtypes.bfloat16).reshape(1, S)
    return tc, ts_, mk, ident, invc


def kernel(L0_attn_x, L0_attn_w_in, L0_attn_w_out, L0_attn_rms_w,
           L0_ffn_w_gate, L0_ffn_w_up, L0_ffn_w_down, L0_ffn_ln_weight):
    x = np.ascontiguousarray(np.asarray(L0_attn_x, dtype=np.float32).reshape(T_ALL, D))
    w_in = np.asarray(L0_attn_w_in, dtype=np.float32)
    w_out = np.asarray(L0_attn_w_out, dtype=np.float32)
    rms_w = np.asarray(L0_attn_rms_w, dtype=np.float32)
    wg = np.asarray(L0_ffn_w_gate, dtype=np.float32)
    wu = np.asarray(L0_ffn_w_up, dtype=np.float32)
    wd = np.asarray(L0_ffn_w_down, dtype=np.float32)
    lnw = np.asarray(L0_ffn_ln_weight, dtype=np.float32)

    tc, ts_, mk, ident, invc = _host_constants()
    wgT = np.ascontiguousarray(wg.T)
    wuT = np.ascontiguousarray(wu.T)
    wdT = np.ascontiguousarray(wd.T)
    rms_colh = np.ascontiguousarray(rms_w.reshape(16, 128).T)
    lnw_h = np.ascontiguousarray(lnw.reshape(16, 512))

    nc, _ = build_nc(debug=False)
    in_maps = []
    for g in range(8):
        rows = np.concatenate([w_in[g * 256:(g + 1) * 256],
                               w_in[2048 + g * 64:2048 + (g + 1) * 64],
                               w_in[2560 + g * 64:2560 + (g + 1) * 64]], axis=0)
        w_inT_g = np.ascontiguousarray(rows.T)
        w_outT_g = np.ascontiguousarray(w_out[:, g * 256:(g + 1) * 256].T)
        in_maps.append({
            "x_all": x,
            "x_own": np.ascontiguousarray(x[g * T_OWN:(g + 1) * T_OWN]),
            "w_inT": w_inT_g,
            "w_outT": w_outT_g,
            "rms_col": rms_colh,
            "wgT": wgT, "wuT": wuT, "wdT": wdT,
            "lnw_in": lnw_h,
            "trig_cos": tc, "trig_sin": ts_,
            "masks": mk, "ident_i": ident, "invcnt": invc,
        })
    res = run_bass_kernel_spmd(nc, in_maps, core_ids=list(range(8)))
    outs = [res.results[g]["out"] for g in range(8)]
    return np.concatenate(outs, axis=0).reshape(B, S, D).astype(np.float32)


# version 07
# speedup vs baseline: 1.2777x; 1.0097x over previous
import numpy as np
import ml_dtypes
import contextlib
import concourse.bass as bass
import concourse.mybir as mybir
from concourse.bass_utils import run_bass_kernel_spmd

F32 = mybir.dt.float32
BF16 = mybir.dt.bfloat16
I8 = mybir.dt.int8
AF = mybir.ActivationFunctionType
ALU = mybir.AluOpType
AX = mybir.AxisListType

B = 2
S = 4096
D = 2048
NTB = S // 128
NT_ALL = B * NTB
T_ALL = B * S
T_OWN = 1024
NT_OWN = 8
CC = 384
DI = 8192




_DRAIN_OPS = {
    "tensor_copy", "tensor_tensor", "tensor_add", "tensor_sub", "tensor_mul",
    "tensor_max", "tensor_scalar", "tensor_single_scalar", "tensor_scalar_add",
    "tensor_scalar_sub", "tensor_scalar_mul", "tensor_scalar_max", "tensor_scalar_min",
    "tensor_relu", "tensor_reduce", "reduce_max", "reduce_sum", "scalar_tensor_tensor",
    "tensor_tensor_scan", "memset", "activation", "copy", "mul", "add", "square",
    "sqrt", "sign", "reciprocal", "select", "bn_stats", "bn_aggr", "transpose",
    "stream_shuffle", "pool", "pool_max", "pool_avg", "copy_predicated", "memzero",
}


class _Drain:

    def __init__(self, eng):
        self._e = eng

    def __getattr__(self, n):
        f = getattr(self._e, n)
        if n in _DRAIN_OPS:
            def wrapped(*a, **kw):
                self._e.drain()
                return f(*a, **kw)
            return wrapped
        return f


def build_nc(debug=False):
    nc = bass.Bass("TRN2", target_bir_lowering=False, debug=False)

    x_all = nc.dram_tensor("x_all", [T_ALL, D], F32, kind="ExternalInput").ap()
    x_own = nc.dram_tensor("x_own", [T_OWN, D], F32, kind="ExternalInput").ap()
    w_inT = nc.dram_tensor("w_inT", [D, CC], F32, kind="ExternalInput").ap()
    w_outT = nc.dram_tensor("w_outT", [256, D], F32, kind="ExternalInput").ap()
    rms_col = nc.dram_tensor("rms_col", [128, 16], F32, kind="ExternalInput").ap()
    wgT = nc.dram_tensor("wgT", [D, DI], F32, kind="ExternalInput").ap()
    wuT = nc.dram_tensor("wuT", [D, DI], F32, kind="ExternalInput").ap()
    wdT = nc.dram_tensor("wdT", [DI, D], F32, kind="ExternalInput").ap()
    lnw_in = nc.dram_tensor("lnw_in", [128, DI], F32, kind="ExternalInput").ap()
    trig_cos = nc.dram_tensor("trig_cos", [128, 32 * NTB], F32, kind="ExternalInput").ap()
    trig_sin = nc.dram_tensor("trig_sin", [128, 32 * NTB], F32, kind="ExternalInput").ap()
    masks = nc.dram_tensor("masks", [128, 4 * 512], F32, kind="ExternalInput").ap()
    ident_i = nc.dram_tensor("ident_i", [128, 128], F32, kind="ExternalInput").ap()
    invcnt = nc.dram_tensor("invcnt", [1, S], BF16, kind="ExternalInput").ap()

    out = nc.dram_tensor("out", [T_OWN, D], F32, kind="ExternalOutput").ap()

    rs_in = nc.dram_tensor("rs_in", [T_ALL, D], BF16, kind="Internal").ap()
    rs_out = nc.dram_tensor("rs_out", [T_OWN, D], BF16, kind="Internal").ap()
    k2t_d = nc.dram_tensor("k2t_d", [DI, T_OWN], BF16, kind="Internal").ap()
    xn_park = nc.dram_tensor("xn_park", [T_OWN, D], BF16, kind="Internal").ap()

    tap = {}
    if debug:
        tap["qaug"] = nc.dram_tensor("tap_qaug", [8 * 65, S], BF16, kind="ExternalOutput").ap()
        tap["kaug"] = nc.dram_tensor("tap_kaug", [B * 65, S], BF16, kind="ExternalOutput").ap()
        tap["v"] = nc.dram_tensor("tap_v", [B * 128, NTB * 65], BF16, kind="ExternalOutput").ap()
        tap["oT"] = nc.dram_tensor("tap_oT", [2 * 128, T_ALL], BF16, kind="ExternalOutput").ap()
        tap["xn"] = nc.dram_tensor("tap_xn", [T_OWN, D], BF16, kind="ExternalOutput").ap()
        tap["g"] = nc.dram_tensor("tap_g", [T_OWN, D], BF16, kind="ExternalOutput").ap()
        tap["wsc"] = nc.dram_tensor("tap_wsc", [128, 8], F32, kind="ExternalOutput").ap()
        tap["rs"] = nc.dram_tensor("tap_rs", [T_OWN, D], BF16, kind="ExternalOutput").ap()
        tap["h"] = nc.dram_tensor("tap_h", [128, DI], BF16, kind="ExternalOutput").ap()
        tap["kx"] = nc.dram_tensor("tap_kx", [128, 16 * NT_OWN * 128], BF16, kind="ExternalOutput").ap()
        tap["hg"] = nc.dram_tensor("tap_hg", [128, 4 * 512], BF16, kind="ExternalOutput").ap()
        tap["wq"] = nc.dram_tensor("tap_wq", [128, 512], BF16, kind="ExternalOutput").ap()
        tap["hu"] = nc.dram_tensor("tap_hu", [128, 4 * 512], BF16, kind="ExternalOutput").ap()

    sb = nc.alloc_sbuf_tensor
    sems = {}

    def S_(name):
        if name not in sems:
            sems[name] = nc.alloc_semaphore(name)
        return sems[name]

    identb = sb("identb", [128, 128], BF16)
    ones_r = sb("ones_r", [1, 128], BF16)
    onesrow = sb("onesrow", [1, 512], F32)
    onescol = sb("onescol", [128, 1], F32)
    onescolb = sb("onescolb", [128, 1], BF16)
    c127 = sb("c127", [128, 1], F32)
    mcol = sb("mcol", [128, 1], F32)
    nmcol = sb("nmcol", [128, 1], F32)
    expb = sb("expb", [128, 1], F32)
    wsum = sb("wsum", [128, 192], F32)
    w1s = sb("w1s", [1, 4], F32)
    wsc = sb("wsc", [128, 4], F32)
    swc = sb("swc", [128, 4], F32)
    dsc8p = sb("dsc8p", [128, 8], F32)

    units = [(b, h, c) for b in range(B) for h in range(4) for c in range(8)]
    NU = len(units)
    grp_of = [2 * c + 2 for (_, _, c) in units]
    cum_grp = np.concatenate([[0], np.cumsum(grp_of)]).astype(int).tolist()
    NGRP = cum_grp[-1]

    _es0 = contextlib.ExitStack()
    masks_sb = _es0.enter_context(nc.sbuf_tensor("masks_sb", [128, 4 * 512], F32))
    invcnt_sb = _es0.enter_context(nc.sbuf_tensor("invcnt_sb", [1, S], BF16))
    winb = _es0.enter_context(nc.sbuf_tensor("winb", [128, 16 * CC], BF16))
    rmsc = _es0.enter_context(nc.sbuf_tensor("rmsc", [128, 16], F32))
    qa0 = _es0.enter_context(nc.sbuf_tensor("qa0", [65, S], BF16))
    qa1 = _es0.enter_context(nc.sbuf_tensor("qa1", [65, S], BF16))
    qa2 = _es0.enter_context(nc.sbuf_tensor("qa2", [65, S], BF16))
    qa3 = _es0.enter_context(nc.sbuf_tensor("qa3", [65, S], BF16))
    qa4 = _es0.enter_context(nc.sbuf_tensor("qa4", [65, S], BF16))
    qa5 = _es0.enter_context(nc.sbuf_tensor("qa5", [65, S], BF16))
    qa6 = _es0.enter_context(nc.sbuf_tensor("qa6", [65, S], BF16))
    qa7 = _es0.enter_context(nc.sbuf_tensor("qa7", [65, S], BF16))
    ka0 = _es0.enter_context(nc.sbuf_tensor("ka0", [65, S], BF16))
    ka1 = _es0.enter_context(nc.sbuf_tensor("ka1", [65, S], BF16))
    vs0 = _es0.enter_context(nc.sbuf_tensor("vs0", [128, NTB * 65], BF16))
    vs1 = _es0.enter_context(nc.sbuf_tensor("vs1", [128, NTB * 65], BF16))
    if True:
        q_aug = [qa0, qa1, qa2, qa3, qa4, qa5, qa6, qa7]
        k_aug = [ka0, ka1]
        v_sb = [vs0, vs1]

        _es1 = contextlib.ExitStack()
        trigc = _es1.enter_context(nc.sbuf_tensor("trigc", [128, 32 * NTB], F32))
        trigs = _es1.enter_context(nc.sbuf_tensor("trigs", [128, 32 * NTB], F32))
        ident32 = _es1.enter_context(nc.sbuf_tensor("ident32", [128, 128], F32))
        win32 = _es1.enter_context(nc.sbuf_tensor("win32", [128, CC], F32))
        xt0 = _es1.enter_context(nc.sbuf_tensor("xt0", [128, D], F32))
        xt1 = _es1.enter_context(nc.sbuf_tensor("xt1", [128, D], F32))
        xt2 = _es1.enter_context(nc.sbuf_tensor("xt2", [128, D], F32))
        xt3 = _es1.enter_context(nc.sbuf_tensor("xt3", [128, D], F32))
        xnt0 = _es1.enter_context(nc.sbuf_tensor("xnt0", [128, D], BF16))
        xnt1 = _es1.enter_context(nc.sbuf_tensor("xnt1", [128, D], BF16))
        xnT = _es1.enter_context(nc.sbuf_tensor("xnT", [128, 16 * 128], BF16))
        sqs = _es1.enter_context(nc.sbuf_tensor("sqs", [128, D], BF16))
        stat = _es1.enter_context(nc.sbuf_tensor("stat", [128, 56], F32))
        ropeq0 = _es1.enter_context(nc.sbuf_tensor("ropeq0", [128, 256], BF16))
        ropeq1 = _es1.enter_context(nc.sbuf_tensor("ropeq1", [128, 256], BF16))
        ropek0 = _es1.enter_context(nc.sbuf_tensor("ropek0", [128, 64], BF16))
        ropek1 = _es1.enter_context(nc.sbuf_tensor("ropek1", [128, 64], BF16))
        rsc1 = _es1.enter_context(nc.sbuf_tensor("rsc1", [128, 128], F32))
        rsc2 = _es1.enter_context(nc.sbuf_tensor("rsc2", [128, 128], F32))
        pw0 = _es1.enter_context(nc.sbuf_tensor("pw0", [128, 2048], F32))
        pw1 = _es1.enter_context(nc.sbuf_tensor("pw1", [128, 2048], F32))
        pabs = _es1.enter_context(nc.sbuf_tensor("pabs", [128, 2048], BF16))
        ps_tr0 = _es1.enter_context(nc.psum_tensor("ps_tr0", [128, 512], F32))
        ps_tr1 = _es1.enter_context(nc.psum_tensor("ps_tr1", [128, 512], F32))
        ps_y = _es1.enter_context(nc.psum_tensor("ps_y", [128, CC], F32))
        ps_qt = _es1.enter_context(nc.psum_tensor("ps_qt", [64, 512], F32))
        ps_kt = _es1.enter_context(nc.psum_tensor("ps_kt", [64, 128], F32))
        blk = _es1.enter_context(nc.Block())
        if True:
            xt = [xt0, xt1, xt2, xt3]
            xnt = [xnt0, xnt1]
            ropeq = [ropeq0, ropeq1]
            ropek = [ropek0, ropek1]
            c_ld, win_ld = S_("c_ld"), S_("win_ld")
            x_ldp = [S_("x_ld0"), S_("x_ld1"), S_("x_ld2"), S_("x_ld3")]
            s_le, s_xn = S_("s_le"), S_("s_xn")
            v_cst, win_cv = S_("v_cst"), S_("win_cv")
            s_sq, v_st, v_red = S_("s_sq"), S_("v_st"), S_("v_red")
            t_tr, v_xnT, t_y = S_("t_tr"), S_("v_xnT"), S_("t_y")
            v_rope, t_qk, v_qkev = S_("v_rope"), S_("t_qk"), S_("v_qkev")
            gu_ldp = [S_("gu_ld0"), S_("gu_ld1")]
            s_pw = S_("s_pw")

            @blk.sync
            def _(sync):
                for dst, src in [(trigc[:], trig_cos[:]), (trigs[:], trig_sin[:]),
                                 (masks_sb[:], masks[:]), (invcnt_sb[:], invcnt[:]),
                                 (rmsc[:], rms_col[:]), (ident32[:], ident_i[:])]:
                    sync.dma_start(out=dst, in_=src).then_inc(c_ld, 16)
                for ti in range(NT_ALL):
                    if ti >= 4:
                        sync.wait_ge(s_xn, ti - 3)
                        sync.wait_ge(v_red, ti - 3)
                    sync.dma_start(out=xt[ti % 4][:], in_=x_all[ti * 128:(ti + 1) * 128, :]).then_inc(x_ldp[ti % 4], 16)

            @blk.gpsimd
            def _(g):
                for mi in range(16):
                    if mi >= 1:
                        g.wait_ge(win_cv, mi)
                    g.dma_start(out=win32[:], in_=w_inT[mi * 128:(mi + 1) * 128, :]).then_inc(win_ld, 16)
                ch = 0
                for w_ap in (wgT, wuT):
                    for r in range(16):
                        for cc2 in range(4):
                            if ch >= 2:
                                g.wait_ge(s_pw, ch - 1)
                            g.dma_start(out=(pw0 if ch % 2 == 0 else pw1)[:],
                                        in_=w_ap[r * 128:(r + 1) * 128, cc2 * 2048:(cc2 + 1) * 2048]).then_inc(gu_ldp[ch % 2], 16)
                            ch += 1

            @blk.vector
            def _(v):
                v = _Drain(v)
                v.wait_ge(c_ld, 16 * 6)
                v.memset(ones_r[:], 1.0)
                v.memset(onesrow[:], 1.0)
                v.memset(onescol[:], 1.0)
                v.memset(onescolb[:], 1.0)
                v.memset(c127[:], 127.0)
                v.memset(mcol[:], 12582912.0)
                v.memset(nmcol[:], -12582912.0)
                v.memset(expb[:], -40.0)
                v.tensor_copy(identb[:], ident32[:])
                for b in range(B):
                    v.memset(k_aug[b][64:65, :], 1.0)
                    v.memset(v_sb[b].rearrange("p (n c) -> p n c", c=65)[:, :, 64:65], 1.0)
                v.memset(wsum[:], 0.0).then_inc(v_cst, 1)
                for mi in range(16):
                    v.wait_ge(win_ld, 16 * (mi + 1))
                    v.tensor_scalar(winb[:, mi * CC:(mi + 1) * CC], win32[:],
                                    rmsc[:, mi:mi + 1], None, ALU.mult).then_inc(win_cv, 1)

                for ti in range(NT_ALL):
                    Q = ti // 4
                    BQ = 28 * (Q % 2)
                    if ti % 4 == 0:
                        for tj in range(4):
                            v.wait_ge(x_ldp[tj], 16 * (Q + 1))
                            v.reduce_sum(stat[:, BQ + tj:BQ + tj + 1], xt[tj][:], AX.X).then_inc(v_red, 1)
                        v.wait_ge(s_sq, 4 * (Q + 1))
                        mu = stat[:, BQ + 8:BQ + 12]
                        t2 = stat[:, BQ + 12:BQ + 16]
                        rec = stat[:, BQ + 16:BQ + 20]
                        c0 = stat[:, BQ + 20:BQ + 24]
                        nb = stat[:, BQ + 24:BQ + 28]
                        v.tensor_scalar_mul(mu, stat[:, BQ + 0:BQ + 4], 1.0 / D)
                        v.tensor_scalar_mul(t2, stat[:, BQ + 4:BQ + 8], 1.0 / D)
                        v.tensor_mul(rec, mu, mu)
                        v.tensor_sub(t2, t2, rec)
                        v.tensor_scalar_add(rec, t2, 1e-5)
                        v.scalar_tensor_tensor(t2, rec, 1e-6, t2, ALU.mult, ALU.add)
                        v.reciprocal(rec, t2).then_inc(v_st, 1)
                        v.wait_ge(s_le, Q + 1)
                        v.tensor_mul(nb, c0, c0)
                        v.scalar_tensor_tensor(nb, nb, -0.5, t2, ALU.mult, ALU.mult)
                        v.tensor_scalar_add(nb, nb, 1.5)
                        v.tensor_mul(c0, nb, c0)
                        v.scalar_tensor_tensor(nb, mu, -1.0, c0, ALU.mult, ALU.mult).then_inc(v_st, 1)

                    for r in range(4):
                        g4 = 4 * ti + r
                        pst = [ps_tr0, ps_tr1][g4 % 2]
                        v.wait_ge(t_tr, g4 + 1)
                        v.tensor_copy(xnT[:, r * 512:(r + 1) * 512], pst[:]).then_inc(v_xnT, 1)

                    v.wait_ge(t_y, ti + 1)
                    b = ti // NTB
                    pt = ti % NTB
                    cos4 = trigc[:, pt * 32:(pt + 1) * 32].unsqueeze(1).to_broadcast([128, 4, 32])
                    sin4 = trigs[:, pt * 32:(pt + 1) * 32].unsqueeze(1).to_broadcast([128, 4, 32])
                    ya = ps_y[:, 0:256].rearrange("p (h x) -> p h x", h=4)
                    rqv = ropeq[ti % 2][:].rearrange("p (h x) -> p h x", h=4)
                    r1 = rsc1[:].rearrange("p (h x) -> p h x", h=4)[:, :, 0:32]
                    r2 = rsc2[:].rearrange("p (h x) -> p h x", h=4)[:, :, 0:32]
                    v.tensor_tensor(r1, ya[:, :, 0:32], cos4, ALU.mult)
                    v.tensor_tensor(r2, ya[:, :, 32:64], sin4, ALU.mult)
                    v.tensor_sub(rqv[:, :, 0:32], r1, r2)
                    v.tensor_tensor(r1, ya[:, :, 32:64], cos4, ALU.mult)
                    v.tensor_tensor(r2, ya[:, :, 0:32], sin4, ALU.mult)
                    v.tensor_add(rqv[:, :, 32:64], r1, r2)
                    cos1 = trigc[:, pt * 32:(pt + 1) * 32]
                    sin1 = trigs[:, pt * 32:(pt + 1) * 32]
                    rk = ropek[ti % 2]
                    v.tensor_tensor(rsc1[:, 0:32], ps_y[:, 256:288], cos1, ALU.mult)
                    v.tensor_tensor(rsc2[:, 0:32], ps_y[:, 288:320], sin1, ALU.mult)
                    v.tensor_sub(rk[:, 0:32], rsc1[:, 0:32], rsc2[:, 0:32])
                    v.tensor_tensor(rsc1[:, 0:32], ps_y[:, 288:320], cos1, ALU.mult)
                    v.tensor_tensor(rsc2[:, 0:32], ps_y[:, 256:288], sin1, ALU.mult)
                    v.tensor_add(rk[:, 32:64], rsc1[:, 0:32], rsc2[:, 0:32])
                    v.tensor_copy(v_sb[b][:, pt * 65:pt * 65 + 64], ps_y[:, 320:384]).then_inc(v_rope, 1)

                    v.wait_ge(t_qk, ti + 1)
                    for h in range(4):
                        v.tensor_copy(q_aug[b * 4 + h][0:64, pt * 128:(pt + 1) * 128],
                                      ps_qt[0:64, h * 128:(h + 1) * 128])
                    v.tensor_copy(k_aug[b][0:64, pt * 128:(pt + 1) * 128], ps_kt[:, :]).then_inc(v_qkev, 1)

            @blk.scalar
            def _(s):
                s = _Drain(s)
                s.wait_ge(v_cst, 1)
                ch = 0
                for Q in range(NT_ALL // 4):
                    BQ = 28 * (Q % 2)
                    for tj in range(4):
                        s.wait_ge(x_ldp[tj], 16 * (Q + 1))
                        s.activation(sqs[:], xt[tj][:], AF.Square,
                                     accum_out=stat[:, BQ + 4 + tj:BQ + 5 + tj]).then_inc(s_sq, 1)
                    s.wait_ge(v_st, 2 * Q + 1)
                    s.activation(stat[:, BQ + 20:BQ + 24], stat[:, BQ + 16:BQ + 20], AF.Ln)
                    s.activation(stat[:, BQ + 20:BQ + 24], stat[:, BQ + 20:BQ + 24], AF.Exp,
                                 scale=-0.5).then_inc(s_le, 1)
                    s.wait_ge(v_st, 2 * Q + 2)
                    for tj in range(4):
                        ti = 4 * Q + tj
                        if ti >= 2:
                            s.wait_ge(t_tr, 4 * (ti - 1))
                        s.activation(xnt[ti % 2][:], xt[tj][:], AF.Identity,
                                     bias=stat[:, BQ + 24 + tj:BQ + 25 + tj],
                                     scale=stat[:, BQ + 20 + tj:BQ + 21 + tj]).then_inc(s_xn, 1)
                        for _2 in range(2):
                            if ch < 128:
                                s.wait_ge(gu_ldp[ch % 2], 16 * (ch // 2 + 1))
                                s.activation(pabs[:], (pw0 if ch % 2 == 0 else pw1)[:], AF.Abs,
                                             accum_out=wsum[:, ch:ch + 1]).then_inc(s_pw, 1)
                                ch += 1

            @blk.tensor
            def _(t):
                t.wait_ge(v_cst, 1)
                t.wait_ge(win_cv, 16)
                for ti in range(NT_ALL):
                    t.wait_ge(s_xn, ti + 1)
                    for r in range(4):
                        g4 = 4 * ti + r
                        pst = [ps_tr0, ps_tr1][g4 % 2]
                        if g4 >= 2:
                            t.wait_ge(v_xnT, g4 - 1)
                        ins = None
                        for j in range(4):
                            mi = r * 4 + j
                            ins = t.matmul(pst[:, j * 128:(j + 1) * 128],
                                           xnt[ti % 2][:, mi * 128:(mi + 1) * 128], identb[:],
                                           start=True, stop=True)
                        ins.then_inc(t_tr, 1)
                    t.wait_ge(v_xnT, 4 * ti + 4)
                    if ti >= 1:
                        t.wait_ge(v_rope, ti)
                    ins = None
                    for mi in range(16):
                        ins = t.matmul(ps_y[:], xnT[:, mi * 128:(mi + 1) * 128],
                                       winb[:, mi * CC:(mi + 1) * CC], start=(mi == 0), stop=(mi == 15))
                    ins.then_inc(t_y, 1)
                    t.wait_ge(v_rope, ti + 1)
                    if ti >= 1:
                        t.wait_ge(v_qkev, ti)
                    for h in range(4):
                        t.matmul(ps_qt[0:64, h * 128:(h + 1) * 128],
                                 ropeq[ti % 2][:, h * 64:(h + 1) * 64], identb[:],
                                 start=True, stop=True)
                    t.matmul(ps_kt[:, :], ropek[ti % 2][:], identb[:],
                             start=True, stop=True).then_inc(t_qk, 1)

        _es1.close()
        _es2 = contextlib.ExitStack()
        kpre = _es2.enter_context(nc.sbuf_tensor("kpre", [64, S], F32))
        dotbf = _es2.enter_context(nc.sbuf_tensor("dotbf", [64, S], BF16))
        wred = _es2.enter_context(nc.sbuf_tensor("wred", [128, 2], F32))
        ps_d0 = _es2.enter_context(nc.psum_tensor("ps_d0", [1, 512], F32))
        ps_d1 = _es2.enter_context(nc.psum_tensor("ps_d1", [1, 512], F32))
        ps_w1 = _es2.enter_context(nc.psum_tensor("ps_w1", [1, 2], F32))
        ps_bc = _es2.enter_context(nc.psum_tensor("ps_bc", [128, 2], F32))
        blk2 = _es2.enter_context(nc.Block())
        if True:
            v_scan, t_dot, v_aug = S_("v_scan"), S_("t_dot"), S_("v_aug")
            v_wred, t_w1, v_w1 = S_("v_wred"), S_("t_w1"), S_("v_w1")

            @blk2.vector
            def _(v):
                v = _Drain(v)
                v.wait_ge(S_("s_pw"), 128)
                v.wait_ge(S_("t_qk"), NT_ALL)
                v.reduce_sum(wred[:, 0:1], wsum[:, 0:64], AX.X)
                v.reduce_sum(wred[:, 1:2], wsum[:, 64:128], AX.X).then_inc(v_wred, 2)
                v.wait_ge(t_w1, 2)
                v.tensor_scalar_mul(w1s[:, 0:2], ps_w1[0:1, 0:2], 1.0 / (D * DI))
                v.tensor_scalar_max(w1s[:, 0:2], w1s[:, 0:2], 1e-5).then_inc(v_w1, 1)
                v.wait_ge(t_w1, 4)
                v.tensor_copy(wsc[:, 0:2], ps_bc[:, 0:2])
                v.reciprocal(swc[:, 0:2], wsc[:, 0:2]).then_inc(v_w1, 1)

                v.wait_ge(S_("v_qkev"), NT_ALL)
                for b in range(B):
                    v.tensor_tensor_scan(kpre[:], onescol[0:64, 0:1].to_broadcast([64, S]),
                                         k_aug[b][0:64, :], 0.0, ALU.mult, ALU.add).then_inc(v_scan, 1)
                    for h in range(4):
                        i = b * 4 + h
                        if i > 0:
                            v.wait_ge(t_dot, i * 8)
                        v.tensor_mul(dotbf[:], q_aug[i][0:64, :], kpre[:]).then_inc(v_scan, 1)
                        for c in range(8):
                            v.wait_ge(t_dot, i * 8 + c + 1)
                            psd = [ps_d0, ps_d1][c % 2]
                            v.scalar_tensor_tensor(q_aug[i][64:65, c * 512:(c + 1) * 512],
                                                   psd[0:1, :], -1.0,
                                                   invcnt_sb[0:1, c * 512:(c + 1) * 512],
                                                   ALU.mult, ALU.mult).then_inc(v_aug, 1)

            @blk2.tensor
            def _(t):
                t.wait_ge(v_wred, 2)
                t.matmul(ps_w1[0:1, 0:1], wred[:, 0:1], onescol[:, 0:1], start=True, stop=True)
                t.matmul(ps_w1[0:1, 1:2], wred[:, 1:2], onescol[:, 0:1], start=True, stop=True).then_inc(t_w1, 2)
                t.wait_ge(v_w1, 1)
                t.matmul(ps_bc[:, 0:1], onesrow[0:1, 0:128], w1s[0:1, 0:1], start=True, stop=True)
                t.matmul(ps_bc[:, 1:2], onesrow[0:1, 0:128], w1s[0:1, 1:2], start=True, stop=True).then_inc(t_w1, 2)
                for b in range(B):
                    for h in range(4):
                        i = b * 4 + h
                        t.wait_ge(S_("v_scan"), b + i + 2)
                        for c in range(8):
                            psd = [ps_d0, ps_d1][c % 2]
                            if c >= 2:
                                t.wait_ge(S_("v_aug"), i * 8 + c - 1)
                            t.matmul(psd[0:1, :], onescolb[0:64, 0:1], dotbf[:, c * 512:(c + 1) * 512],
                                     start=True, stop=True).then_inc(t_dot, 1)

        _es2.close()
        _es3 = contextlib.ExitStack()
        oT0 = _es3.enter_context(nc.sbuf_tensor("oT0", [128, T_ALL], BF16))
        oT1 = _es3.enter_context(nc.sbuf_tensor("oT1", [128, T_ALL], BF16))
        if True:
            oT = [oT0, oT1]
            _es4 = contextlib.ExitStack()
            pt0 = _es4.enter_context(nc.sbuf_tensor("pt0", [128, 1024], BF16))
            pt1 = _es4.enter_context(nc.sbuf_tensor("pt1", [128, 1024], BF16))
            rden = _es4.enter_context(nc.sbuf_tensor("rden", [1, 512], BF16))
            rdenf = _es4.enter_context(nc.sbuf_tensor("rdenf", [1, 512], F32))
            sce = _es4.enter_context(nc.sbuf_tensor("sce", [64, 512], BF16))
            tmpo0 = _es4.enter_context(nc.sbuf_tensor("tmpo0", [64, 512], BF16))
            tmpo1 = _es4.enter_context(nc.sbuf_tensor("tmpo1", [64, 512], BF16))
            pwd0 = _es4.enter_context(nc.sbuf_tensor("pwd0", [128, 2048], F32))
            pwd1 = _es4.enter_context(nc.sbuf_tensor("pwd1", [128, 2048], F32))
            ps_s0 = _es4.enter_context(nc.psum_tensor("ps_s0", [128, 1024], F32))
            ps_s1 = _es4.enter_context(nc.psum_tensor("ps_s1", [128, 1024], F32))
            ps_o0 = _es4.enter_context(nc.psum_tensor("ps_o0", [65, 512], F32))
            ps_o1 = _es4.enter_context(nc.psum_tensor("ps_o1", [65, 512], F32))
            ps_e = _es4.enter_context(nc.psum_tensor("ps_e", [64, 512], F32))
            ps_wb = _es4.enter_context(nc.psum_tensor("ps_wb", [128, 2], F32))
            blk3 = _es4.enter_context(nc.Block())
            if True:
                ptile = [pt0, pt1]
                tmpo = [tmpo0, tmpo1]
                ps_s = [ps_s0, ps_s1]
                ps_o = [ps_o0, ps_o1]
                t_qk3, v_msk, s_exp = S_("t_qk3"), S_("v_msk"), S_("s_exp")
                t_av, v_den1, t_ex = S_("t_av"), S_("v_den1"), S_("t_ex")
                v_den2 = S_("v_den2")
                g_crossp = [S_("g_cross0"), S_("g_cross1")]
                wd_ldp = [S_("wd_ld0"), S_("wd_ld1")]
                v_pwd = S_("v_pwd")
                v_wredd, t_w1d, v_w1d = S_("v_wredd"), S_("t_w1d"), S_("v_w1d")

                @blk3.gpsimd
                def _(g):
                    g.wait_ge(S_("v_aug"), 64)
                    g.wait_ge(S_("t_dot"), 64)
                    for u, (b, h, c) in enumerate(units):
                        ch = u
                        if ch < 64:
                            if ch >= 2:
                                g.wait_ge(v_pwd, ch - 1)
                            g.dma_start(out=(pwd0 if ch % 2 == 0 else pwd1)[:],
                                        in_=wdT[ch * 128:(ch + 1) * 128, :]).then_inc(wd_ldp[ch % 2], 16)
                        g.wait_ge(v_den2, u + 1)
                        g.dma_start(out=oT[h // 2][(h % 2) * 64:(h % 2) * 64 + 64,
                                                   b * S + c * 512:b * S + (c + 1) * 512],
                                    in_=tmpo[u % 2][:, :]).then_inc(g_crossp[u % 2], 16)

                @blk3.tensor
                def _(t):
                    t.wait_ge(S_("v_aug"), 64)
                    for u, (b, h, c) in enumerate(units):
                        i = b * 4 + h
                        ng = 2 * c + 2

                        def qk(g2):
                            gg = cum_grp[u] + g2
                            if gg >= 2:
                                t.wait_ge(s_exp, gg - 1)
                            ins = None
                            for j in range(2):
                                kvt = 2 * g2 + j
                                ins = t.matmul(ps_s[gg % 2][:, j * 512:(j + 1) * 512],
                                               k_aug[b][:, kvt * 128:(kvt + 1) * 128],
                                               q_aug[i][:, c * 512:(c + 1) * 512],
                                               start=True, stop=True)
                            ins.then_inc(t_qk3, 1)

                        def av(g2):
                            gg = cum_grp[u] + g2
                            t.wait_ge(s_exp, gg + 1)
                            if g2 == 0 and u >= 2:
                                t.wait_ge(v_den2, u - 1)
                            ins = None
                            for j in range(2):
                                kvt = 2 * g2 + j
                                ins = t.matmul(ps_o[u % 2][:, :],
                                               v_sb[b][:, kvt * 65:(kvt + 1) * 65],
                                               ptile[gg % 2][:, j * 512:(j + 1) * 512],
                                               start=(kvt == 0), stop=(kvt == 2 * ng - 1))
                            ins.then_inc(t_av, 1)

                        for g2 in range(ng):
                            qk(g2)
                            if g2 >= 1:
                                av(g2 - 1)
                        av(ng - 1)
                        t.wait_ge(v_den1, u + 1)
                        t.matmul(ps_e[:, :], ones_r[0:1, 0:64], rden[0:1, :],
                                 start=True, stop=True).then_inc(t_ex, 1)
                    t.wait_ge(v_wredd, 1)
                    t.matmul(ps_wb[0:1, 0:1], wsc[:, 3:4], onescol[:, 0:1], start=True, stop=True).then_inc(t_w1d, 1)
                    t.wait_ge(v_w1d, 1)
                    t.matmul(ps_wb[:, 1:2], onesrow[0:1, 0:128], w1s[0:1, 2:3], start=True, stop=True).then_inc(t_w1d, 1)

                @blk3.vector
                def _(v):
                    v = _Drain(v)
                    wd_ch = 0
                    for u, (b, h, c) in enumerate(units):
                        ng = 2 * c + 2
                        for dg in range(2):
                            g2 = ng - 2 + dg
                            gg = cum_grp[u] + g2
                            v.wait_ge(t_qk3, gg + 1)
                            for j in range(2):
                                kvt = 2 * g2 + j
                                r = kvt - 4 * c
                                v.tensor_add(ps_s[gg % 2][:, j * 512:(j + 1) * 512],
                                             ps_s[gg % 2][:, j * 512:(j + 1) * 512],
                                             masks_sb[:, r * 512:(r + 1) * 512])
                            v.engine_nop().then_inc(v_msk, 1)
                        v.wait_ge(t_av, cum_grp[u + 1])
                        v.reciprocal(rdenf[0:1, :], ps_o[u % 2][64:65, :])
                        v.tensor_copy(rden[0:1, :], rdenf[0:1, :]).then_inc(v_den1, 1)
                        v.wait_ge(t_ex, u + 1)
                        v.tensor_copy(sce[:, :], ps_e[:, :])
                        if u >= 2:
                            v.wait_ge(g_crossp[u % 2], 16 * (u // 2))
                        v.tensor_mul(tmpo[u % 2][:, :], ps_o[u % 2][0:64, :], sce[:, :]).then_inc(v_den2, 1)
                        if wd_ch < 64:
                            v.wait_ge(wd_ldp[wd_ch % 2], 16 * (wd_ch // 2 + 1))
                            v.reduce_sum(wsum[:, 128 + wd_ch:129 + wd_ch],
                                         (pwd0 if wd_ch % 2 == 0 else pwd1)[:], AX.X,
                                         apply_absolute_value=True).then_inc(v_pwd, 1)
                            wd_ch += 1
                    v.reduce_sum(wsc[:, 3:4], wsum[:, 128:192], AX.X).then_inc(v_wredd, 1)
                    v.wait_ge(t_w1d, 1)
                    v.tensor_scalar_mul(w1s[:, 2:3], ps_wb[0:1, 0:1], 1.0 / (D * DI))
                    v.tensor_scalar_max(w1s[:, 2:3], w1s[:, 2:3], 1e-5).then_inc(v_w1d, 1)
                    v.wait_ge(t_w1d, 2)
                    v.tensor_copy(wsc[:, 2:3], ps_wb[:, 1:2])
                    v.reciprocal(swc[:, 2:3], wsc[:, 2:3]).then_inc(v_w1d, 1)

                @blk3.scalar
                def _(s):
                    s = _Drain(s)
                    for u, (b, h, c) in enumerate(units):
                        ng = 2 * c + 2
                        for g2 in range(ng):
                            gg = cum_grp[u] + g2
                            if g2 >= ng - 2:
                                s.wait_ge(v_msk, 2 * u + (g2 - (ng - 2)) + 1)
                            else:
                                s.wait_ge(t_qk3, gg + 1)
                            if gg >= 2:
                                s.wait_ge(t_av, gg - 1)
                            s.activation(ptile[gg % 2][:, :], ps_s[gg % 2][:, :], AF.Exp,
                                         bias=expb[:, 0:1], scale=0.125).then_inc(s_exp, 1)

            _es4.close()
            _es5 = contextlib.ExitStack()
            wo32 = _es5.enter_context(nc.sbuf_tensor("wo32", [128, 512], F32))
            woutb = _es5.enter_context(nc.sbuf_tensor("woutb", [128, 2 * D], BF16))
            rst = _es5.enter_context(nc.sbuf_tensor("rst", [128, 4 * 512], BF16))
            ps_w0 = _es5.enter_context(nc.psum_tensor("ps_w0", [128, 512], F32))
            ps_w1b = _es5.enter_context(nc.psum_tensor("ps_w1b", [128, 512], F32))
            blk4 = _es5.enter_context(nc.Block())
            if True:
                wo_ld, wo_cv, t_wo = S_("wo_ld"), S_("wo_cv"), S_("t_wo")
                v_rs, g_coll = S_("v_rs"), S_("g_coll")
                rs_stp = [S_("rs_st0"), S_("rs_st1"), S_("rs_st2"), S_("rs_st3")]
                ps_w = [ps_w0, ps_w1b]

                @blk4.gpsimd
                def _(g):
                    g.wait_ge(S_("g_cross0"), 16 * (NU // 2))
                    g.wait_ge(S_("g_cross1"), 16 * (NU // 2))
                    for j in range(8):
                        ct, mc = j // 4, j % 4
                        if j >= 1:
                            g.wait_ge(wo_cv, j)
                        g.dma_start(out=wo32[:],
                                    in_=w_outT[ct * 128:(ct + 1) * 128, mc * 512:(mc + 1) * 512]).then_inc(wo_ld, 16)
                    for q4 in range(4):
                        g.wait_ge(rs_stp[q4], 16 * 64)
                    g.collective_compute(
                        "ReduceScatter", ALU.add,
                        replica_groups=[list(range(8))],
                        ins=[rs_in[:, :]],
                        outs=[rs_out[:, :]],
                    ).then_inc(g_coll, 1)

                @blk4.vector
                def _(v):
                    v = _Drain(v)
                    v.wait_ge(S_("g_cross0"), 16 * (NU // 2))
                    v.wait_ge(S_("g_cross1"), 16 * (NU // 2))
                    for j in range(8):
                        v.wait_ge(wo_ld, 16 * (j + 1))
                        ct, mc = j // 4, j % 4
                        v.tensor_copy(woutb[:, ct * D + mc * 512:ct * D + (mc + 1) * 512],
                                      wo32[:]).then_inc(wo_cv, 1)
                    for idx in range(256):
                        v.wait_ge(t_wo, idx + 1)
                        if idx >= 4:
                            v.wait_ge(rs_stp[idx % 4], 16 * (idx // 4))
                        v.tensor_copy(rst[:, (idx % 4) * 512:(idx % 4 + 1) * 512],
                                      ps_w[idx % 2][:, :]).then_inc(v_rs, 1)

                @blk4.tensor
                def _(t):
                    t.wait_ge(wo_cv, 8)
                    t.wait_ge(S_("g_cross0"), 16 * (NU // 2))
                    t.wait_ge(S_("g_cross1"), 16 * (NU // 2))
                    for idx in range(256):
                        tt, mc = idx // 4, idx % 4
                        if idx >= 2:
                            t.wait_ge(v_rs, idx - 1)
                        ins = None
                        for ct in range(2):
                            ins = t.matmul(ps_w[idx % 2][:, :], oT[ct][:, tt * 128:(tt + 1) * 128],
                                           woutb[:, ct * D + mc * 512:ct * D + (mc + 1) * 512],
                                           start=(ct == 0), stop=(ct == 1))
                        ins.then_inc(t_wo, 1)

                @blk4.sync
                def _(sync):
                    for idx in range(256):
                        tt, mc = idx // 4, idx % 4
                        sync.wait_ge(v_rs, idx + 1)
                        sync.dma_start(out=rs_in[tt * 128:(tt + 1) * 128, mc * 512:(mc + 1) * 512],
                                       in_=rst[:, (idx % 4) * 512:(idx % 4 + 1) * 512]).then_inc(rs_stp[idx % 4], 16)

                if debug:
                    @blk4.scalar
                    def _(s):
                        s = _Drain(s)
                        s.wait_ge(S_("g_cross0"), 16 * (NU // 2))
                        s.wait_ge(S_("g_cross1"), 16 * (NU // 2))
                        for i in range(8):
                            s.dma_start(out=tap["qaug"][i * 65:(i + 1) * 65, :], in_=q_aug[i][:, :]).then_inc(S_("tapsem"), 16)
                        for b in range(B):
                            s.dma_start(out=tap["kaug"][b * 65:(b + 1) * 65, :], in_=k_aug[b][:, :]).then_inc(S_("tapsem"), 16)
                            s.dma_start(out=tap["v"][b * 128:(b + 1) * 128, :], in_=v_sb[b][:, :]).then_inc(S_("tapsem"), 16)
                        for p in range(2):
                            s.dma_start(out=tap["oT"][p * 128:(p + 1) * 128, :], in_=oT[p][:, :]).then_inc(S_("tapsem"), 16)
                        s.wait_ge(S_("tapsem"), 16 * 14)

    _es5.close()
    _es3.close()
    _es0.close()
    with nc.sbuf_tensor("hslab", [128, 8 * DI], BF16) as hslab:
        _es6 = contextlib.ExitStack()
        kxT = _es6.enter_context(nc.sbuf_tensor("kxT", [128, 16 * NT_OWN * 128], BF16))
        mxc8 = _es6.enter_context(nc.sbuf_tensor("mxc8", [128, 8], F32))
        gsc8 = _es6.enter_context(nc.sbuf_tensor("gsc8", [128, 8], F32))
        usc8 = _es6.enter_context(nc.sbuf_tensor("usc8", [128, 8], F32))
        if True:
            _es7 = contextlib.ExitStack()
            xo0 = _es7.enter_context(nc.sbuf_tensor("xo0", [128, D], F32))
            xo1 = _es7.enter_context(nc.sbuf_tensor("xo1", [128, D], F32))
            xno0 = _es7.enter_context(nc.sbuf_tensor("xno0", [128, D], BF16))
            xno1 = _es7.enter_context(nc.sbuf_tensor("xno1", [128, D], BF16))
            sq5 = _es7.enter_context(nc.sbuf_tensor("sq5", [128, D], BF16))
            st5 = _es7.enter_context(nc.sbuf_tensor("st5", [128, 24], F32))
            kxf = _es7.enter_context(nc.sbuf_tensor("kxf", [128, D], F32))
            kxb = _es7.enter_context(nc.sbuf_tensor("kxb", [128, D], BF16))
            ps_t50 = _es7.enter_context(nc.psum_tensor("ps_t50", [128, 512], F32))
            ps_t51 = _es7.enter_context(nc.psum_tensor("ps_t51", [128, 512], F32))
            blk5a = _es7.enter_context(nc.Block())
            if True:
                xo = [xo0, xo1]
                xno = [xno0, xno1]
                ps_t5 = [ps_t50, ps_t51]
                xo_ldp = [S_("xo_ld0"), S_("xo_ld1")]
                s_sq5, v_st5, v_red5 = S_("s_sq5"), S_("v_st5"), S_("v_red5")
                v_kx, t_kxT, v_kxT = S_("v_kx"), S_("t_kxT"), S_("v_kxT")
                xp_stp = [S_("xp_st0"), S_("xp_st1")]

                @blk5a.sync
                def _(sync):
                    for q4 in range(4):
                        sync.wait_ge(S_(f"rs_st{q4}"), 16 * 64)
                    def park(j):
                        sync.wait_ge(v_kx, 3 * j + 2)
                        sync.dma_start(out=xn_park[j * 128:(j + 1) * 128, :], in_=xno[j % 2][:]).then_inc(xp_stp[j % 2], 16)
                    for ti in range(NT_OWN):
                        if ti >= 2:
                            sync.wait_ge(s_sq5, 4 * (ti - 2) + 4)
                            sync.wait_ge(v_red5, ti - 1)
                        sync.dma_start(out=xo[ti % 2][:], in_=x_own[ti * 128:(ti + 1) * 128, :]).then_inc(xo_ldp[ti % 2], 16)
                        if ti >= 2:
                            park(ti - 2)
                    park(NT_OWN - 2)
                    park(NT_OWN - 1)

                @blk5a.scalar
                def _(s):
                    s = _Drain(s)
                    for q4 in range(4):
                        s.wait_ge(S_(f"rs_st{q4}"), 16 * 64)
                    for ti in range(NT_OWN):
                        xb = xo[ti % 2]
                        o = 12 * (ti % 2)
                        s.wait_ge(xo_ldp[ti % 2], 16 * (ti // 2 + 1))
                        s.activation(sq5[:], xb[:], AF.Square, accum_out=st5[:, o + 1:o + 2]).then_inc(s_sq5, 1)
                        s.wait_ge(v_st5, 2 * ti + 1)
                        s.activation(st5[:, o + 9:o + 10], st5[:, o + 6:o + 7], AF.Ln).then_inc(s_sq5, 1)
                        s.activation(st5[:, o + 10:o + 11], st5[:, o + 9:o + 10], AF.Exp, scale=-0.5).then_inc(s_sq5, 1)
                        s.wait_ge(v_st5, 2 * ti + 2)
                        if ti >= 2:
                            s.wait_ge(v_kx, 3 * (ti - 2) + 3)
                            s.wait_ge(xp_stp[ti % 2], 16 * (ti // 2))
                        s.activation(xno[ti % 2][:], xb[:], AF.Identity,
                                     bias=st5[:, o + 11:o + 12], scale=st5[:, o + 10:o + 11]).then_inc(s_sq5, 1)

                @blk5a.vector
                def _(v):
                    v = _Drain(v)
                    for q4 in range(4):
                        v.wait_ge(S_(f"rs_st{q4}"), 16 * 64)
                    for ti in range(NT_OWN):
                        xb = xo[ti % 2]
                        o = 12 * (ti % 2)
                        v.wait_ge(xo_ldp[ti % 2], 16 * (ti // 2 + 1))
                        v.reduce_sum(st5[:, o + 0:o + 1], xb[:], AX.X).then_inc(v_red5, 1)
                        v.wait_ge(s_sq5, 4 * ti + 1)
                        v.tensor_scalar_mul(st5[:, o + 2:o + 3], st5[:, o + 0:o + 1], 1.0 / D)
                        v.tensor_scalar_mul(st5[:, o + 3:o + 4], st5[:, o + 1:o + 2], 1.0 / D)
                        v.tensor_scalar(st5[:, o + 4:o + 5], st5[:, o + 2:o + 3], st5[:, o + 2:o + 3], None, ALU.mult)
                        v.tensor_sub(st5[:, o + 5:o + 6], st5[:, o + 3:o + 4], st5[:, o + 4:o + 5])
                        v.tensor_scalar_add(st5[:, o + 6:o + 7], st5[:, o + 5:o + 6], 1e-5).then_inc(v_st5, 1)
                        v.wait_ge(s_sq5, 4 * ti + 3)
                        v.tensor_scalar(st5[:, o + 11:o + 12], st5[:, o + 10:o + 11], st5[:, o + 10:o + 11], None, ALU.mult)
                        v.scalar_tensor_tensor(st5[:, o + 11:o + 12], st5[:, o + 11:o + 12], -0.5,
                                               st5[:, o + 6:o + 7], ALU.mult, ALU.mult)
                        v.tensor_scalar_add(st5[:, o + 11:o + 12], st5[:, o + 11:o + 12], 1.5)
                        v.tensor_mul(st5[:, o + 10:o + 11], st5[:, o + 11:o + 12], st5[:, o + 10:o + 11])
                        v.tensor_scalar(st5[:, o + 11:o + 12], st5[:, o + 2:o + 3],
                                        st5[:, o + 10:o + 11], -1.0, ALU.mult, ALU.mult).then_inc(v_st5, 1)
                        v.wait_ge(s_sq5, 4 * ti + 4)
                        v.reduce_max(st5[:, o + 4:o + 5], xno[ti % 2][:], AX.X, apply_absolute_value=True)
                        v.tensor_scalar_max(st5[:, o + 4:o + 5], st5[:, o + 4:o + 5], 1e-5)
                        v.reciprocal(st5[:, o + 5:o + 6], st5[:, o + 4:o + 5])
                        v.tensor_scalar_mul(st5[:, o + 5:o + 6], st5[:, o + 5:o + 6], 127.0)
                        v.tensor_scalar_mul(mxc8[:, ti:ti + 1], st5[:, o + 4:o + 5], 1.0 / 127.0).then_inc(v_kx, 1)
                        if ti >= 1:
                            v.wait_ge(t_kxT, 4 * ti)
                        v.tensor_scalar(kxf[:], xno[ti % 2][:], st5[:, o + 5:o + 6], 12582912.0,
                                        ALU.mult, ALU.add).then_inc(v_kx, 1)
                        v.tensor_scalar_add(kxb[:], kxf[:], -12582912.0).then_inc(v_kx, 1)
                        for r in range(4):
                            g4 = 4 * ti + r
                            v.wait_ge(t_kxT, g4 + 1)
                            pst = ps_t5[g4 % 2]
                            for j in range(4):
                                hi = r * 4 + j
                                v.tensor_copy(kxT[:, (hi * NT_OWN + ti) * 128:(hi * NT_OWN + ti + 1) * 128],
                                              pst[:, j * 128:(j + 1) * 128])
                            v.engine_nop().then_inc(v_kxT, 1)
                    v.wait_ge(S_("v_w1"), 2)
                    v.tensor_scalar(gsc8[:], mxc8[:], wsc[:, 0:1], None, ALU.mult)
                    v.tensor_scalar(usc8[:], mxc8[:], wsc[:, 1:2], None, ALU.mult).then_inc(v_kx, 1)

                @blk5a.tensor
                def _(t):
                    for q4 in range(4):
                        t.wait_ge(S_(f"rs_st{q4}"), 16 * 64)
                    for ti in range(NT_OWN):
                        t.wait_ge(v_kx, 3 * ti + 3)
                        for r in range(4):
                            g4 = 4 * ti + r
                            pst = ps_t5[g4 % 2]
                            if g4 >= 2:
                                t.wait_ge(v_kxT, g4 - 1)
                            ins = None
                            for j in range(4):
                                hi = r * 4 + j
                                ins = t.matmul(pst[:, j * 128:(j + 1) * 128],
                                               kxb[:, hi * 128:(hi + 1) * 128], identb[:],
                                               start=True, stop=True)
                            ins.then_inc(t_kxT, 1)

            _es7.close()
            _es8 = contextlib.ExitStack()
            wst0 = _es8.enter_context(nc.sbuf_tensor("wst0", [128, 512], F32))
            wst1 = _es8.enter_context(nc.sbuf_tensor("wst1", [128, 512], F32))
            wst2 = _es8.enter_context(nc.sbuf_tensor("wst2", [128, 512], F32))
            wst3 = _es8.enter_context(nc.sbuf_tensor("wst3", [128, 512], F32))
            wq32x0 = _es8.enter_context(nc.sbuf_tensor("wq32x0", [128, 512], F32))
            wq32x1 = _es8.enter_context(nc.sbuf_tensor("wq32x1", [128, 512], F32))
            wq32x2 = _es8.enter_context(nc.sbuf_tensor("wq32x2", [128, 512], F32))
            wq32x3 = _es8.enter_context(nc.sbuf_tensor("wq32x3", [128, 512], F32))
            wqm0 = _es8.enter_context(nc.sbuf_tensor("wqm0", [128, 512], BF16))
            wqm1 = _es8.enter_context(nc.sbuf_tensor("wqm1", [128, 512], BF16))
            wqm2 = _es8.enter_context(nc.sbuf_tensor("wqm2", [128, 512], BF16))
            wqm3 = _es8.enter_context(nc.sbuf_tensor("wqm3", [128, 512], BF16))
            wqi8a0 = _es8.enter_context(nc.sbuf_tensor("wqi8a", [128, 512], I8))
            wqi8a1 = _es8.enter_context(nc.sbuf_tensor("wqi8b", [128, 512], I8))
            wqb0 = _es8.enter_context(nc.sbuf_tensor("wqb0", [128, 512], BF16))
            wqb1 = _es8.enter_context(nc.sbuf_tensor("wqb1", [128, 512], BF16))
            wqb2 = _es8.enter_context(nc.sbuf_tensor("wqb2", [128, 512], BF16))
            wqb3 = _es8.enter_context(nc.sbuf_tensor("wqb3", [128, 512], BF16))
            hg = _es8.enter_context(nc.sbuf_tensor("hg", [128, 8 * 512], BF16))
            hu = _es8.enter_context(nc.sbuf_tensor("hu", [128, 8 * 512], BF16))
            ps_g = _es8.enter_context(nc.psum_tensor("ps_g", [128, 4 * 512], F32))
            ps_u = _es8.enter_context(nc.psum_tensor("ps_u", [128, 4 * 512], F32))
            blk5b = _es8.enter_context(nc.Block())
            if True:
                wst = [wst0, wst1, wst2, wst3]
                wq32x = [wq32x0, wq32x1, wq32x2, wq32x3]
                wqm = [wqm0, wqm1, wqm2, wqm3]
                wqi8 = [wqi8a0, wqi8a1]
                wqb = [wqb0, wqb1, wqb2, wqb3]
                def g2glob(kg):
                    return (kg // 16) * 32 + (kg % 16)
                def u2glob(ku):
                    return (ku // 16) * 32 + 16 + (ku % 16)
                wgu_ldp = [S_("wgu_ld0"), S_("wgu_ld1")]
                wgu_up = [S_("wgu_up0"), S_("wgu_up1")]
                v_wq, s_wq = S_("v_wq"), S_("s_wq")
                t_gu, s_hg, s_hu, v_h = S_("t_gu"), S_("s_hg"), S_("s_hu"), S_("v_h")

                def s2glob(kq, wsel):
                    return (kq // 16) * 32 + wsel * 16 + (kq % 16)

                def bank(tt):
                    return (ps_g if tt < 4 else ps_u)[:, (tt % 4) * 512:(tt % 4 + 1) * 512]

                @blk5b.gpsimd
                def _(g):
                    g.wait_ge(S_("xp_st0"), 16 * (NT_OWN // 2))
                    g.wait_ge(S_("xp_st1"), 16 * (NT_OWN // 2))
                    g.wait_ge(S_("t_kxT"), 4 * NT_OWN)
                    g.wait_ge(S_("v_kxT"), 4 * NT_OWN)
                    kg = 0
                    for ic in range(16):
                        for hi in range(16):
                            if kg >= 2:
                                g.wait_ge(s_wq, s2glob(kg - 2, 0) + 1)
                            g.dma_start(out=wst[2 * (kg % 2)][:],
                                        in_=wgT[hi * 128:(hi + 1) * 128,
                                                ic * 512:(ic + 1) * 512]).then_inc(wgu_ldp[kg % 2], 16)
                            kg += 1

                @blk5b.sync
                def _(sync):
                    for q4 in range(4):
                        sync.wait_ge(S_(f"rs_st{q4}"), 16 * 64)
                    sync.wait_ge(S_("t_kxT"), 4 * NT_OWN)
                    ku = 0
                    for ic in range(16):
                        for hi in range(16):
                            if ku >= 2:
                                sync.wait_ge(s_wq, s2glob(ku - 2, 1) + 1)
                            sync.dma_start(out=wst[2 * (ku % 2) + 1][:],
                                           in_=wuT[hi * 128:(hi + 1) * 128,
                                                   ic * 512:(ic + 1) * 512]).then_inc(wgu_up[ku % 2], 16)
                            ku += 1

                @blk5b.vector
                def _(v):
                    v = _Drain(v)
                    v.wait_ge(S_("xp_st0"), 16 * (NT_OWN // 2))
                    v.wait_ge(S_("xp_st1"), 16 * (NT_OWN // 2))
                    v.wait_ge(S_("v_kx"), 3 * NT_OWN + 1)
                    k = 0
                    for ic in range(16):
                        if ic >= 1:
                            v.wait_ge(s_hu, 8 * ic)
                            for tt in range(8):
                                v.tensor_mul(hslab[:, tt * DI + (ic - 1) * 512:tt * DI + ic * 512],
                                             hg[:, tt * 512:(tt + 1) * 512], hu[:, tt * 512:(tt + 1) * 512])
                            v.engine_nop().then_inc(v_h, 1)
                        for wsel in range(2):
                            for hi in range(16):
                                v.wait_ge(s_wq, k + 1)
                                if k >= 4:
                                    v.wait_ge(t_gu, k - 3)
                                v.tensor_scalar(wqb[k % 4][:], wqm[k % 4][:], 1.0, -1.0,
                                                ALU.min, ALU.max).then_inc(v_wq, 1)
                                k += 1
                    v.wait_ge(s_hu, 8 * 16)
                    for tt in range(8):
                        v.tensor_mul(hslab[:, tt * DI + 15 * 512:tt * DI + 16 * 512],
                                     hg[:, tt * 512:(tt + 1) * 512], hu[:, tt * 512:(tt + 1) * 512])
                    v.engine_nop().then_inc(v_h, 1)

                @blk5b.scalar
                def _(s):
                    s = _Drain(s)
                    s.wait_ge(S_("xp_st0"), 16 * (NT_OWN // 2))
                    s.wait_ge(S_("xp_st1"), 16 * (NT_OWN // 2))
                    k = 0
                    for ic in range(16):
                        for wsel in range(2):
                            for hi in range(16):
                                kq = ic * 16 + hi
                                stg = wst[2 * (kq % 2) + wsel]
                                sem = wgu_ldp if wsel == 0 else wgu_up
                                s.wait_ge(sem[kq % 2], 16 * (kq // 2 + 1))
                                if k >= 4:
                                    s.wait_ge(v_wq, k - 3)
                                s.activation(wq32x[k % 4][:], stg[:], AF.Identity,
                                             bias=mcol[:, 0:1], scale=swc[:, wsel:wsel + 1])
                                s.activation(wqm[k % 4][:], wq32x[k % 4][:], AF.Identity,
                                             bias=nmcol[:, 0:1]).then_inc(s_wq, 1)
                                k += 1
                            if wsel == 0:
                                s.wait_ge(t_gu, ic * 32 + 16)
                                s.wait_ge(v_h, ic)
                                if debug and ic == 1:
                                    s.wait_ge(S_("tapsem"), 256)
                                for tt in range(8):
                                    s.activation(hg[:, tt * 512:(tt + 1) * 512], bank(tt),
                                                 AF.Silu, scale=gsc8[:, tt:tt + 1]).then_inc(s_hg, 1)
                        s.wait_ge(t_gu, ic * 32 + 32)
                        for tt in range(8):
                            s.activation(hu[:, tt * 512:(tt + 1) * 512], bank(tt),
                                         AF.Identity, scale=usc8[:, tt:tt + 1]).then_inc(s_hu, 1)
                        if debug and ic == 0:
                            s.dma_start(out=tap["hg"][:, :], in_=hg[:, 0:2048]).then_inc(S_("tapsem"), 16)
                            s.dma_start(out=tap["hu"][:, :], in_=hu[:, 0:2048]).then_inc(S_("tapsem"), 16)

                @blk5b.tensor
                def _(t):
                    k = 0
                    for ic in range(16):
                        for wsel in range(2):
                            for hi in range(16):
                                t.wait_ge(v_wq, k + 1)
                                ins = None
                                for tt in range(8):
                                    if hi == 0:
                                        if wsel == 0:
                                            if ic >= 1:
                                                t.wait_ge(s_hu, 8 * (ic - 1) + tt + 1)
                                        else:
                                            t.wait_ge(s_hg, 8 * ic + tt + 1)
                                    ins = t.matmul(bank(tt),
                                                   kxT[:, (hi * NT_OWN + tt) * 128:(hi * NT_OWN + tt + 1) * 128],
                                                   wqb[k % 4][:],
                                                   start=(hi == 0), stop=(hi == 15))
                                ins.then_inc(t_gu, 1)
                                k += 1

        _es8.close()
        _es6.close()
        _es9 = contextlib.ExitStack()
        lnwT = _es9.enter_context(nc.sbuf_tensor("lnwT", [128, DI], BF16))
        sq8k = _es9.enter_context(nc.sbuf_tensor("sq8k", [128, DI], BF16))
        hsq8 = _es9.enter_context(nc.sbuf_tensor("hsq8", [128, 8], F32))
        mxh8 = _es9.enter_context(nc.sbuf_tensor("mxh8", [128, 8], F32))
        rs8 = _es9.enter_context(nc.sbuf_tensor("rs8", [128, 8], F32))
        f28 = _es9.enter_context(nc.sbuf_tensor("f28", [128, 8], F32))
        dscr = _es9.enter_context(nc.sbuf_tensor("dscr", [128, 8], F32))
        sqout = _es9.enter_context(nc.sbuf_tensor("sqout", [128, DI], BF16))
        k2f = _es9.enter_context(nc.sbuf_tensor("k2f", [128, DI // 2], F32))
        ktr0 = _es9.enter_context(nc.sbuf_tensor("ktr", [128, 512], BF16))
        ktr1 = _es9.enter_context(nc.sbuf_tensor("ktr1", [128, 512], BF16))
        ps_c0 = _es9.enter_context(nc.psum_tensor("ps_c0", [128, 512], F32))
        ps_c1 = _es9.enter_context(nc.psum_tensor("ps_c1", [128, 512], F32))
        blk5c = _es9.enter_context(nc.Block())
        if True:
            ktr = [ktr0, ktr1]
            ps_c = [ps_c0, ps_c1]
            lnw_ld, t_lnw, v_lnw = S_("lnw_ld"), S_("t_lnw"), S_("v_lnw")
            s_hsq, v_hlw, v_col5, s_ln5 = S_("s_hsq"), S_("v_hlw"), S_("v_col5"), S_("s_ln5")
            v_k2, s_k2 = S_("v_k2"), S_("s_k2")
            t_k2t, v_k2t = S_("t_k2t"), S_("v_k2t")
            k2t_stp = [S_("k2t_st0"), S_("k2t_st1")]

            @blk5c.gpsimd
            def _(g):
                g.wait_ge(S_("t_gu"), 512)
                g.wait_ge(S_("v_h"), 16)
                if debug:
                    g.wait_ge(S_("tapsem"), 224)
                g.dma_start(out=lnwT[:, :], in_=lnw_in[:, :]).then_inc(lnw_ld, 16)

            @blk5c.tensor
            def _(t):
                for tt in range(NT_OWN):
                    t.wait_ge(v_k2, tt + 1)
                    for r in range(16):
                        g4 = tt * 16 + r
                        if g4 >= 2:
                            t.wait_ge(v_k2t, g4 - 1)
                        ins = None
                        for j in range(4):
                            ii = r * 4 + j
                            ins = t.matmul(ps_c[g4 % 2][:, j * 128:(j + 1) * 128],
                                           hslab[:, tt * DI + ii * 128:tt * DI + (ii + 1) * 128],
                                           identb[:], start=True, stop=True)
                        ins.then_inc(t_k2t, 1)

            @blk5c.vector
            def _(v):
                v = _Drain(v)
                v.wait_ge(S_("v_h"), 16)
                v.wait_ge(lnw_ld, 16)
                for tt in range(NT_OWN):
                    v.tensor_mul(sq8k[:], hslab[:, tt * DI:(tt + 1) * DI], lnwT[:])
                    v.reduce_max(mxh8[:, tt:tt + 1], sq8k[:], AX.X, apply_absolute_value=True).then_inc(v_hlw, 1)
                v.wait_ge(S_("s_hsq"), 8)
                v.tensor_scalar(rs8[:], hsq8[:], 1.0 / DI, 1e-6, ALU.mult, ALU.add).then_inc(v_col5, 1)
                v.wait_ge(s_ln5, 2)
                v.wait_ge(S_("v_w1d"), 2)
                v.tensor_mul(dscr[:], f28[:], f28[:])
                v.tensor_tensor(dscr[:], dscr[:], rs8[:], ALU.mult)
                v.tensor_scalar(dscr[:], dscr[:], -0.5, 1.5, ALU.mult, ALU.add)
                v.tensor_mul(rs8[:], dscr[:], f28[:])
                v.tensor_mul(f28[:], rs8[:], mxh8[:])
                v.tensor_scalar_max(f28[:], f28[:], 1e-5)
                v.tensor_scalar(dsc8p[:], f28[:], wsc[:, 2:3], 1.0 / 127.0, ALU.mult, ALU.mult)
                v.tensor_scalar_mul(mxh8[:], rs8[:], 127.0)
                v.reciprocal(dscr[:], f28[:])
                v.tensor_mul(f28[:], mxh8[:], dscr[:]).then_inc(v_col5, 1)
                for tt in range(NT_OWN):
                    if debug and tt == 0:
                        v.wait_ge(S_("tapsem"), 272)
                    v.tensor_mul(sq8k[:], hslab[:, tt * DI:(tt + 1) * DI], lnwT[:])
                    for hf in range(2):
                        v.tensor_scalar(k2f[:], sq8k[:, hf * (DI // 2):(hf + 1) * (DI // 2)],
                                        f28[:, tt:tt + 1], 12582912.0, ALU.mult, ALU.add)
                        ins = v.tensor_scalar_add(hslab[:, tt * DI + hf * (DI // 2):tt * DI + (hf + 1) * (DI // 2)],
                                                  k2f[:], -12582912.0)
                    ins.then_inc(v_k2, 1)
                for tt in range(NT_OWN):
                    for r in range(16):
                        g4 = tt * 16 + r
                        v.wait_ge(t_k2t, g4 + 1)
                        if g4 >= 2:
                            v.wait_ge(k2t_stp[g4 % 2], 16 * (g4 // 2))
                        v.tensor_copy(ktr[g4 % 2][:], ps_c[g4 % 2][:, :]).then_inc(v_k2t, 1)

            @blk5c.scalar
            def _(s):
                s = _Drain(s)
                for tt in range(NT_OWN):
                    s.wait_ge(S_("v_h"), 16)
                    s.activation(sqout[:], hslab[:, tt * DI:(tt + 1) * DI], AF.Square,
                                 accum_out=hsq8[:, tt:tt + 1]).then_inc(s_hsq, 1)
                s.wait_ge(v_col5, 1)
                s.activation(f28[:], rs8[:], AF.Ln).then_inc(s_ln5, 1)
                s.activation(f28[:], f28[:], AF.Exp, scale=-0.5).then_inc(s_ln5, 1)
                s.wait_ge(S_("tapsem"), 272) if debug else None

            @blk5c.sync
            def _(sync):
                if debug:
                    sync.wait_ge(S_("s_hsq"), 8)
                    sync.dma_start(out=tap["h"][:, :], in_=hslab[:, 0:DI]).then_inc(S_("tapsem"), 16)
                for tt in range(NT_OWN):
                    for r in range(16):
                        g4 = tt * 16 + r
                        sync.wait_ge(v_k2t, g4 + 1)
                        dst = k2t_d[(r * 4) * 128:(r * 4 + 4) * 128,
                                    tt * 128:(tt + 1) * 128].rearrange("(n p) c -> p n c", p=128)
                        srcv = ktr[g4 % 2][:].rearrange("p (n c) -> p n c", n=4)
                        sync.dma_start(out=dst, in_=srcv).then_inc(k2t_stp[g4 % 2], 16)

        _es9.close()
    gslab = sb("gslab", [128, 8 * D], BF16)
    _es10 = contextlib.ExitStack()
    wst5a = _es10.enter_context(nc.sbuf_tensor("wst5a", [128, 512], F32))
    wst5b = _es10.enter_context(nc.sbuf_tensor("wst5b", [128, 512], F32))
    wq32dx0 = _es10.enter_context(nc.sbuf_tensor("wq32dx0", [128, 512], F32))
    wq32dx1 = _es10.enter_context(nc.sbuf_tensor("wq32dx1", [128, 512], F32))
    wqi8d0 = _es10.enter_context(nc.sbuf_tensor("wqi8d0", [128, 512], I8))
    wqi8d1 = _es10.enter_context(nc.sbuf_tensor("wqi8d1", [128, 512], I8))
    wqbd0 = _es10.enter_context(nc.sbuf_tensor("wqbd0", [128, 512], BF16))
    wqbd1 = _es10.enter_context(nc.sbuf_tensor("wqbd1", [128, 512], BF16))
    k2c0 = _es10.enter_context(nc.sbuf_tensor("k2c0", [128, 1024], BF16))
    k2c1 = _es10.enter_context(nc.sbuf_tensor("k2c1", [128, 1024], BF16))
    ps_dn0 = _es10.enter_context(nc.psum_tensor("ps_dn0", [128, 512], F32))
    ps_dn1 = _es10.enter_context(nc.psum_tensor("ps_dn1", [128, 512], F32))
    ps_dn2 = _es10.enter_context(nc.psum_tensor("ps_dn2", [128, 512], F32))
    ps_dn3 = _es10.enter_context(nc.psum_tensor("ps_dn3", [128, 512], F32))
    ps_dn4 = _es10.enter_context(nc.psum_tensor("ps_dn4", [128, 512], F32))
    ps_dn5 = _es10.enter_context(nc.psum_tensor("ps_dn5", [128, 512], F32))
    ps_dn6 = _es10.enter_context(nc.psum_tensor("ps_dn6", [128, 512], F32))
    ps_dn7 = _es10.enter_context(nc.psum_tensor("ps_dn7", [128, 512], F32))
    blk5e = _es10.enter_context(nc.Block())
    if True:
        wst5 = [wst5a, wst5b]
        wq32dx = [wq32dx0, wq32dx1]
        wqi8d = [wqi8d0, wqi8d1]
        wqbd = [wqbd0, wqbd1]
        k2c = [k2c0, k2c1]
        ps_dn = [ps_dn0, ps_dn1, ps_dn2, ps_dn3, ps_dn4, ps_dn5, ps_dn6, ps_dn7]
        wd2_ldp = [S_("wd2_ld0"), S_("wd2_ld1")]
        k2_ldp = [S_("k2_ld0"), S_("k2_ld1")]
        v_wq2, s_wq2 = S_("v_wq2"), S_("s_wq2")
        t_dn, s_g = S_("t_dn"), S_("s_g")

        @blk5e.gpsimd
        def _(g):
            g.wait_ge(S_("k2t_st0"), 16 * 64)
            g.wait_ge(S_("k2t_st1"), 16 * 64)
            k = 0
            for mc in range(4):
                for ii in range(64):
                    if k >= 2:
                        g.wait_ge(s_wq2, k - 1)
                    g.dma_start(out=wst5[k % 2][:],
                                in_=wdT[ii * 128:(ii + 1) * 128, mc * 512:(mc + 1) * 512]).then_inc(wd2_ldp[k % 2], 16)
                    k += 1

        @blk5e.sync
        def _(sync):
            sync.wait_ge(S_("k2t_st0"), 16 * 64)
            sync.wait_ge(S_("k2t_st1"), 16 * 64)
            k = 0
            for mc in range(4):
                for ii in range(64):
                    if k >= 2:
                        sync.wait_ge(t_dn, 8 * (k - 1))
                    sync.dma_start(out=k2c[k % 2][:], in_=k2t_d[ii * 128:(ii + 1) * 128, :]).then_inc(k2_ldp[k % 2], 16)
                    k += 1

        @blk5e.vector
        def _(v):
            v = _Drain(v)
            k = 0
            for mc in range(4):
                for ii in range(64):
                    v.wait_ge(s_wq2, k + 1)
                    if k >= 2:
                        v.wait_ge(t_dn, 8 * (k - 1))
                    v.tensor_scalar(wqbd[k % 2][:], wq32dx[k % 2][:], 1.0, -1.0,
                                    ALU.min, ALU.max).then_inc(v_wq2, 2)
                    k += 1

        @blk5e.scalar
        def _(s):
            s = _Drain(s)
            k = 0
            for mc in range(4):
                for ii in range(64):
                    s.wait_ge(wd2_ldp[k % 2], 16 * (k // 2 + 1))
                    if k >= 2:
                        s.wait_ge(v_wq2, 2 * (k - 1))
                    s.activation(wq32dx[k % 2][:], wst5[k % 2][:], AF.Identity,
                                 bias=mcol[:, 0:1], scale=swc[:, 2:3])
                    s.activation(wq32dx[k % 2][:], wq32dx[k % 2][:], AF.Identity,
                                 bias=nmcol[:, 0:1]).then_inc(s_wq2, 1)
                    k += 1
                if mc == 0:
                    s.wait_ge(S_("v_col5"), 2)
                s.wait_ge(t_dn, 8 * 64 * (mc + 1))
                ins = None
                for tt in range(8):
                    ins = s.activation(gslab[:, tt * D + mc * 512:tt * D + (mc + 1) * 512],
                                       ps_dn[tt][:, :], AF.Silu, scale=dsc8p[:, tt:tt + 1])
                ins.then_inc(s_g, 1)

        @blk5e.tensor
        def _(t):
            for mc in range(4):
                if mc >= 1:
                    t.wait_ge(s_g, mc)
                for ii in range(64):
                    k = mc * 64 + ii
                    t.wait_ge(v_wq2, 2 * (k + 1))
                    t.wait_ge(k2_ldp[k % 2], 16 * (k // 2 + 1))
                    ins = None
                    for tt in range(8):
                        ins = t.matmul(ps_dn[tt][:, :],
                                       k2c[k % 2][:, tt * 128:(tt + 1) * 128],
                                       wqbd[k % 2][:],
                                       start=(ii == 0), stop=(ii == 63))
                    ins.then_inc(t_dn, 8)

    _es10.close()
    _es11 = contextlib.ExitStack()
    rso = _es11.enter_context(nc.sbuf_tensor("rso", [128, D], BF16))
    xnp = _es11.enter_context(nc.sbuf_tensor("xnp", [128, D], BF16))
    xob = _es11.enter_context(nc.sbuf_tensor("xob", [128, D], F32))
    srow = _es11.enter_context(nc.sbuf_tensor("srow", [128, D], F32))
    sgrow = _es11.enter_context(nc.sbuf_tensor("sgrow", [128, D], F32))
    outb0 = _es11.enter_context(nc.sbuf_tensor("outb0", [128, D], F32))
    outb1 = _es11.enter_context(nc.sbuf_tensor("outb1", [128, D], F32))
    blk6 = _es11.enter_context(nc.Block())
    if True:
        outb = [outb0, outb1]
        rso_ld, v_out = S_("rso_ld"), S_("v_out")
        out_stp = [S_("out_st0"), S_("out_st1")]

        @blk6.sync
        def _(sync):
            sync.wait_ge(S_("g_coll"), 1)
            sync.wait_ge(S_("s_g"), 4)
            for tt in range(NT_OWN):
                if tt >= 1:
                    sync.wait_ge(v_out, 3 * tt)
                sync.dma_start(out=rso[:], in_=rs_out[tt * 128:(tt + 1) * 128, :]).then_inc(rso_ld, 16)
                sync.dma_start(out=xnp[:], in_=xn_park[tt * 128:(tt + 1) * 128, :]).then_inc(rso_ld, 16)
                sync.dma_start(out=xob[:], in_=x_own[tt * 128:(tt + 1) * 128, :]).then_inc(rso_ld, 16)
                sync.wait_ge(v_out, 3 * tt + 3)
                sync.dma_start(out=out[tt * 128:(tt + 1) * 128, :], in_=outb[tt % 2][:]).then_inc(out_stp[tt % 2], 16)
            if debug:
                tapsem = S_("tapsem")
                sync.dma_start(out=tap["xn"][:, :], in_=xn_park[:, :]).then_inc(tapsem, 16)
                sync.dma_start(out=tap["wsc"][:, 0:3], in_=wsc[:, 0:3]).then_inc(tapsem, 16)
                sync.dma_start(out=tap["wsc"][:, 4:7], in_=swc[:, 0:3]).then_inc(tapsem, 16)
                sync.dma_start(out=tap["rs"][:, :], in_=rs_out[:, :]).then_inc(tapsem, 16)
                for tt in range(NT_OWN):
                    sync.dma_start(out=tap["g"][tt * 128:(tt + 1) * 128, :],
                                   in_=gslab[:, tt * D:(tt + 1) * D]).then_inc(tapsem, 16)
                sync.wait_ge(tapsem, 272 + 16 * 12)
            sync.wait_ge(out_stp[0], 16 * (NT_OWN // 2))
            sync.wait_ge(out_stp[1], 16 * (NT_OWN // 2))

        @blk6.vector
        def _(v):
            v = _Drain(v)
            v.wait_ge(S_("s_g"), 4)
            for tt in range(NT_OWN):
                v.wait_ge(rso_ld, 16 * (3 * tt + 3))
                v.tensor_add(srow[:], rso[:], xnp[:]).then_inc(v_out, 1)
                v.tensor_mul(sgrow[:], srow[:], gslab[:, tt * D:(tt + 1) * D]).then_inc(v_out, 1)
                if tt >= 2:
                    v.wait_ge(out_stp[tt % 2], 16 * (tt // 2))
                v.tensor_add(outb[tt % 2][:], sgrow[:], xob[:]).then_inc(v_out, 1)

    _es11.close()
    return nc, tap


def _host_constants():
    d = 64
    inv = 1.0 / (10000.0 ** (np.arange(0, d, 2, dtype=np.float64) / d))
    pos = np.arange(S, dtype=np.float64)
    ang = pos[:, None] * inv[None, :]
    cos = np.cos(ang).astype(np.float32)
    sin = np.sin(ang).astype(np.float32)
    tc = np.zeros((128, 32 * NTB), np.float32)
    ts_ = np.zeros((128, 32 * NTB), np.float32)
    for pt in range(NTB):
        tc[:, pt * 32:(pt + 1) * 32] = cos[pt * 128:(pt + 1) * 128, :]
        ts_[:, pt * 32:(pt + 1) * 32] = sin[pt * 128:(pt + 1) * 128, :]
    mk = np.zeros((128, 4 * 512), np.float32)
    for r in range(4):
        p = np.arange(128)[:, None]
        f = np.arange(512)[None, :]
        mk[:, r * 512:(r + 1) * 512] = np.where(r * 128 + p > f, -1.0e30, 0.0)
    ident = np.eye(128, dtype=np.float32)
    invc = (1.0 / (np.arange(S, dtype=np.float64) + 1.0)).astype(ml_dtypes.bfloat16).reshape(1, S)
    return tc, ts_, mk, ident, invc


def kernel(L0_attn_x, L0_attn_w_in, L0_attn_w_out, L0_attn_rms_w,
           L0_ffn_w_gate, L0_ffn_w_up, L0_ffn_w_down, L0_ffn_ln_weight):
    x = np.ascontiguousarray(np.asarray(L0_attn_x, dtype=np.float32).reshape(T_ALL, D))
    w_in = np.asarray(L0_attn_w_in, dtype=np.float32)
    w_out = np.asarray(L0_attn_w_out, dtype=np.float32)
    rms_w = np.asarray(L0_attn_rms_w, dtype=np.float32)
    wg = np.asarray(L0_ffn_w_gate, dtype=np.float32)
    wu = np.asarray(L0_ffn_w_up, dtype=np.float32)
    wd = np.asarray(L0_ffn_w_down, dtype=np.float32)
    lnw = np.asarray(L0_ffn_ln_weight, dtype=np.float32)

    tc, ts_, mk, ident, invc = _host_constants()
    wgT = np.ascontiguousarray(wg.T)
    wuT = np.ascontiguousarray(wu.T)
    wdT = np.ascontiguousarray(wd.T)
    rms_colh = np.ascontiguousarray(rms_w.reshape(16, 128).T)
    lnw_h = np.ascontiguousarray(np.broadcast_to(lnw.reshape(1, DI), (128, DI)).copy())

    nc, _ = build_nc(debug=False)
    in_maps = []
    for g in range(8):
        rows = np.concatenate([w_in[g * 256:(g + 1) * 256],
                               w_in[2048 + g * 64:2048 + (g + 1) * 64],
                               w_in[2560 + g * 64:2560 + (g + 1) * 64]], axis=0)
        w_inT_g = np.ascontiguousarray(rows.T)
        w_outT_g = np.ascontiguousarray(w_out[:, g * 256:(g + 1) * 256].T)
        in_maps.append({
            "x_all": x,
            "x_own": np.ascontiguousarray(x[g * T_OWN:(g + 1) * T_OWN]),
            "w_inT": w_inT_g,
            "w_outT": w_outT_g,
            "rms_col": rms_colh,
            "wgT": wgT, "wuT": wuT, "wdT": wdT,
            "lnw_in": lnw_h,
            "trig_cos": tc, "trig_sin": ts_,
            "masks": mk, "ident_i": ident, "invcnt": invc,
        })
    res = run_bass_kernel_spmd(nc, in_maps, core_ids=list(range(8)))
    outs = [res.results[g]["out"] for g in range(8)]
    return np.concatenate(outs, axis=0).reshape(B, S, D).astype(np.float32)
